# version 13
# baseline (speedup 1.0000x reference)
import math
import numpy as np
from contextlib import ExitStack

import concourse.bass as bass
import concourse.mybir as mybir
from concourse.bass import broadcast_tensor_aps
from concourse.bass_utils import run_bass_kernel_spmd

F32 = mybir.dt.float32
BF16 = mybir.dt.bfloat16
AF = mybir.ActivationFunctionType
ALU = mybir.AluOpType
AX = mybir.AxisListType

THETA = 10000.0
EPS = 1e-5
CSHIFT = 12.0
SUB = 99
FENCE_Q = True
DBG_SCRATCH = False
NT1CAP = 999
HKVCAP = 999
SUB3MODE = 0
def rotk_src(t):
    return t[:, :]


class Geo:
    def __init__(self, T=4096, C=4096, H=32, HKV=8, NE=4, R=16):
        self.T, self.C, self.H, self.HKV, self.NE, self.R = T, C, H, HKV, NE, R
        self.DH = 128
        self.G = H // HKV
        self.NCH = C // 128
        self.KVD = HKV * 128
        self.HKVh = HKV // 2
        self.KVDh = self.KVD // 2
        self.NS = T // 256
        self.NPAIR = self.NS // 2
        self.NQ4 = self.NS // 4
        self.QT = T // 2
        self.NKT = T // 128
        self.ER = NE * R
        self.GW = self.G * 256
        self.SCALE = R / math.sqrt(R)
        assert self.NS % 4 == 0 and self.C % 512 == 0 and self.QT % 512 == 0


def jlist(g, half):
    if half == 0:
        return [2 * s + (s & 1) for s in range(g.NS)]
    return [2 * s + 1 - (s & 1) for s in range(g.NS)]


class SemT:
    __slots__ = ("h", "n", "name")

    def __init__(self, h, name):
        self.h, self.n, self.name = h, 0, name


class Q:
    __slots__ = ("items", "hi")

    def __init__(self):
        self.items = []
        self.hi = {}

    def w(self, sem, val):
        if val is None or val <= 0:
            return
        if self.hi.get(id(sem), -1) >= val:
            return
        self.hi[id(sem)] = val
        self.items.append((0, sem, val))

    def do(self, fn, *incs):
        toks = []
        for s, k in incs:
            s.n += k
            toks.append(s.n)
        self.items.append((1, fn, [(s, k) for s, k in incs]))
        if len(toks) == 1:
            return toks[0]
        return toks


def _replay(q, eng):
    for it in q.items:
        if it[0] == 0:
            eng.wait_ge(it[1].h, it[2])
        else:
            inst = it[1](eng)
            for s, k in it[2]:
                inst = inst.then_inc(s.h, k)


def build_nc(g, _phases=99, n_cores=8):
    nc = bass.Bass("TRN2", target_bir_lowering=False)
    T, C, H, HKV, NE, R = g.T, g.C, g.H, g.HKV, g.NE, g.R
    HKVh, KVDh = g.HKVh, g.KVDh
    NCH, KVD, NS, NPAIR, NQ4, QT, NKT, ER, G, GW = (
        g.NCH, g.KVD, g.NS, g.NPAIR, g.NQ4, g.QT, g.NKT, g.ER, g.G, g.GW)
    CBW = 256
    NCB = C // CBW

    d_xT = nc.dram_tensor("xT", [C, T], F32, kind="ExternalInput").ap()
    d_xTq = nc.dram_tensor("xTq", [C, QT], F32, kind="ExternalInput").ap()
    d_xq = nc.dram_tensor("xq", [QT, C], F32, kind="ExternalInput").ap()
    d_wq = nc.dram_tensor("wq", [C, C], F32, kind="ExternalInput").ap()
    d_wk = nc.dram_tensor("wk", [C, KVDh], F32, kind="ExternalInput").ap()
    d_wv = nc.dram_tensor("wv", [C, KVDh], F32, kind="ExternalInput").ap()
    d_wo = nc.dram_tensor("wo", [C, C], F32, kind="ExternalInput").ap()
    d_aw = nc.dram_tensor("awT", [C, ER + NE], F32, kind="ExternalInput").ap()
    d_bS = nc.dram_tensor("bS", [ER, C], F32, kind="ExternalInput").ap()
    d_em = nc.dram_tensor("emat", [NE, ER], F32, kind="ExternalInput").ap()
    d_ckk = nc.dram_tensor("ckk", [128, T], F32, kind="ExternalInput").ap()
    d_skk = nc.dram_tensor("skk", [128, T], F32, kind="ExternalInput").ap()
    d_ckq = nc.dram_tensor("ckq", [128, QT], F32, kind="ExternalInput").ap()
    d_skq = nc.dram_tensor("skq", [128, QT], F32, kind="ExternalInput").ap()
    d_mask = nc.dram_tensor("maskq", [NPAIR * 4 * 128, 256], F32, kind="ExternalInput").ap()
    d_id = nc.dram_tensor("ident", [128, 128], F32, kind="ExternalInput").ap()
    d_out = nc.dram_tensor("out", [QT, C], F32, kind="ExternalOutput").ap()
    d_dby = nc.dram_tensor("dby", [QT, C], F32, kind="ExternalOutput").ap() if DBG_SCRATCH else None
    s_dby = None

    _sk = "ExternalOutput" if DBG_SCRATCH else "Internal"
    VCH = 4 if NKT % 4 == 0 else 1
    KTQ = NKT // VCH
    s_kh = nc.dram_tensor("scr_kh", [HKVh * 128, T], BF16, kind="Internal").ap()
    s_k = nc.dram_tensor("scr_k", [HKVh, 2, 128, T], BF16, kind=_sk).ap()
    s_vh = nc.dram_tensor("scr_vh", [NKT * 128, KVDh], BF16, kind="Internal").ap()
    s_v = nc.dram_tensor("scr_v", [VCH, 2, KTQ, 128, KVDh], BF16, kind=_sk).ap()
    s_q = nc.dram_tensor("scr_q", [HKV, 128, NPAIR * GW], BF16, kind=_sk).ap()
    s_at = nc.dram_tensor("scr_at", [HKV, NPAIR, 128, GW], BF16, kind=_sk).ap()

    es = ExitStack()
    semlist = []

    def S(name):
        h = es.enter_context(nc.semaphore(name))
        s = SemT(h, name)
        semlist.append(s)
        return s

    qpe, qact, qdve, qgp, qsp = Q(), Q(), Q(), Q(), Q()
    ALLQ = (qpe, qact, qdve, qgp, qsp)

    s_pe, s_act, s_dve = S("s_pe"), S("s_act"), S("s_dve")
    s_sp = S("s_sp")
    s_out = S("s_out")
    s_ld0 = S("s_ld0")
    s_ld1 = S("s_ld1")
    s_ld2 = S("s_ld2")
    s_ld3 = S("s_ld3")
    s_xt = [S("s_xt0"), S("s_xt1")]
    s_wq = [S("s_wq0"), S("s_wq1")]
    s_kv = [S("s_kv0"), S("s_kv1")]
    s_aq = S("s_aq")
    s_wo = [S("s_wo0"), S("s_wo1")]
    s_xq = [S("s_xq0"), S("s_xq1")]
    h_cc = es.enter_context(nc.semaphore("s_cc"))
    s_cc = SemT(h_cc, "s_cc")
    s_ko = [S("s_ko0"), S("s_ko1")]
    s_vo = [S("s_vo0"), S("s_vo1")]
    s_qo = [S("s_qo0"), S("s_qo1")]
    s_ab = [S("s_ab0"), S("s_ab1")]
    s_ob = [S("s_ob0"), S("s_ob1"), S("s_ob2"), S("s_ob3")]

    def barrier():
        for q in ALLQ:
            for s in semlist:
                q.w(s, s.n)

    def fence_compute():
        for q in (qpe, qact, qdve):
            for s in (s_pe, s_act, s_dve, s_wo[0], s_wo[1], s_xq[0], s_xq[1], s_aq):
                q.w(s, s.n)

    pctx = ExitStack()

    def sb(name, shape, dt, ctx):
        return ctx.enter_context(nc.sbuf_tensor(name, shape, dt))

    def ps(name, shape, dt, ctx):
        return ctx.enter_context(nc.psum_tensor(name, shape, dt))

    ident_b = sb("ident_b", [128, 128], BF16, pctx)
    ones_b = sb("ones_b", [128, 128], BF16, pctx)
    kon = sb("kon", [128, 8], F32, pctx)
    EW = ER + NE
    aw_b = sb("aw_b", [128, NCH * EW], BF16, pctx)
    bS_b = sb("bS_b", [ER, C], BF16, pctx)
    em_b = sb("em_b", [NE, ER], BF16, pctx)

    t_id = qgp.do(lambda e: e.dma_start(ident_b[:, :], d_id), (s_ld0, 16))
    t_aT = qgp.do(lambda e: e.dma_start(
        aw_b[:, :].rearrange("p (c e) -> p c e", c=NCH),
        d_aw.rearrange("(c p) e -> p c e", p=128)), (s_ld0, 16))
    t_bS = qgp.do(lambda e: e.dma_start(bS_b[:, :], d_bS), (s_ld0, 16))
    t_em = qgp.do(lambda e: e.dma_start(em_b[:, :], d_em), (s_ld0, 16))
    tok_ld0 = s_ld0.n
    qdve.do(lambda e: e.memset(ones_b[:, :], 1.0), (s_dve, 1))
    qdve.do(lambda e: e.memset(kon[:, 0:1], -CSHIFT), (s_dve, 1))
    qdve.do(lambda e: e.memset(kon[:, 1:2], -0.5), (s_dve, 1))
    qdve.do(lambda e: e.memset(kon[:, 2:3], 0.0), (s_dve, 1))
    qdve.do(lambda e: e.memset(kon[:, 3:4], 1.0), (s_dve, 1))
    tok_kon = qdve.do(lambda e: e.memset(kon[:, 4:5], -1.0), (s_dve, 1))

    p1a = ExitStack()
    wk_sb = sb("wk_sb", [128, NCH * KVDh], BF16, p1a)
    wv_sb = sb("wv_sb", [128, NCH * KVDh], BF16, p1a)
    ckk_b = sb("ckk_b", [128, T], BF16, p1a)
    skk_b = sb("skk_b", [128, T], BF16, p1a)
    xts = [sb(f"xts{i}", [128, NCH * 256], BF16, p1a) for i in range(2)]
    rot_sb = sb("rot_sb", [128, 256], F32, p1a)
    t1_sb = sb("t1_sb", [128, 256], F32, p1a)
    t2_sb = sb("t2_sb", [128, 256], F32, p1a)
    kob = [sb(f"kob{i}", [128, 256], BF16, p1a) for i in range(2)]
    vob = [sb(f"vob{i}", [128, KVDh], BF16, p1a) for i in range(2)]
    NVS = (KVDh + 511) // 512
    pk = [ps(f"pk{i}", [128, 256], F32, p1a) for i in range(4)]
    pv = [ps(f"pv{i}", [128, min(512, KVDh - i * 512)], F32, p1a) for i in range(NVS)]

    csub = max(1, NCH // 4)
    for c0 in range(0, NCH, csub):
        c1 = min(c0 + csub, NCH)
        qgp.do(lambda e, c0=c0, c1=c1: e.dma_start(
            wk_sb[:, :].rearrange("p (c d) -> p c d", c=NCH)[:, c0:c1, :],
            d_wk.rearrange("(c p) d -> p c d", p=128)[:, c0:c1, :]), (s_ld1, 16))
        qgp.do(lambda e, c0=c0, c1=c1: e.dma_start(
            wv_sb[:, :].rearrange("p (c d) -> p c d", c=NCH)[:, c0:c1, :],
            d_wv.rearrange("(c p) d -> p c d", p=128)[:, c0:c1, :]), (s_ld1, 16))
    qgp.do(lambda e: e.dma_start(ckk_b[:, :], d_ckk), (s_ld1, 16))
    qgp.do(lambda e: e.dma_start(skk_b[:, :], d_skk), (s_ld1, 16))
    tok_ld1 = s_ld1.n

    NT1 = T // 256
    pk_free = [(None, 0)] * 4
    pk_free_act = [0] * 4
    kob_free = [0, 0]
    vob_free = [0, 0]
    xt_fills = [0, 0]
    xt_free_tok = [0, 0]
    rot_rd = 0
    t1_rd = 0
    pv_free = [0] * NVS

    for tt in range(min(NT1, NT1CAP)):
        bi = tt % 2
        qgp.w(s_pe, xt_free_tok[bi])
        for c0 in range(0, NCH, csub):
            c1 = min(c0 + csub, NCH)
            qgp.do(lambda e, c0=c0, c1=c1, bi=bi, tt=tt: e.dma_start(
                xts[bi][:, :].rearrange("p (c t) -> p c t", c=NCH)[:, c0:c1, :],
                d_xT.rearrange("(c p) t -> p c t", p=128)[:, c0:c1, tt * 256:(tt + 1) * 256]),
                (s_xt[bi], 16))
        xt_fills[bi] += 1
        tok_xt = 16 * (NCH // csub) * xt_fills[bi]

        qpe.w(s_ld1, tok_ld1)
        qpe.w(s_xt[bi], tok_xt)
        for ks in range(min(HKVh, HKVCAP) if SUB >= 2 else 0):
            pb = ks % 4
            qpe.w(s_dve, pk_free[pb][1])
            qpe.w(s_act, pk_free_act[pb])
            tokS = None
            for c in range(NCH):
                tokS = qpe.do(lambda e, pb=pb, c=c, ks=ks, bi=bi: e.matmul(
                    pk[pb][:, :],
                    wk_sb[:, :].rearrange("p (c d) -> p c d", c=NCH)[:, c, ks * 128:(ks + 1) * 128],
                    xts[bi][:, :].rearrange("p (c t) -> p c t", c=NCH)[:, c, :],
                    start=(c == 0), stop=(c == NCH - 1)),
                    (s_pe, 1))
            if SUB < 3:
                continue
            qact.w(s_pe, tokS)
            qact.w(s_dve, rot_rd)
            qact.do(lambda e, pb=pb: e.copy(rot_sb[0:64, :], pk[pb][64:128, :]), (s_act, 1))
            tokR = qact.do(lambda e, pb=pb: e.copy(rot_sb[64:128, :], pk[pb][0:64, :]), (s_act, 1))
            pk_free_act[pb] = tokR
            if SUB < 4:
                continue
            qdve.w(s_pe, tokS)
            qdve.w(s_act, tokR)
            qdve.w(s_ld1, tok_ld1)
            qdve.w(s_dve, t1_rd)
            tok1 = qdve.do(lambda e, pb=pb, tt=tt: e.tensor_tensor(
                t1_sb[:, :], pk[pb][:, :], ckk_b[:, tt * 256:(tt + 1) * 256], ALU.mult), (s_dve, 1))
            pk_free[pb] = (s_dve, tok1)
            t1_rd = tok1
            if SUB < 5:
                continue
            qdve.w(s_act, tokR)
            tok2 = qdve.do(lambda e, tt=tt: e.tensor_tensor(
                t2_sb[:, :], rotk_src(rot_sb), skk_b[:, tt * 256:(tt + 1) * 256], ALU.mult), (s_dve, 1))
            rot_rd = tok2
            if SUB < 6:
                continue
            ko = ks % 2
            qdve.w(s_dve, tok2)
            qdve.w(s_ko[ko], kob_free[ko])
            tokK = qdve.do(lambda e, ko=ko: e.tensor_tensor(
                kob[ko][:, :], t1_sb[:, :], t2_sb[:, :], ALU.add), (s_dve, 1))
            t1_rd = tokK
            if SUB < 7:
                continue
            qsp.w(s_dve, tokK)
            kob_free[ko] = qsp.do(lambda e, ks=ks, tt=tt, ko=ko: e.dma_start(
                s_kh[ks * 128:(ks + 1) * 128, tt * 256:(tt + 1) * 256], kob[ko][:, :]), (s_ko[ko], 16))
        last_pe_tok = None
        for tb in range(2 if SUB >= 8 else 0):
            vo = tb % 2
            for i in range(NVS):
                qpe.w(s_dve, pv_free[i])
            for c in range(NCH):
                for i in range(NVS):
                    w0 = i * 512
                    w1 = min(w0 + 512, KVDh)
                    last_pe_tok = qpe.do(lambda e, i=i, c=c, tb=tb, bi=bi, w0=w0, w1=w1: e.matmul(
                        pv[i][:, :],
                        xts[bi][:, :].rearrange("p (c t) -> p c t", c=NCH)[:, c, tb * 128:(tb + 1) * 128],
                        wv_sb[:, :].rearrange("p (c d) -> p c d", c=NCH)[:, c, w0:w1],
                        start=(c == 0), stop=(c == NCH - 1)),
                        (s_pe, 1))
            if SUB < 9:
                continue
            qdve.w(s_pe, last_pe_tok)
            qdve.w(s_vo[vo], vob_free[vo])
            tokV = None
            for i in range(NVS):
                w0 = i * 512
                w1 = min(w0 + 512, KVDh)
                tokV = qdve.do(lambda e, i=i, vo=vo, w0=w0, w1=w1: e.tensor_copy(
                    vob[vo][:, w0:w1], pv[i][:, :]), (s_dve, 1))
                pv_free[i] = tokV
            if SUB >= 10:
                qsp.w(s_dve, tokV)
                vob_free[vo] = qsp.do(lambda e, tt=tt, tb=tb, vo=vo: e.dma_start(
                    s_vh[(tt * 2 + tb) * 128:(tt * 2 + tb + 1) * 128, :], vob[vo][:, :]), (s_vo[vo], 16))
        if last_pe_tok is not None:
            xt_free_tok[bi] = last_pe_tok

    RG = [[2 * i, 2 * i + 1] for i in range(max(1, n_cores // 2))]
    spill_toks = [(sx, sx.n) for sx in (s_ko[0], s_ko[1], s_vo[0], s_vo[1])]

    def mk_cc():
        ccs = []
        for ks in range(HKVh):
            ccs.append(lambda e, ks=ks: e.collective_compute(
                "AllGather", ALU.bypass, replica_groups=RG,
                ins=[s_kh[ks * 128:(ks + 1) * 128, :].opt()],
                outs=[s_k[ks].rearrange("g p t -> (g p) t").opt()]))
        RQ = KTQ * 128
        for q in range(VCH):
            ccs.append(lambda e, q=q: e.collective_compute(
                "AllGather", ALU.bypass, replica_groups=RG,
                ins=[s_vh[q * RQ:(q + 1) * RQ, :].opt()],
                outs=[s_v[q].rearrange("g k p d -> (g k p) d").opt()]))
        return ccs

    cc_list = mk_cc()
    cc_total = len(cc_list)

    def emit_cc_chunk():
        if not cc_list:
            return
        for sx, v in spill_toks:
            qgp.w(sx, v)
        qgp.do(cc_list.pop(0), (s_cc, 1))

    p1a.close()
    barrier()

    if _phases < 2:
        qsp.w(s_ko[0], s_ko[0].n); qsp.w(s_ko[1], s_ko[1].n)
        qsp.w(s_vo[0], s_vo[0].n); qsp.w(s_vo[1], s_vo[1].n)
        return _finish(nc, ALLQ, es, pctx)

    p1b = ExitStack()
    xtq_sb = sb("xtq_sb", [128, NCH * QT], BF16, p1b)
    wqs = [sb(f"wqs{i}", [128, NCH * 128], BF16, p1b) for i in range(2)]
    ckq_b = sb("ckq_b", [128, QT], BF16, p1b)
    skq_b = sb("skq_b", [128, QT], BF16, p1b)
    rotq = sb("rotq", [128, QT], F32, p1b)
    t1q = sb("t1q", [128, QT], F32, p1b)
    t2q = sb("t2q", [128, QT], F32, p1b)
    qob = [sb(f"qob{i}", [128, QT], BF16, p1b) for i in range(2)]
    pq2 = [ps(f"pq{i}", [128, min(2048, QT)], F32, p1b) for i in range(2)]
    NQB = QT // 512

    for c0 in range(0, NCH, csub):
        c1 = min(c0 + csub, NCH)
        qgp.do(lambda e, c0=c0, c1=c1: e.dma_start(
            xtq_sb[:, :].rearrange("p (c t) -> p c t", c=NCH)[:, c0:c1, :],
            d_xTq.rearrange("(c p) t -> p c t", p=128)[:, c0:c1, :]), (s_ld2, 16))
    qgp.do(lambda e: e.dma_start(ckq_b[:, :], d_ckq), (s_ld2, 16))
    qgp.do(lambda e: e.dma_start(skq_b[:, :], d_skq), (s_ld2, 16))
    tok_ld2 = s_ld2.n

    wq_fills = [0, 0]
    wq_free = [0, 0]
    pq_free_dve = [0, 0]
    pq_free_act = [0, 0]
    rotq_rd = 0
    t1q_rd = 0
    qob_free = [0, 0]

    for qs in range(H):
        bi = qs % 2
        pq = pq2[qs % 2]
        qgp.w(s_pe, wq_free[bi])
        for c0 in range(0, NCH, csub):
            c1 = min(c0 + csub, NCH)
            qgp.do(lambda e, c0=c0, c1=c1, bi=bi, qs=qs: e.dma_start(
                wqs[bi][:, :].rearrange("p (c d) -> p c d", c=NCH)[:, c0:c1, :],
                d_wq.rearrange("(c p) d -> p c d", p=128)[:, c0:c1, qs * 128:(qs + 1) * 128]),
                (s_wq[bi], 16))
        wq_fills[bi] += 1
        tok_w = 16 * (NCH // csub) * wq_fills[bi]
        if qs >= 2 and qs % 3 == 2:
            emit_cc_chunk()

        qpe.w(s_ld2, tok_ld2)
        qpe.w(s_wq[bi], tok_w)
        qpe.w(s_dve, pq_free_dve[qs % 2])
        qpe.w(s_act, pq_free_act[qs % 2])
        tokS = None
        for c in range(NCH):
            for nb in range(NQB):
                tokS = qpe.do(lambda e, c=c, nb=nb, bi=bi, pq=pq: e.matmul(
                    pq[:, nb * 512:(nb + 1) * 512],
                    wqs[bi][:, :].rearrange("p (c d) -> p c d", c=NCH)[:, c, :],
                    xtq_sb[:, :].rearrange("p (c t) -> p c t", c=NCH)[:, c, nb * 512:(nb + 1) * 512],
                    start=(c == 0), stop=(c == NCH - 1)),
                    (s_pe, 1))
        wq_free[bi] = tokS
        qact.w(s_pe, tokS)
        qact.w(s_dve, rotq_rd)
        qact.do(lambda e, pq=pq: e.copy(rotq[0:64, :], pq[64:128, :]), (s_act, 1))
        tokR = qact.do(lambda e, pq=pq: e.copy(rotq[64:128, :], pq[0:64, :]), (s_act, 1))
        pq_free_act[qs % 2] = tokR
        qdve.w(s_pe, tokS)
        qdve.w(s_act, tokR)
        qdve.w(s_ld2, tok_ld2)
        qdve.w(s_dve, t1q_rd)
        tok1 = qdve.do(lambda e, pq=pq: e.tensor_tensor(t1q[:, :], pq[:, :], ckq_b[:, :], ALU.mult), (s_dve, 1))
        pq_free_dve[qs % 2] = tok1
        qdve.w(s_act, tokR)
        tok2 = qdve.do(lambda e: e.tensor_tensor(t2q[:, :], rotq[:, :], skq_b[:, :], ALU.mult), (s_dve, 1))
        rotq_rd = tok2
        qo = qs % 2
        qdve.w(s_dve, tok2)
        qdve.w(s_qo[qo], qob_free[qo])
        tokQ = qdve.do(lambda e, qo=qo: e.tensor_tensor(qob[qo][:, :], t1q[:, :], t2q[:, :], ALU.add), (s_dve, 1))
        t1q_rd = tokQ
        kvh, hig = qs // G, qs % G
        qsp.w(s_dve, tokQ)
        qob_free[qo] = qsp.do(lambda e, kvh=kvh, hig=hig, qo=qo: e.dma_start(
            s_q[kvh, :, :].rearrange("p (pr g t) -> p pr g t", pr=NPAIR, g=G)[:, :, hig, :],
            qob[qo][:, :].rearrange("p (pr t) -> p pr t", pr=NPAIR)), (s_qo[qo], 16))

    while cc_list:
        emit_cc_chunk()

    p1b.close()
    barrier()

    if _phases < 3:
        qsp.w(s_qo[0], s_qo[0].n); qsp.w(s_qo[1], s_qo[1].n)
        return _finish(nc, ALLQ, es, pctx)

    p2 = ExitStack()
    kb = [sb(f"kb{i}", [128, T], BF16, p2) for i in range(2)]
    praw = sb("praw", [128, GW], BF16, p2)
    pd_sb = sb("pd_sb", [128, GW], F32, p2)
    pl_sb = sb("pl_sb", [128, GW], F32, p2)
    vb = [sb(f"vb{i}", [128, NKT * 128], BF16, p2) for i in range(2)]
    qb = [sb(f"qb{i}", [128, NPAIR * GW], BF16, p2) for i in range(2)]
    mask_b = sb("mask_b", [128, NPAIR * 4 * 256], BF16, p2)
    ptb = [sb(f"ptb{i}", [128, GW], BF16, p2) for i in range(3)]
    rd_sb = sb("rd_sb", [128, GW], F32, p2)
    atb = [sb(f"atb{i}", [128, GW], BF16, p2) for i in range(2)]
    pS = [ps(f"pS{i}", [128, GW], F32, p2) for i in range(2)]
    pO = ps("pO", [128, GW], F32, p2)
    pD = ps("pD", [128, GW], F32, p2)
    NSB = GW // 512

    msub = max(1, (NPAIR * 4) // 4)
    for m0 in range(0, NPAIR * 4, msub):
        m1 = min(m0 + msub, NPAIR * 4)
        qgp.do(lambda e, m0=m0, m1=m1: e.dma_start(
            mask_b[:, :].rearrange("p (m t) -> p m t", m=NPAIR * 4)[:, m0:m1, :],
            d_mask.rearrange("(m p) t -> p m t", p=128)[:, m0:m1, :]), (s_ld3, 16))
    tok_ld3 = s_ld3.n

    ln_tok = 0
    pO_free = 0
    pD_free = 0
    kv_fills = [0, 0]
    kv_free = [0, 0]
    atb_free = [0, 0]
    pOD_free = 0
    rd_rd = 0

    def kv_load(kvh):
        bi = kvh % 2
        qgp.w(s_cc, cc_total)
        qgp.w(s_pe, kv_free[bi])
        grp, loc = kvh // HKVh, kvh % HKVh
        qgp.do(lambda e, bi=bi, grp=grp, loc=loc: e.dma_start(
            kb[bi][:, :], s_k[loc, grp, :, :]), (s_kv[bi], 16))
        for q in range(VCH):
            qgp.do(lambda e, bi=bi, q=q, grp=grp, loc=loc: e.dma_start(
                vb[bi][:, :].rearrange("p (k d) -> p k d", k=NKT)[:, q * KTQ:(q + 1) * KTQ, :],
                s_v[q, grp].rearrange("k p d -> p k d")[:, :, loc * 128:(loc + 1) * 128]),
                (s_kv[bi], 16))
        qgp.do(lambda e, kvh=kvh, bi=bi: e.dma_start(qb[bi][:, :], s_q[kvh, :, :]), (s_kv[bi], 16))
        kv_fills[bi] += 1
        return 16 * (2 + VCH) * kv_fills[bi]

    kv_tok = [0, 0]
    kv_tok[0] = kv_load(0)
    if HKV > 1:
        kv_tok[1] = kv_load(1)

    for kvh in range(HKV):
        bi = kvh % 2
        if kvh + 2 < HKV:
            pass
        qpe.w(s_kv[bi], kv_tok[bi])
        qpe.w(s_ld3, tok_ld3)
        qpe.w(s_dve, tok_kon)
        last_pe = None
        for pair in range(NPAIR):
            nkt = 4 * pair + 4
            exp_tok = {}
            rdy_tok = {}
            av_done = {}
            for kt in range(nkt + 2):
                if kt < nkt:
                    sx = pS[kt % 2]
                    if kt >= 2:
                        qpe.w(s_act, exp_tok[kt - 2])
                    tokS = None
                    for sbk in range(NSB):
                        tokS = qpe.do(lambda e, sx=sx, kt=kt, sbk=sbk, pair=pair, bi=bi: e.matmul(
                            sx[:, sbk * 512:(sbk + 1) * 512],
                            kb[bi][:, kt * 128:(kt + 1) * 128],
                            qb[bi][:, pair * GW + sbk * 512: pair * GW + (sbk + 1) * 512],
                            start=True, stop=True),
                            (s_pe, 1))
                    po = kt % 3
                    qact.w(s_pe, tokS)
                    qact.w(s_dve, tok_kon)
                    if kt >= 3:
                        qact.w(s_pe, av_done[kt - 3])
                    tE = qact.do(lambda e, sx=sx, po=po: e.activation(
                        ptb[po][:, :], sx[:, :], AF.Exp, bias=kon[:, 0:1]), (s_act, 1))
                    exp_tok[kt] = tE
                    rdy_tok[kt] = (s_act, tE)
                    if kt >= 4 * pair:
                        off = kt - 4 * pair
                        qdve.w(s_act, tE)
                        qdve.w(s_ld3, tok_ld3)
                        mcol = (pair * 4 + off) * 256
                        def mfn(e, po=po, mcol=mcol):
                            a3 = ptb[po][:, :].rearrange("p (g t) -> p g t", g=G)
                            m3 = mask_b[:, mcol:mcol + 256].rearrange("p (g t) -> p g t", g=1)
                            a3b, m3b = broadcast_tensor_aps(a3, m3)
                            return e.tensor_tensor(a3, a3b, m3b, ALU.mult)
                        tM = qdve.do(mfn, (s_dve, 1))
                        rdy_tok[kt] = (s_dve, tM)
                ktp = kt - 2
                if ktp >= 0:
                    sem, tok = rdy_tok[ktp]
                    qpe.w(sem, tok)
                    po = ktp % 3
                    t_last = None
                    if ktp == 0:
                        qpe.w(s_dve, pO_free)
                    for sbk in range(NSB):
                        qpe.do(lambda e, ktp=ktp, po=po, sbk=sbk, pair=pair, bi=bi, nkt=nkt: e.matmul(
                            pO[:, sbk * 512:(sbk + 1) * 512],
                            vb[bi][:, ktp * 128:(ktp + 1) * 128],
                            ptb[po][:, sbk * 512:(sbk + 1) * 512],
                            start=(ktp == 0), stop=(ktp == nkt - 1)),
                            (s_pe, 1))
                        if ktp == 0 and sbk == 0:
                            qpe.w(s_dve, pD_free)
                        t_last = qpe.do(lambda e, ktp=ktp, po=po, sbk=sbk, nkt=nkt: e.matmul(
                            pD[:, sbk * 512:(sbk + 1) * 512],
                            ones_b[:, :],
                            ptb[po][:, sbk * 512:(sbk + 1) * 512],
                            start=(ktp == 0), stop=(ktp == nkt - 1)),
                            (s_pe, 1))
                    av_done[ktp] = t_last
                    last_pe = t_last
            ao = pair % 2
            qdve.w(s_pe, last_pe)
            qdve.w(s_dve, rd_rd)
            qdve.w(s_act, ln_tok)
            tokCO = qdve.do(lambda e: e.tensor_copy(praw[:, :], pO[:, :]), (s_dve, 1))
            tokCP = qdve.do(lambda e: e.tensor_copy(pd_sb[:, :], pD[:, :]), (s_dve, 1))
            pO_free = tokCO
            pD_free = tokCP
            qact.w(s_dve, tokCP)
            t_ln = qact.do(lambda e: e.activation(pl_sb[:, :], pd_sb[:, :], AF.Ln), (s_act, 1))
            qact.w(s_act, t_ln)
            ln_tok = qact.do(lambda e: e.activation(
                rd_sb[:, :], pl_sb[:, :], AF.Exp, scale=kon[:, 4:5]), (s_act, 1))
            qdve.w(s_act, ln_tok)
            qdve.w(s_ab[ao], atb_free[ao])
            tokAT = qdve.do(lambda e, ao=ao: e.tensor_tensor(
                atb[ao][:, :], praw[:, :], rd_sb[:, :], ALU.mult), (s_dve, 1))
            rd_rd = tokAT
            qsp.w(s_dve, tokAT)
            atb_free[ao] = qsp.do(lambda e, kvh=kvh, pair=pair, ao=ao: e.dma_start(
                s_at[kvh, pair, :, :], atb[ao][:, :]), (s_ab[ao], 16))
        kv_free[bi] = last_pe
        if kvh + 2 < HKV:
            kv_tok[bi] = kv_load(kvh + 2)

    p2.close()
    barrier()

    if _phases < 4:
        qsp.w(s_ab[0], s_ab[0].n); qsp.w(s_ab[1], s_ab[1].n)
        return _finish(nc, ALLQ, es, pctx)

    RW = min(1024, C)
    NRND = C // RW
    NSQ = RW // 512

    p34 = ExitStack()
    atq = sb("atq", [128, 2 * HKV * GW], BF16, p34)
    wob = [sb(f"wob{i}", [128, NCH * CBW], BF16, p34) for i in range(2)]
    xqp = [sb(f"xqp{i}", [128, CBW], F32, p34) for i in range(2)]
    pw = [ps(f"pw{i}", [128, CBW], F32, p34) for i in range(2)]

    out_dma_count = 0
    atq_fills = 0
    atq_free = 0
    wo_fills = [0, 0]
    wo_free = [0, 0]
    xq_fills = [0, 0]
    xq_free = [0, 0]
    pw_free = [0, 0]
    trash_war = 0
    trans_last = 0
    pd2_rd = 0
    st1_war = 0
    ob_cnt = [0, 0, 0, 0]
    ytok = [(0, 0)] * 4

    def emit_quarter(qx):
        nonlocal atq_fills, atq_free, out_dma_count, trash_war, trans_last, pd2_rd, st1_war
        q4 = ExitStack()
        y_sb = sb(f"y_{qx}", [128, 4 * C], F32, q4)
        hbf = sb(f"hbf_{qx}", [128, 4 * C], BF16, q4)
        sums = sb(f"sums_{qx}", [128, 4 * NCB], F32, q4)
        sqs = sb(f"sqs_{qx}", [128, 4 * NCB], F32, q4)
        trash = sb(f"trash_{qx}", [128, 512], F32, q4)
        mst = sb(f"mst_{qx}", [128, 4], F32, q4)
        rst = sb(f"rst_{qx}", [128, 4], F32, q4)
        st1 = sb(f"st1_{qx}", [128, 6], F32, q4)
        gmid = sb(f"gmid_{qx}", [ER, 512], BF16, q4)

        qdve.w(s_dve, s_dve.n)
        qact.w(s_act, s_act.n)

        qsp.w(s_pe, atq_free)
        for kvh in range(HKV):
            for pri in range(2):
                qsp.do(lambda e, kvh=kvh, pri=pri: e.dma_start(
                    atq[:, (kvh * 2 + pri) * GW:(kvh * 2 + pri + 1) * GW],
                    s_at[kvh, 2 * qx + pri, :, :]), (s_aq, 16))
        atq_fills += 1
        tok_atq = 16 * 2 * HKV * atq_fills

        stt_tok = [[0] * NCB for _ in range(4)]
        sq_tok = [[0] * NCB for _ in range(4)]
        tokW = None
        for cb in range(NCB):
            wi = cb % 2
            qgp.w(s_pe, wo_free[wi])
            for c0 in range(0, NCH, csub):
                c1 = min(c0 + csub, NCH)
                qgp.do(lambda e, c0=c0, c1=c1, wi=wi, cb=cb: e.dma_start(
                    wob[wi][:, :].rearrange("p (c d) -> p c d", c=NCH)[:, c0:c1, :],
                    d_wo.rearrange("(c p) d -> p c d", p=128)[:, c0:c1, cb * CBW:(cb + 1) * CBW]),
                    (s_wo[wi], 16))
            wo_fills[wi] += 1
            tok_wo = 16 * (NCH // csub) * wo_fills[wi]
            for tb in range(4):
                pri, off = tb // 2, tb % 2
                xi = (cb * 4 + tb) % 2
                qsp.w(s_dve, xq_free[xi])
                qsp.do(lambda e, tb=tb, cb=cb, xi=xi: e.dma_start(
                    xqp[xi][:, :], d_xq[qx * 512 + tb * 128: qx * 512 + (tb + 1) * 128,
                                        cb * CBW:(cb + 1) * CBW]), (s_xq[xi], 16))
                xq_fills[xi] += 1
                tok_xq = 16 * xq_fills[xi]

                pwi = tb % 2
                qpe.w(s_aq, tok_atq)
                qpe.w(s_wo[wi], tok_wo)
                qpe.w(s_dve, pw_free[pwi])
                for h in range(H):
                    kvh, hig = h // G, h % G
                    scol = (kvh * 2 + pri) * GW + hig * 256 + off * 128
                    tokW = qpe.do(lambda e, h=h, scol=scol, wi=wi, pwi=pwi: e.matmul(
                        pw[pwi][:, :],
                        atq[:, scol:scol + 128],
                        wob[wi][:, :].rearrange("p (c d) -> p c d", c=NCH)[:, h, :],
                        start=(h == 0), stop=(h == H - 1)),
                        (s_pe, 1))
                par, cnt = ytok[tb]
                qdve.w(s_ob[par], cnt)
                qdve.w(s_pe, tokW)
                qdve.w(s_xq[xi], tok_xq)
                tokY = qdve.do(lambda e, tb=tb, cb=cb, pwi=pwi, xi=xi: e.scalar_tensor_tensor(
                    y_sb[:, tb * C + cb * CBW: tb * C + (cb + 1) * CBW],
                    pw[pwi][:, :], 0.0, xqp[xi][:, :], ALU.add, ALU.add,
                    accum_out=sums[:, tb * NCB + cb: tb * NCB + cb + 1]), (s_dve, 1))
                pw_free[pwi] = tokY
                xq_free[xi] = tokY
                stt_tok[tb][cb] = tokY
                qact.w(s_dve, tokY)
                qact.w(s_act, trash_war)
                tokQ2 = qact.do(lambda e, tb=tb, cb=cb: e.activation(
                    trash[:, 0:CBW], y_sb[:, tb * C + cb * CBW: tb * C + (cb + 1) * CBW],
                    AF.Square, accum_out=sqs[:, tb * NCB + cb: tb * NCB + cb + 1]), (s_act, 1))
                trash_war = tokQ2
                sq_tok[tb][cb] = tokQ2
            wo_free[wi] = tokW
        wo_last_pe = s_pe.n

        hbf_tok = [0] * 4
        norm_tok = [0] * 4
        st1_last = 0
        for tb in range(4):
            qdve.w(s_dve, stt_tok[tb][NCB - 1])
            qdve.w(s_act, sq_tok[tb][NCB - 1])
            qdve.w(s_dve, st1_last)
            qdve.w(s_act, st1_war)
            t1 = qdve.do(lambda e, tb=tb: e.reduce_sum(
                st1[:, 0:1], sums[:, tb * NCB:(tb + 1) * NCB], axis=AX.X), (s_dve, 1))
            t2 = qdve.do(lambda e, tb=tb: e.reduce_sum(
                st1[:, 1:2], sqs[:, tb * NCB:(tb + 1) * NCB], axis=AX.X), (s_dve, 1))
            qdve.w(s_dve, t2)
            t3 = qdve.do(lambda e, tb=tb: e.tensor_scalar_mul(
                mst[:, tb:tb + 1], st1[:, 0:1], 1.0 / C), (s_dve, 1))
            qdve.w(s_dve, t3)
            t4 = qdve.do(lambda e, tb=tb: e.tensor_tensor(
                st1[:, 2:3], mst[:, tb:tb + 1], mst[:, tb:tb + 1], ALU.mult), (s_dve, 1))
            qdve.w(s_dve, t4)
            t5 = qdve.do(lambda e: e.scalar_tensor_tensor(
                st1[:, 3:4], st1[:, 1:2], 1.0 / C, st1[:, 2:3], ALU.mult, ALU.subtract), (s_dve, 1))
            qdve.w(s_dve, t5)
            t6 = qdve.do(lambda e: e.tensor_scalar_add(st1[:, 4:5], st1[:, 3:4], EPS), (s_dve, 1))
            qact.w(s_dve, t6)
            t7 = qact.do(lambda e: e.activation(st1[:, 5:6], st1[:, 4:5], AF.Ln), (s_act, 1))
            qact.w(s_act, t7)
            t8 = qact.do(lambda e, tb=tb: e.activation(
                rst[:, tb:tb + 1], st1[:, 5:6], AF.Exp, scale=kon[:, 1:2]), (s_act, 1))
            qdve.w(s_act, t8)
            t9 = qdve.do(lambda e, tb=tb: e.tensor_scalar(
                y_sb[:, tb * C:(tb + 1) * C], y_sb[:, tb * C:(tb + 1) * C],
                mst[:, tb:tb + 1], rst[:, tb:tb + 1], ALU.subtract, ALU.mult), (s_dve, 1))
            norm_tok[tb] = t9
            st1_last = t6
            st1_war = t8
            qact.w(s_dve, t9)
            qact.w(s_pe, trans_last)
            hbf_tok[tb] = qact.do(lambda e, tb=tb: e.copy(
                hbf[:, tb * C:(tb + 1) * C], y_sb[:, tb * C:(tb + 1) * C]), (s_act, 1))

        RW2 = 512
        NR2 = C // RW2
        p4a = ExitStack()
        hT2 = [sb(f"hT_{qx}{i}", [128, NCH * 128], BF16, p4a) for i in range(2)]
        lg = sb(f"lg_{qx}", [32, 512], F32, p4a)
        tA = sb(f"tA_{qx}", [32, 512], F32, p4a)
        tB = sb(f"tB_{qx}", [32, 512], F32, p4a)
        tC = sb(f"tC_{qx}", [32, 512], F32, p4a)
        tD = sb(f"tD_{qx}", [32, 512], F32, p4a)
        gb = sb(f"gb_{qx}", [32, 512], BF16, p4a)
        g64 = sb(f"g64_{qx}", [ER, 512], BF16, p4a)
        sums2 = sb(f"sums2_{qx}", [128, 4 * NR2], F32, p4a)
        sqs2 = sb(f"sqs2_{qx}", [128, 4 * NR2], F32, p4a)
        ptr = [ps(f"ptr_{qx}{i}", [128, 128], BF16, p4a) for i in range(2)]
        pm68 = ps(f"pm68_{qx}", [EW, 512], F32, p4a)
        pg = ps(f"pg_{qx}", [ER, 512], F32, p4a)
        pd2 = [ps(f"pd2_{qx}{i}", [128, RW2], F32, p4a) for i in range(2)]

        perm1 = [(i + 1) % NE if i < NE else i for i in range(32)]
        perm2 = [(i + 2) % NE if i < NE else i for i in range(32)]
        ptr_free = [0, 0]
        hT_free = [0, 0]
        mid_tok = [0] * 4
        gate_tok = [0] * 4
        gmid_tok = [0] * 4
        pd2_free = [0, 0]
        lgz = qdve.do(lambda e: e.memset(lg[:, :], 0.0), (s_dve, 1))

        def emit_A(tb):
            nonlocal trans_last
            hi2 = tb % 2
            qpe.w(s_act, hbf_tok[tb])
            qpe.w(s_ld0, tok_ld0)
            qpe.w(s_dve, pd2_rd)
            ev_tok = 0
            for cc in range(NCH):
                pi = (tb * NCH + cc) % 2
                qpe.w(s_dve, ptr_free[pi])
                tokT = qpe.do(lambda e, tb=tb, cc=cc, pi=pi: e.transpose(
                    ptr[pi][:, :], hbf[:, tb * C + cc * 128: tb * C + (cc + 1) * 128],
                    ident_b[:, :]), (s_pe, 1))
                trans_last = tokT
                qdve.w(s_pe, tokT)
                qdve.w(s_pe, hT_free[hi2])
                ev_tok = qdve.do(lambda e, cc=cc, pi=pi, hi2=hi2: e.tensor_copy(
                    hT2[hi2][:, cc * 128:(cc + 1) * 128], ptr[pi][:, :]), (s_dve, 1))
                ptr_free[pi] = ev_tok
            qpe.w(s_dve, ev_tok)
            tokM = None
            for cc in range(NCH):
                tokM = qpe.do(lambda e, cc=cc, tb=tb, hi2=hi2: e.matmul(
                    pm68[:, tb * 128:(tb + 1) * 128],
                    aw_b[:, :].rearrange("p (c e) -> p c e", c=NCH)[:, cc, :],
                    hT2[hi2][:, cc * 128:(cc + 1) * 128],
                    start=(cc == 0), stop=(cc == NCH - 1)), (s_pe, 1))
            hT_free[hi2] = tokM
            mid_tok[tb] = tokM

        def emit_gate(tb):
            s0, s1 = tb * 128, (tb + 1) * 128
            L = lambda t: t[:, s0:s1]
            qdve.w(s_dve, lgz)
            qdve.w(s_pe, mid_tok[tb])
            tg = qdve.do(lambda e: e.tensor_copy(lg[0:NE, s0:s1], pm68[ER:EW, s0:s1]), (s_dve, 1))
            qdve.w(s_dve, tg)
            tg = qdve.do(lambda e: e.stream_shuffle(L(tA), L(lg), perm1), (s_dve, 1))
            qdve.w(s_dve, tg)
            tg = qdve.do(lambda e: e.tensor_max(L(tB), L(lg), L(tA)), (s_dve, 1))
            qdve.w(s_dve, tg)
            tg = qdve.do(lambda e: e.stream_shuffle(L(tA), L(tB), perm2), (s_dve, 1))
            qdve.w(s_dve, tg)
            tg = qdve.do(lambda e: e.tensor_max(L(tB), L(tB), L(tA)), (s_dve, 1))
            qdve.w(s_dve, tg)
            tg = qdve.do(lambda e: e.tensor_tensor(L(tC), L(lg), L(tB), ALU.is_equal), (s_dve, 1))
            qdve.w(s_dve, tg)
            tg = qdve.do(lambda e: e.scalar_tensor_tensor(
                L(tC), L(tC), -1e30, L(lg), ALU.mult, ALU.add), (s_dve, 1))
            qdve.w(s_dve, tg)
            tg = qdve.do(lambda e: e.stream_shuffle(L(tA), L(tC), perm1), (s_dve, 1))
            qdve.w(s_dve, tg)
            tg = qdve.do(lambda e: e.tensor_max(L(tC), L(tC), L(tA)), (s_dve, 1))
            qdve.w(s_dve, tg)
            tg = qdve.do(lambda e: e.stream_shuffle(L(tA), L(tC), perm2), (s_dve, 1))
            qdve.w(s_dve, tg)
            tg = qdve.do(lambda e: e.tensor_max(L(tC), L(tC), L(tA)), (s_dve, 1))
            qdve.w(s_dve, tg)
            tg = qdve.do(lambda e: e.tensor_tensor(L(tD), L(lg), L(tC), ALU.is_ge), (s_dve, 1))
            qdve.w(s_dve, tg)
            tg = qdve.do(lambda e: e.tensor_tensor(L(tA), L(lg), L(tB), ALU.subtract), (s_dve, 1))
            qact.w(s_dve, tg)
            ta = qact.do(lambda e: e.activation(
                L(tB), L(tA), AF.Exp, bias=kon[0:32, 2:3], scale=kon[0:32, 3:4]), (s_act, 1))
            qdve.w(s_act, ta)
            tg = qdve.do(lambda e: e.tensor_tensor(L(tB), L(tB), L(tD), ALU.mult), (s_dve, 1))
            qdve.w(s_dve, tg)
            tg = qdve.do(lambda e: e.stream_shuffle(L(tA), L(tB), perm1), (s_dve, 1))
            qdve.w(s_dve, tg)
            tg = qdve.do(lambda e: e.tensor_add(L(tC), L(tB), L(tA)), (s_dve, 1))
            qdve.w(s_dve, tg)
            tg = qdve.do(lambda e: e.stream_shuffle(L(tA), L(tC), perm2), (s_dve, 1))
            qdve.w(s_dve, tg)
            tg = qdve.do(lambda e: e.tensor_add(L(tC), L(tC), L(tA)), (s_dve, 1))
            qdve.w(s_dve, tg)
            tg = qdve.do(lambda e: e.reciprocal(L(tA), L(tC)), (s_dve, 1))
            qdve.w(s_dve, tg)
            tg = qdve.do(lambda e, sc=g.SCALE: e.scalar_tensor_tensor(
                L(tD), L(tB), sc, L(tA), ALU.mult, ALU.mult), (s_dve, 1))
            qdve.w(s_dve, tg)
            gate_tok[tb] = qdve.do(lambda e: e.tensor_copy(L(gb), L(tD)), (s_dve, 1))

        def emit_B1(tb):
            s0, s1 = tb * 128, (tb + 1) * 128
            qpe.w(s_dve, gate_tok[tb])
            qpe.w(s_ld0, tok_ld0)
            tokG = qpe.do(lambda e, s0=s0, s1=s1: e.matmul(
                pg[:, s0:s1], em_b[:, :], gb[0:NE, s0:s1], start=True, stop=True), (s_pe, 1))
            qdve.w(s_pe, tokG)
            tg = qdve.do(lambda e, s0=s0, s1=s1: e.tensor_copy(g64[:, s0:s1], pg[:, s0:s1]), (s_dve, 1))
            qdve.w(s_dve, tg)
            gmid_tok[tb] = qdve.do(lambda e, s0=s0, s1=s1: e.tensor_tensor(
                gmid[:, s0:s1], pm68[0:ER, s0:s1], g64[:, s0:s1], ALU.mult), (s_dve, 1))

        def emit_B2(tb):
            nonlocal trash_war, pd2_rd, out_dma_count, st1_last, st1_war
            y2_last = 0
            sq2_last = 0
            for rnd in range(NR2):
                di = (tb * NR2 + rnd) % 2
                qpe.w(s_dve, gmid_tok[tb])
                qpe.w(s_ld0, tok_ld0)
                qpe.w(s_dve, pd2_free[di])
                tokB = qpe.do(lambda e, tb=tb, rnd=rnd, di=di: e.matmul(
                    pd2[di][:, :],
                    gmid[:, tb * 128:(tb + 1) * 128],
                    bS_b[:, rnd * RW2:(rnd + 1) * RW2],
                    start=True, stop=True), (s_pe, 1))
                qdve.w(s_pe, tokB)
                qdve.w(s_dve, norm_tok[tb])
                qdve.w(s_act, hbf_tok[tb])
                tokY2 = qdve.do(lambda e, tb=tb, rnd=rnd, di=di: e.scalar_tensor_tensor(
                    y_sb[:, tb * C + rnd * RW2: tb * C + (rnd + 1) * RW2],
                    pd2[di][:, :], 0.0, y_sb[:, tb * C + rnd * RW2: tb * C + (rnd + 1) * RW2],
                    ALU.add, ALU.add,
                    accum_out=sums2[:, tb * NR2 + rnd: tb * NR2 + rnd + 1]), (s_dve, 1))
                pd2_free[di] = tokY2
                pd2_rd = tokY2
                y2_last = tokY2
                qact.w(s_dve, tokY2)
                qact.w(s_act, trash_war)
                tokS2 = qact.do(lambda e, tb=tb, rnd=rnd: e.activation(
                    trash[:, 0:RW2],
                    y_sb[:, tb * C + rnd * RW2: tb * C + (rnd + 1) * RW2],
                    AF.Square, accum_out=sqs2[:, tb * NR2 + rnd: tb * NR2 + rnd + 1]), (s_act, 1))
                trash_war = tokS2
                sq2_last = tokS2
            qdve.w(s_dve, y2_last)
            qdve.w(s_act, sq2_last)
            qdve.w(s_dve, st1_last)
            qdve.w(s_act, st1_war)
            t1 = qdve.do(lambda e, tb=tb: e.reduce_sum(
                st1[:, 0:1], sums2[:, tb * NR2:(tb + 1) * NR2], axis=AX.X), (s_dve, 1))
            t2 = qdve.do(lambda e, tb=tb: e.reduce_sum(
                st1[:, 1:2], sqs2[:, tb * NR2:(tb + 1) * NR2], axis=AX.X), (s_dve, 1))
            qdve.w(s_dve, t2)
            t3 = qdve.do(lambda e, tb=tb: e.tensor_scalar_mul(
                mst[:, tb:tb + 1], st1[:, 0:1], 1.0 / C), (s_dve, 1))
            qdve.w(s_dve, t3)
            t4 = qdve.do(lambda e, tb=tb: e.tensor_tensor(
                st1[:, 2:3], mst[:, tb:tb + 1], mst[:, tb:tb + 1], ALU.mult), (s_dve, 1))
            qdve.w(s_dve, t4)
            t5 = qdve.do(lambda e: e.scalar_tensor_tensor(
                st1[:, 3:4], st1[:, 1:2], 1.0 / C, st1[:, 2:3], ALU.mult, ALU.subtract), (s_dve, 1))
            qdve.w(s_dve, t5)
            t6 = qdve.do(lambda e: e.tensor_scalar_add(st1[:, 4:5], st1[:, 3:4], EPS), (s_dve, 1))
            qact.w(s_dve, t6)
            t7 = qact.do(lambda e: e.activation(st1[:, 5:6], st1[:, 4:5], AF.Ln), (s_act, 1))
            qact.w(s_act, t7)
            t8 = qact.do(lambda e, tb=tb: e.activation(
                rst[:, tb:tb + 1], st1[:, 5:6], AF.Exp, scale=kon[:, 1:2]), (s_act, 1))
            st1_last = t6
            st1_war = t8
            qdve.w(s_act, t8)
            t9o = qdve.do(lambda e, tb=tb: e.tensor_scalar(
                y_sb[:, tb * C:(tb + 1) * C], y_sb[:, tb * C:(tb + 1) * C],
                mst[:, tb:tb + 1], rst[:, tb:tb + 1], ALU.subtract, ALU.mult), (s_dve, 1))
            qsp.w(s_dve, t9o)
            ob_cnt[tb] += 16
            qsp.do(lambda e, tb=tb: e.dma_start(
                d_out[qx * 512 + tb * 128: qx * 512 + (tb + 1) * 128, :],
                y_sb[:, tb * C:(tb + 1) * C]), (s_ob[tb], 16))
            ytok[tb] = (tb, ob_cnt[tb])
            out_dma_count += 1

        for tb in range(4):
            emit_A(tb)
            emit_gate(tb)
            if tb >= 1:
                emit_B1(tb - 1)
            if tb >= 2:
                emit_B2(tb - 2)
        emit_B1(3)
        emit_B2(2)
        emit_B2(3)
        p4a.close()
        atq_free = wo_last_pe
        q4.close()

    for qx in range(NQ4):
        emit_quarter(qx)


    p34.close()

    for _s in s_ob:
        qsp.w(_s, _s.n)
    return _finish(nc, ALLQ, es, pctx)


def _finish(nc, ALLQ, es, pctx):
    qpe, qact, qdve, qgp, qsp = ALLQ
    with nc.Block() as block:
        @block.tensor
        def _(e):
            _replay(qpe, e)

        @block.scalar
        def _(e):
            _replay(qact, e)

        @block.vector
        def _(e):
            _replay(qdve, e)

        @block.gpsimd
        def _(e):
            _replay(qgp, e)

        @block.sync
        def _(e):
            _replay(qsp, e)

    es.close()
    pctx.close()
    return nc



def stage_core(g, core, X, Wq, Wk, Wv, Wo, A, B, Wr):
    T, C = g.T, g.C
    b, half = core // 2, core % 2
    js = jlist(g, half)
    qcols = np.concatenate([np.arange(j * 128, (j + 1) * 128) for j in js])
    Xb = np.asarray(X[b], dtype=np.float32)
    xT = np.ascontiguousarray(Xb.T)
    xTq = np.ascontiguousarray(Xb.T[:, qcols])
    xq = np.ascontiguousarray(Xb[qcols, :])

    inv = 1.0 / THETA ** (np.arange(64, dtype=np.float64) / 64.0)
    posk = np.arange(T, dtype=np.float64)
    angk = np.outer(posk, inv)
    ct = np.cos(angk).T.astype(np.float32)
    st = np.sin(angk).T.astype(np.float32)
    ckk = np.ascontiguousarray(np.vstack([ct, ct]))
    skk = np.ascontiguousarray(np.vstack([-st, st]))
    angq = angk[qcols]
    cq = np.cos(angq).T.astype(np.float32)
    sq = np.sin(angq).T.astype(np.float32)
    sc = 1.0 / math.sqrt(g.DH)
    ckq = np.ascontiguousarray(np.vstack([cq, cq]) * sc)
    skq = np.ascontiguousarray(np.vstack([-sq, sq]) * sc)

    m = np.zeros((g.NPAIR, 4, 128, 256), dtype=np.float32)
    pk = np.arange(128)
    for pr in range(g.NPAIR):
        for off in range(4):
            kt = 4 * pr + off
            kidx = kt * 128 + pk
            for sl2 in range(2):
                j = js[2 * pr + sl2]
                tok = j * 128 + np.arange(128)
                m[pr, off, :, sl2 * 128:(sl2 + 1) * 128] = (kidx[:, None] <= tok[None, :])
    maskq = np.ascontiguousarray(m.reshape(g.NPAIR * 4 * 128, 256))

    wkh = np.ascontiguousarray(Wk[:, half * g.KVDh:(half + 1) * g.KVDh])
    wvh = np.ascontiguousarray(Wv[:, half * g.KVDh:(half + 1) * g.KVDh])
    return {
        "xT": xT, "xTq": xTq, "xq": xq,
        "wq": Wq, "wk": wkh, "wv": wvh, "wo": Wo,
        "awT": None, "bS": None, "emat": None,
        "ckk": ckk, "skk": skk, "ckq": ckq, "skq": skq,
        "maskq": maskq, "ident": None,
    }


def kernel(L0_attn_X, L0_attn_Wq, L0_attn_Wk, L0_attn_Wv, L0_attn_Wo,
           L0_ffn_A, L0_ffn_B, L0_ffn_Wr, _geo=None, _ncores=None):
    g = _geo or Geo()
    X = np.asarray(L0_attn_X, dtype=np.float32)
    Wq = np.ascontiguousarray(np.asarray(L0_attn_Wq, dtype=np.float32))
    Wk = np.ascontiguousarray(np.asarray(L0_attn_Wk, dtype=np.float32))
    Wv = np.ascontiguousarray(np.asarray(L0_attn_Wv, dtype=np.float32))
    Wo = np.ascontiguousarray(np.asarray(L0_attn_Wo, dtype=np.float32))
    A = np.asarray(L0_ffn_A, dtype=np.float32)
    B = np.asarray(L0_ffn_B, dtype=np.float32)
    Wr = np.asarray(L0_ffn_Wr, dtype=np.float32)

    n_cores = _ncores if _ncores is not None else 2 * X.shape[0]
    awT = np.ascontiguousarray(
        np.concatenate([A.reshape(g.ER, g.C), Wr], axis=0).T)
    bS = np.ascontiguousarray(B.transpose(0, 2, 1).reshape(g.ER, g.C))
    emat = np.ascontiguousarray(np.kron(np.eye(g.NE, dtype=np.float32),
                                        np.ones((1, g.R), dtype=np.float32)))
    ident = np.eye(128, dtype=np.float32)

    in_maps = []
    for core in range(n_cores):
        d = stage_core(g, core, X, Wq, Wk, Wv, Wo, A, B, Wr)
        d["awT"], d["bS"], d["emat"], d["ident"] = awT, bS, emat, ident
        in_maps.append(d)

    nc = build_nc(g, n_cores=n_cores)
    res = run_bass_kernel_spmd(nc, in_maps, core_ids=list(range(n_cores)))

    out = np.empty((X.shape[0], g.T, g.C), dtype=np.float32)
    for core in range(n_cores):
        b, half = core // 2, core % 2
        js = jlist(g, half)
        qcols = np.concatenate([np.arange(j * 128, (j + 1) * 128) for j in js])
        out[b, qcols, :] = res.results[core]["out"]
    return out


# version 14
# speedup vs baseline: 1.0933x; 1.0933x over previous
import math
import numpy as np
from contextlib import ExitStack

import concourse.bass as bass
import concourse.mybir as mybir
from concourse.bass import broadcast_tensor_aps
from concourse.bass_utils import run_bass_kernel_spmd

F32 = mybir.dt.float32
BF16 = mybir.dt.bfloat16
AF = mybir.ActivationFunctionType
ALU = mybir.AluOpType
AX = mybir.AxisListType

THETA = 10000.0
EPS = 1e-5
CSHIFT = 12.0
SUB = 99
FENCE_Q = True
DBG_SCRATCH = False
NT1CAP = 999
HKVCAP = 999
SUB3MODE = 0
def rotk_src(t):
    return t[:, :]


class Geo:
    def __init__(self, T=4096, C=4096, H=32, HKV=8, NE=4, R=16):
        self.T, self.C, self.H, self.HKV, self.NE, self.R = T, C, H, HKV, NE, R
        self.DH = 128
        self.G = H // HKV
        self.NCH = C // 128
        self.KVD = HKV * 128
        self.HKVh = HKV // 2
        self.KVDh = self.KVD // 2
        self.NS = T // 256
        self.NPAIR = self.NS // 2
        self.NQ4 = self.NS // 4
        self.QT = T // 2
        self.NKT = T // 128
        self.ER = NE * R
        self.GW = self.G * 256
        self.SCALE = R / math.sqrt(R)
        assert self.NS % 4 == 0 and self.C % 512 == 0 and self.QT % 512 == 0


def jlist(g, half):
    if half == 0:
        return [2 * s + (s & 1) for s in range(g.NS)]
    return [2 * s + 1 - (s & 1) for s in range(g.NS)]


class SemT:
    __slots__ = ("h", "n", "name")

    def __init__(self, h, name):
        self.h, self.n, self.name = h, 0, name


class Q:
    __slots__ = ("items", "hi")

    def __init__(self):
        self.items = []
        self.hi = {}

    def w(self, sem, val):
        if val is None or val <= 0:
            return
        if self.hi.get(id(sem), -1) >= val:
            return
        self.hi[id(sem)] = val
        self.items.append((0, sem, val))

    def do(self, fn, *incs):
        toks = []
        for s, k in incs:
            s.n += k
            toks.append(s.n)
        self.items.append((1, fn, [(s, k) for s, k in incs]))
        if len(toks) == 1:
            return toks[0]
        return toks


def _replay(q, eng):
    for it in q.items:
        if it[0] == 0:
            eng.wait_ge(it[1].h, it[2])
        else:
            inst = it[1](eng)
            for s, k in it[2]:
                inst = inst.then_inc(s.h, k)


def build_nc(g, _phases=99, n_cores=8):
    nc = bass.Bass("TRN2", target_bir_lowering=False)
    T, C, H, HKV, NE, R = g.T, g.C, g.H, g.HKV, g.NE, g.R
    HKVh, KVDh = g.HKVh, g.KVDh
    NCH, KVD, NS, NPAIR, NQ4, QT, NKT, ER, G, GW = (
        g.NCH, g.KVD, g.NS, g.NPAIR, g.NQ4, g.QT, g.NKT, g.ER, g.G, g.GW)
    CBW = 256
    NCB = C // CBW

    d_xT = nc.dram_tensor("xT", [C, T], F32, kind="ExternalInput").ap()
    d_xTq = nc.dram_tensor("xTq", [C, QT], F32, kind="ExternalInput").ap()
    d_xq = nc.dram_tensor("xq", [QT, C], F32, kind="ExternalInput").ap()
    d_wq = nc.dram_tensor("wq", [C, C], F32, kind="ExternalInput").ap()
    d_wk = nc.dram_tensor("wk", [C, KVDh], F32, kind="ExternalInput").ap()
    d_wv = nc.dram_tensor("wv", [C, KVDh], F32, kind="ExternalInput").ap()
    d_wo = nc.dram_tensor("wo", [C, C], F32, kind="ExternalInput").ap()
    d_aw = nc.dram_tensor("awT", [C, ER + NE], F32, kind="ExternalInput").ap()
    d_bS = nc.dram_tensor("bS", [ER, C], F32, kind="ExternalInput").ap()
    d_em = nc.dram_tensor("emat", [NE, ER], F32, kind="ExternalInput").ap()
    d_ckk = nc.dram_tensor("ckk", [128, T], F32, kind="ExternalInput").ap()
    d_skk = nc.dram_tensor("skk", [128, T], F32, kind="ExternalInput").ap()
    d_ckq = nc.dram_tensor("ckq", [128, QT], F32, kind="ExternalInput").ap()
    d_skq = nc.dram_tensor("skq", [128, QT], F32, kind="ExternalInput").ap()
    d_mask = nc.dram_tensor("maskq", [NPAIR * 4 * 128, 256], F32, kind="ExternalInput").ap()
    d_id = nc.dram_tensor("ident", [128, 128], F32, kind="ExternalInput").ap()
    d_out = nc.dram_tensor("out", [QT, C], F32, kind="ExternalOutput").ap()
    d_dby = nc.dram_tensor("dby", [QT, C], F32, kind="ExternalOutput").ap() if DBG_SCRATCH else None
    s_dby = None

    _sk = "ExternalOutput" if DBG_SCRATCH else "Internal"
    VCH = 4 if NKT % 4 == 0 else 1
    KTQ = NKT // VCH
    s_kh = nc.dram_tensor("scr_kh", [HKVh * 128, T], BF16, kind="Internal").ap()
    s_k = nc.dram_tensor("scr_k", [HKVh, 2, 128, T], BF16, kind=_sk).ap()
    s_vh = nc.dram_tensor("scr_vh", [NKT * 128, KVDh], BF16, kind="Internal").ap()
    s_v = nc.dram_tensor("scr_v", [VCH, 2, KTQ, 128, KVDh], BF16, kind=_sk).ap()
    s_q = nc.dram_tensor("scr_q", [HKV, 128, NPAIR * GW], BF16, kind=_sk).ap()
    s_at = nc.dram_tensor("scr_at", [HKV, NPAIR, 128, GW], BF16, kind=_sk).ap()

    es = ExitStack()
    semlist = []

    def S(name):
        h = es.enter_context(nc.semaphore(name))
        s = SemT(h, name)
        semlist.append(s)
        return s

    qpe, qact, qdve, qgp, qsp = Q(), Q(), Q(), Q(), Q()
    ALLQ = (qpe, qact, qdve, qgp, qsp)

    s_pe, s_act, s_dve = S("s_pe"), S("s_act"), S("s_dve")
    s_sp = S("s_sp")
    s_out = S("s_out")
    s_ld0 = S("s_ld0")
    s_ld1 = S("s_ld1")
    s_ld2 = S("s_ld2")
    s_ld3 = S("s_ld3")
    s_xt = [S("s_xt0"), S("s_xt1")]
    s_wq = [S("s_wq0"), S("s_wq1")]
    s_kv = [S("s_kv0"), S("s_kv1")]
    s_aq = S("s_aq")
    s_wo = [S("s_wo0"), S("s_wo1")]
    s_xq = [S("s_xq0"), S("s_xq1")]
    h_cc = es.enter_context(nc.semaphore("s_cc"))
    s_cc = SemT(h_cc, "s_cc")
    s_ko = [S("s_ko0"), S("s_ko1")]
    s_vo = [S("s_vo0"), S("s_vo1")]
    s_qo = [S("s_qo0"), S("s_qo1")]
    s_ab = [S("s_ab0"), S("s_ab1")]
    s_ob = [S("s_ob0"), S("s_ob1"), S("s_ob2"), S("s_ob3")]

    def barrier():
        for q in ALLQ:
            for s in semlist:
                q.w(s, s.n)

    def fence_compute():
        for q in (qpe, qact, qdve):
            for s in (s_pe, s_act, s_dve, s_wo[0], s_wo[1], s_xq[0], s_xq[1], s_aq):
                q.w(s, s.n)

    pctx = ExitStack()

    def sb(name, shape, dt, ctx):
        return ctx.enter_context(nc.sbuf_tensor(name, shape, dt))

    def ps(name, shape, dt, ctx):
        return ctx.enter_context(nc.psum_tensor(name, shape, dt))

    ident_b = sb("ident_b", [128, 128], BF16, pctx)
    ones_b = sb("ones_b", [128, 128], BF16, pctx)
    kon = sb("kon", [128, 8], F32, pctx)
    EW = ER + NE
    aw_b = sb("aw_b", [128, NCH * EW], BF16, pctx)
    bS_b = sb("bS_b", [ER, C], BF16, pctx)
    em_b = sb("em_b", [NE, ER], BF16, pctx)

    t_id = qgp.do(lambda e: e.dma_start(ident_b[:, :], d_id), (s_ld0, 16))
    t_aT = qgp.do(lambda e: e.dma_start(
        aw_b[:, :].rearrange("p (c e) -> p c e", c=NCH),
        d_aw.rearrange("(c p) e -> p c e", p=128)), (s_ld0, 16))
    t_bS = qgp.do(lambda e: e.dma_start(bS_b[:, :], d_bS), (s_ld0, 16))
    t_em = qgp.do(lambda e: e.dma_start(em_b[:, :], d_em), (s_ld0, 16))
    tok_ld0 = s_ld0.n
    qdve.do(lambda e: e.memset(ones_b[:, :], 1.0), (s_dve, 1))
    qdve.do(lambda e: e.memset(kon[:, 0:1], -CSHIFT), (s_dve, 1))
    qdve.do(lambda e: e.memset(kon[:, 1:2], -0.5), (s_dve, 1))
    qdve.do(lambda e: e.memset(kon[:, 2:3], 0.0), (s_dve, 1))
    qdve.do(lambda e: e.memset(kon[:, 3:4], 1.0), (s_dve, 1))
    tok_kon = qdve.do(lambda e: e.memset(kon[:, 4:5], -1.0), (s_dve, 1))

    p1a = ExitStack()
    wk_sb = sb("wk_sb", [128, NCH * KVDh], BF16, p1a)
    wv_sb = sb("wv_sb", [128, NCH * KVDh], BF16, p1a)
    ckk_b = sb("ckk_b", [128, T], BF16, p1a)
    skk_b = sb("skk_b", [128, T], BF16, p1a)
    xts = [sb(f"xts{i}", [128, NCH * 256], BF16, p1a) for i in range(2)]
    rot_sb = sb("rot_sb", [128, 256], F32, p1a)
    t1_sb = sb("t1_sb", [128, 256], F32, p1a)
    t2_sb = sb("t2_sb", [128, 256], F32, p1a)
    kob = [sb(f"kob{i}", [128, 256], BF16, p1a) for i in range(2)]
    vob = [sb(f"vob{i}", [128, KVDh], BF16, p1a) for i in range(2)]
    NVS = (KVDh + 511) // 512
    pk = [ps(f"pk{i}", [128, 256], F32, p1a) for i in range(4)]
    pv = [ps(f"pv{i}", [128, min(512, KVDh - i * 512)], F32, p1a) for i in range(NVS)]

    csub = max(1, NCH // 4)
    for c0 in range(0, NCH, csub):
        c1 = min(c0 + csub, NCH)
        qgp.do(lambda e, c0=c0, c1=c1: e.dma_start(
            wk_sb[:, :].rearrange("p (c d) -> p c d", c=NCH)[:, c0:c1, :],
            d_wk.rearrange("(c p) d -> p c d", p=128)[:, c0:c1, :]), (s_ld1, 16))
        qgp.do(lambda e, c0=c0, c1=c1: e.dma_start(
            wv_sb[:, :].rearrange("p (c d) -> p c d", c=NCH)[:, c0:c1, :],
            d_wv.rearrange("(c p) d -> p c d", p=128)[:, c0:c1, :]), (s_ld1, 16))
    qgp.do(lambda e: e.dma_start(ckk_b[:, :], d_ckk), (s_ld1, 16))
    qgp.do(lambda e: e.dma_start(skk_b[:, :], d_skk), (s_ld1, 16))
    tok_ld1 = s_ld1.n

    NT1 = T // 256
    pk_free = [(None, 0)] * 4
    pk_free_act = [0] * 4
    kob_free = [0, 0]
    vob_free = [0, 0]
    xt_fills = [0, 0]
    xt_free_tok = [0, 0]
    rot_rd = 0
    t1_rd = 0
    pv_free = [0] * NVS

    for tt in range(min(NT1, NT1CAP)):
        bi = tt % 2
        qgp.w(s_pe, xt_free_tok[bi])
        for c0 in range(0, NCH, csub):
            c1 = min(c0 + csub, NCH)
            qgp.do(lambda e, c0=c0, c1=c1, bi=bi, tt=tt: e.dma_start(
                xts[bi][:, :].rearrange("p (c t) -> p c t", c=NCH)[:, c0:c1, :],
                d_xT.rearrange("(c p) t -> p c t", p=128)[:, c0:c1, tt * 256:(tt + 1) * 256]),
                (s_xt[bi], 16))
        xt_fills[bi] += 1
        tok_xt = 16 * (NCH // csub) * xt_fills[bi]

        qpe.w(s_ld1, tok_ld1)
        qpe.w(s_xt[bi], tok_xt)
        for ks in range(min(HKVh, HKVCAP) if SUB >= 2 else 0):
            pb = ks % 4
            qpe.w(s_dve, pk_free[pb][1])
            qpe.w(s_act, pk_free_act[pb])
            tokS = None
            for c in range(NCH):
                tokS = qpe.do(lambda e, pb=pb, c=c, ks=ks, bi=bi: e.matmul(
                    pk[pb][:, :],
                    wk_sb[:, :].rearrange("p (c d) -> p c d", c=NCH)[:, c, ks * 128:(ks + 1) * 128],
                    xts[bi][:, :].rearrange("p (c t) -> p c t", c=NCH)[:, c, :],
                    start=(c == 0), stop=(c == NCH - 1)),
                    (s_pe, 1))
            if SUB < 3:
                continue
            qact.w(s_pe, tokS)
            qact.w(s_dve, rot_rd)
            qact.do(lambda e, pb=pb: e.copy(rot_sb[0:64, :], pk[pb][64:128, :]), (s_act, 1))
            tokR = qact.do(lambda e, pb=pb: e.copy(rot_sb[64:128, :], pk[pb][0:64, :]), (s_act, 1))
            pk_free_act[pb] = tokR
            if SUB < 4:
                continue
            qdve.w(s_pe, tokS)
            qdve.w(s_act, tokR)
            qdve.w(s_ld1, tok_ld1)
            qdve.w(s_dve, t1_rd)
            tok1 = qdve.do(lambda e, pb=pb, tt=tt: e.tensor_tensor(
                t1_sb[:, :], pk[pb][:, :], ckk_b[:, tt * 256:(tt + 1) * 256], ALU.mult), (s_dve, 1))
            pk_free[pb] = (s_dve, tok1)
            t1_rd = tok1
            if SUB < 5:
                continue
            qdve.w(s_act, tokR)
            tok2 = qdve.do(lambda e, tt=tt: e.tensor_tensor(
                t2_sb[:, :], rotk_src(rot_sb), skk_b[:, tt * 256:(tt + 1) * 256], ALU.mult), (s_dve, 1))
            rot_rd = tok2
            if SUB < 6:
                continue
            ko = ks % 2
            qdve.w(s_dve, tok2)
            qdve.w(s_ko[ko], kob_free[ko])
            tokK = qdve.do(lambda e, ko=ko: e.tensor_tensor(
                kob[ko][:, :], t1_sb[:, :], t2_sb[:, :], ALU.add), (s_dve, 1))
            t1_rd = tokK
            if SUB < 7:
                continue
            qsp.w(s_dve, tokK)
            kob_free[ko] = qsp.do(lambda e, ks=ks, tt=tt, ko=ko: e.dma_start(
                s_kh[ks * 128:(ks + 1) * 128, tt * 256:(tt + 1) * 256], kob[ko][:, :]), (s_ko[ko], 16))
        last_pe_tok = None
        for tb in range(2 if SUB >= 8 else 0):
            vo = tb % 2
            for i in range(NVS):
                qpe.w(s_dve, pv_free[i])
            for c in range(NCH):
                for i in range(NVS):
                    w0 = i * 512
                    w1 = min(w0 + 512, KVDh)
                    last_pe_tok = qpe.do(lambda e, i=i, c=c, tb=tb, bi=bi, w0=w0, w1=w1: e.matmul(
                        pv[i][:, :],
                        xts[bi][:, :].rearrange("p (c t) -> p c t", c=NCH)[:, c, tb * 128:(tb + 1) * 128],
                        wv_sb[:, :].rearrange("p (c d) -> p c d", c=NCH)[:, c, w0:w1],
                        start=(c == 0), stop=(c == NCH - 1)),
                        (s_pe, 1))
            if SUB < 9:
                continue
            qdve.w(s_pe, last_pe_tok)
            qdve.w(s_vo[vo], vob_free[vo])
            tokV = None
            for i in range(NVS):
                w0 = i * 512
                w1 = min(w0 + 512, KVDh)
                tokV = qdve.do(lambda e, i=i, vo=vo, w0=w0, w1=w1: e.tensor_copy(
                    vob[vo][:, w0:w1], pv[i][:, :]), (s_dve, 1))
                pv_free[i] = tokV
            if SUB >= 10:
                qsp.w(s_dve, tokV)
                vob_free[vo] = qsp.do(lambda e, tt=tt, tb=tb, vo=vo: e.dma_start(
                    s_vh[(tt * 2 + tb) * 128:(tt * 2 + tb + 1) * 128, :], vob[vo][:, :]), (s_vo[vo], 16))
        if last_pe_tok is not None:
            xt_free_tok[bi] = last_pe_tok

    RG = [[2 * i, 2 * i + 1] for i in range(max(1, n_cores // 2))]
    spill_toks = [(sx, sx.n) for sx in (s_ko[0], s_ko[1], s_vo[0], s_vo[1])]

    def mk_cc():
        ccs = []
        for ks in range(HKVh):
            ccs.append(lambda e, ks=ks: e.collective_compute(
                "AllGather", ALU.bypass, replica_groups=RG,
                ins=[s_kh[ks * 128:(ks + 1) * 128, :].opt()],
                outs=[s_k[ks].rearrange("g p t -> (g p) t").opt()]))
        RQ = KTQ * 128
        for q in range(VCH):
            ccs.append(lambda e, q=q: e.collective_compute(
                "AllGather", ALU.bypass, replica_groups=RG,
                ins=[s_vh[q * RQ:(q + 1) * RQ, :].opt()],
                outs=[s_v[q].rearrange("g k p d -> (g k p) d").opt()]))
        return ccs

    cc_list = mk_cc()
    cc_total = len(cc_list)

    def emit_cc_chunk():
        if not cc_list:
            return
        for sx, v in spill_toks:
            qgp.w(sx, v)
        qgp.do(cc_list.pop(0), (s_cc, 1))

    p1a.close()
    barrier()

    if _phases < 2:
        qsp.w(s_ko[0], s_ko[0].n); qsp.w(s_ko[1], s_ko[1].n)
        qsp.w(s_vo[0], s_vo[0].n); qsp.w(s_vo[1], s_vo[1].n)
        return _finish(nc, ALLQ, es, pctx)

    p1b = ExitStack()
    xtq_sb = sb("xtq_sb", [128, NCH * QT], BF16, p1b)
    wqs = [sb(f"wqs{i}", [128, NCH * 128], BF16, p1b) for i in range(2)]
    ckq_b = sb("ckq_b", [128, QT], BF16, p1b)
    skq_b = sb("skq_b", [128, QT], BF16, p1b)
    rotq = sb("rotq", [128, QT], F32, p1b)
    t1q = sb("t1q", [128, QT], F32, p1b)
    t2q = sb("t2q", [128, QT], F32, p1b)
    qob = [sb(f"qob{i}", [128, QT], BF16, p1b) for i in range(2)]
    pq2 = [ps(f"pq{i}", [128, min(2048, QT)], F32, p1b) for i in range(2)]
    NQB = QT // 512

    for c0 in range(0, NCH, csub):
        c1 = min(c0 + csub, NCH)
        qgp.do(lambda e, c0=c0, c1=c1: e.dma_start(
            xtq_sb[:, :].rearrange("p (c t) -> p c t", c=NCH)[:, c0:c1, :],
            d_xTq.rearrange("(c p) t -> p c t", p=128)[:, c0:c1, :]), (s_ld2, 16))
    qgp.do(lambda e: e.dma_start(ckq_b[:, :], d_ckq), (s_ld2, 16))
    qgp.do(lambda e: e.dma_start(skq_b[:, :], d_skq), (s_ld2, 16))
    tok_ld2 = s_ld2.n

    wq_fills = [0, 0]
    wq_free = [0, 0]
    pq_free_dve = [0, 0]
    pq_free_act = [0, 0]
    rotq_rd = 0
    t1q_rd = 0
    qob_free = [0, 0]

    for qs in range(H):
        bi = qs % 2
        pq = pq2[qs % 2]
        qgp.w(s_pe, wq_free[bi])
        for c0 in range(0, NCH, csub):
            c1 = min(c0 + csub, NCH)
            qgp.do(lambda e, c0=c0, c1=c1, bi=bi, qs=qs: e.dma_start(
                wqs[bi][:, :].rearrange("p (c d) -> p c d", c=NCH)[:, c0:c1, :],
                d_wq.rearrange("(c p) d -> p c d", p=128)[:, c0:c1, qs * 128:(qs + 1) * 128]),
                (s_wq[bi], 16))
        wq_fills[bi] += 1
        tok_w = 16 * (NCH // csub) * wq_fills[bi]
        if qs >= 2 and qs % 3 == 2:
            emit_cc_chunk()

        qpe.w(s_ld2, tok_ld2)
        qpe.w(s_wq[bi], tok_w)
        qpe.w(s_dve, pq_free_dve[qs % 2])
        qpe.w(s_act, pq_free_act[qs % 2])
        tokS = None
        for c in range(NCH):
            for nb in range(NQB):
                tokS = qpe.do(lambda e, c=c, nb=nb, bi=bi, pq=pq: e.matmul(
                    pq[:, nb * 512:(nb + 1) * 512],
                    wqs[bi][:, :].rearrange("p (c d) -> p c d", c=NCH)[:, c, :],
                    xtq_sb[:, :].rearrange("p (c t) -> p c t", c=NCH)[:, c, nb * 512:(nb + 1) * 512],
                    start=(c == 0), stop=(c == NCH - 1)),
                    (s_pe, 1))
        wq_free[bi] = tokS
        qact.w(s_pe, tokS)
        qact.w(s_dve, rotq_rd)
        qact.do(lambda e, pq=pq: e.copy(rotq[0:64, :], pq[64:128, :]), (s_act, 1))
        tokR = qact.do(lambda e, pq=pq: e.copy(rotq[64:128, :], pq[0:64, :]), (s_act, 1))
        pq_free_act[qs % 2] = tokR
        qdve.w(s_pe, tokS)
        qdve.w(s_act, tokR)
        qdve.w(s_ld2, tok_ld2)
        qdve.w(s_dve, t1q_rd)
        tok1 = qdve.do(lambda e, pq=pq: e.tensor_tensor(t1q[:, :], pq[:, :], ckq_b[:, :], ALU.mult), (s_dve, 1))
        pq_free_dve[qs % 2] = tok1
        qdve.w(s_act, tokR)
        tok2 = qdve.do(lambda e: e.tensor_tensor(t2q[:, :], rotq[:, :], skq_b[:, :], ALU.mult), (s_dve, 1))
        rotq_rd = tok2
        qo = qs % 2
        qdve.w(s_dve, tok2)
        qdve.w(s_qo[qo], qob_free[qo])
        tokQ = qdve.do(lambda e, qo=qo: e.tensor_tensor(qob[qo][:, :], t1q[:, :], t2q[:, :], ALU.add), (s_dve, 1))
        t1q_rd = tokQ
        kvh, hig = qs // G, qs % G
        qsp.w(s_dve, tokQ)
        qob_free[qo] = qsp.do(lambda e, kvh=kvh, hig=hig, qo=qo: e.dma_start(
            s_q[kvh, :, :].rearrange("p (pr g t) -> p pr g t", pr=NPAIR, g=G)[:, :, hig, :],
            qob[qo][:, :].rearrange("p (pr t) -> p pr t", pr=NPAIR)), (s_qo[qo], 16))

    while cc_list:
        emit_cc_chunk()

    p1b.close()
    barrier()

    if _phases < 3:
        qsp.w(s_qo[0], s_qo[0].n); qsp.w(s_qo[1], s_qo[1].n)
        return _finish(nc, ALLQ, es, pctx)

    p2 = ExitStack()
    kb = [sb(f"kb{i}", [128, T], BF16, p2) for i in range(2)]
    praw = sb("praw", [128, GW], BF16, p2)
    pd_sb = sb("pd_sb", [128, GW], F32, p2)
    pl_sb = sb("pl_sb", [128, GW], F32, p2)
    vb = [sb(f"vb{i}", [128, NKT * 128], BF16, p2) for i in range(2)]
    qb = [sb(f"qb{i}", [128, NPAIR * GW], BF16, p2) for i in range(2)]
    mask_b = sb("mask_b", [128, NPAIR * 4 * 256], BF16, p2)
    ptb = [sb(f"ptb{i}", [128, GW], BF16, p2) for i in range(3)]
    rd_sb = sb("rd_sb", [128, GW], F32, p2)
    atb = [sb(f"atb{i}", [128, GW], BF16, p2) for i in range(2)]
    pS = [ps(f"pS{i}", [128, GW], F32, p2) for i in range(2)]
    pO = ps("pO", [128, GW], F32, p2)
    pD = ps("pD", [128, GW], F32, p2)
    NSB = GW // 512

    msub = max(1, (NPAIR * 4) // 4)
    for m0 in range(0, NPAIR * 4, msub):
        m1 = min(m0 + msub, NPAIR * 4)
        qgp.do(lambda e, m0=m0, m1=m1: e.dma_start(
            mask_b[:, :].rearrange("p (m t) -> p m t", m=NPAIR * 4)[:, m0:m1, :],
            d_mask.rearrange("(m p) t -> p m t", p=128)[:, m0:m1, :]), (s_ld3, 16))
    tok_ld3 = s_ld3.n

    ln_tok = 0
    pO_free = 0
    pD_free = 0
    kv_fills = [0, 0]
    kv_free = [0, 0]
    atb_free = [(None, 0), (None, 0)]
    pOD_free = 0
    rd_rd = 0

    def kv_load(kvh):
        bi = kvh % 2
        qgp.w(s_cc, cc_total)
        qgp.w(s_pe, kv_free[bi])
        grp, loc = kvh // HKVh, kvh % HKVh
        qgp.do(lambda e, bi=bi, grp=grp, loc=loc: e.dma_start(
            kb[bi][:, :], s_k[loc, grp, :, :]), (s_kv[bi], 16))
        for q in range(VCH):
            qgp.do(lambda e, bi=bi, q=q, grp=grp, loc=loc: e.dma_start(
                vb[bi][:, :].rearrange("p (k d) -> p k d", k=NKT)[:, q * KTQ:(q + 1) * KTQ, :],
                s_v[q, grp].rearrange("k p d -> p k d")[:, :, loc * 128:(loc + 1) * 128]),
                (s_kv[bi], 16))
        qgp.do(lambda e, kvh=kvh, bi=bi: e.dma_start(qb[bi][:, :], s_q[kvh, :, :]), (s_kv[bi], 16))
        kv_fills[bi] += 1
        return 16 * (2 + VCH) * kv_fills[bi]

    gktc = [0]
    exp_slot = [0, 0]
    av_slot = [0, 0, 0]
    pending_tail = []
    last_pe = 0

    def emit_av(pair, ktp, bi, nkt, rdy_tok, po_of):
        nonlocal last_pe
        sem, tok = rdy_tok[ktp]
        qpe.w(sem, tok)
        po = po_of[ktp]
        t_last = None
        if ktp == 0:
            qpe.w(s_dve, pO_free)
        for sbk in range(NSB):
            qpe.do(lambda e, ktp=ktp, po=po, sbk=sbk, bi=bi, nkt=nkt: e.matmul(
                pO[:, sbk * 512:(sbk + 1) * 512],
                vb[bi][:, ktp * 128:(ktp + 1) * 128],
                ptb[po][:, sbk * 512:(sbk + 1) * 512],
                start=(ktp == 0), stop=(ktp == nkt - 1)),
                (s_pe, 1))
            if ktp == 0 and sbk == 0:
                qpe.w(s_dve, pD_free)
            t_last = qpe.do(lambda e, ktp=ktp, po=po, sbk=sbk, nkt=nkt: e.matmul(
                pD[:, sbk * 512:(sbk + 1) * 512],
                ones_b[:, :],
                ptb[po][:, sbk * 512:(sbk + 1) * 512],
                start=(ktp == 0), stop=(ktp == nkt - 1)),
                (s_pe, 1))
        av_slot[po] = t_last
        last_pe = t_last

    def emit_evac(kvh, pair):
        nonlocal rd_rd, ln_tok, pO_free, pD_free
        ao = pair % 2
        qdve.w(s_pe, last_pe)
        qdve.w(s_dve, rd_rd)
        qdve.w(s_act, ln_tok)
        tokCO = qdve.do(lambda e: e.tensor_copy(praw[:, :], pO[:, :]), (s_dve, 1))
        tokCP = qdve.do(lambda e: e.tensor_copy(pd_sb[:, :], pD[:, :]), (s_dve, 1))
        pO_free = tokCO
        pD_free = tokCP
        qact.w(s_dve, tokCP)
        t_ln = qact.do(lambda e: e.activation(pl_sb[:, :], pd_sb[:, :], AF.Ln), (s_act, 1))
        qact.w(s_act, t_ln)
        ln_tok = qact.do(lambda e: e.activation(
            rd_sb[:, :], pl_sb[:, :], AF.Exp, scale=kon[:, 4:5]), (s_act, 1))
        qdve.w(s_act, ln_tok)
        ao_sem, ao_tok = atb_free[ao]
        qdve.w(ao_sem, ao_tok) if ao_sem is not None else None
        tokAT = qdve.do(lambda e, ao=ao: e.tensor_tensor(
            atb[ao][:, :], praw[:, :], rd_sb[:, :], ALU.mult), (s_dve, 1))
        rd_rd = tokAT
        qsp.w(s_dve, tokAT)
        atb_free[ao] = (s_ab[ao], qsp.do(lambda e, kvh=kvh, pair=pair, ao=ao: e.dma_start(
            s_at[kvh, pair, :, :], atb[ao][:, :]), (s_ab[ao], 16)))

    kv_tok = [0, 0]
    kv_tok[0] = kv_load(0)
    if HKV > 1:
        kv_tok[1] = kv_load(1)

    for kvh in range(HKV):
        bi = kvh % 2
        if kvh + 2 < HKV:
            pass
        qpe.w(s_kv[bi], kv_tok[bi])
        qpe.w(s_ld3, tok_ld3)
        qpe.w(s_dve, tok_kon)
        last_pe = None
        for pair in range(NPAIR):
            nkt = 4 * pair + 4
            exp_tok = {}
            rdy_tok = {}
            po_of = {}
            for kt in range(nkt):
                gkt = gktc[0]
                gktc[0] += 1
                sx = pS[gkt % 2]
                qpe.w(s_act, exp_slot[gkt % 2])
                tokS = None
                for sbk in range(NSB):
                    tokS = qpe.do(lambda e, sx=sx, kt=kt, sbk=sbk, pair=pair, bi=bi: e.matmul(
                        sx[:, sbk * 512:(sbk + 1) * 512],
                        kb[bi][:, kt * 128:(kt + 1) * 128],
                        qb[bi][:, pair * GW + sbk * 512: pair * GW + (sbk + 1) * 512],
                        start=True, stop=True),
                        (s_pe, 1))
                if kt == 1 and pending_tail:
                    pending_tail.pop(0)()
                po = gkt % 3
                po_of[kt] = po
                qact.w(s_pe, tokS)
                qact.w(s_dve, tok_kon)
                qact.w(s_pe, av_slot[po])
                tE = qact.do(lambda e, sx=sx, po=po: e.activation(
                    ptb[po][:, :], sx[:, :], AF.Exp, bias=kon[:, 0:1]), (s_act, 1))
                exp_tok[kt] = tE
                exp_slot[gkt % 2] = tE
                rdy_tok[kt] = (s_act, tE)
                if kt >= 4 * pair:
                    off = kt - 4 * pair
                    qdve.w(s_act, tE)
                    qdve.w(s_ld3, tok_ld3)
                    mcol = (pair * 4 + off) * 256
                    def mfn(e, po=po, mcol=mcol):
                        a3 = ptb[po][:, :].rearrange("p (g t) -> p g t", g=G)
                        m3 = mask_b[:, mcol:mcol + 256].rearrange("p (g t) -> p g t", g=1)
                        a3b, m3b = broadcast_tensor_aps(a3, m3)
                        return e.tensor_tensor(a3, a3b, m3b, ALU.mult)
                    tM = qdve.do(mfn, (s_dve, 1))
                    rdy_tok[kt] = (s_dve, tM)
                ktp = kt - 2
                if ktp >= 0:
                    emit_av(pair, ktp, bi, nkt, rdy_tok, po_of)
            def mk_tail(pair=pair, bi=bi, nkt=nkt, rdy_tok=rdy_tok, po_of=po_of, kvh=kvh):
                def tail():
                    emit_av(pair, nkt - 2, bi, nkt, rdy_tok, po_of)
                    emit_av(pair, nkt - 1, bi, nkt, rdy_tok, po_of)
                    emit_evac(kvh, pair)
                    if pair == NPAIR - 1:
                        kv_free[bi] = last_pe
                        if kvh + 2 < HKV:
                            kv_tok[bi] = kv_load(kvh + 2)
                return tail
            pending_tail.append(mk_tail())
        pass

    while pending_tail:
        pending_tail.pop(0)()

    p2.close()
    barrier()

    if _phases < 4:
        qsp.w(s_ab[0], s_ab[0].n); qsp.w(s_ab[1], s_ab[1].n)
        return _finish(nc, ALLQ, es, pctx)

    RW = min(1024, C)
    NRND = C // RW
    NSQ = RW // 512

    p34 = ExitStack()
    atq = sb("atq", [128, 2 * HKV * GW], BF16, p34)
    wob = [sb(f"wob{i}", [128, NCH * CBW], BF16, p34) for i in range(2)]
    xqp = [sb(f"xqp{i}", [128, CBW], F32, p34) for i in range(2)]
    pw = [ps(f"pw{i}", [128, CBW], F32, p34) for i in range(2)]

    out_dma_count = 0
    atq_fills = 0
    atq_free = 0
    wo_fills = [0, 0]
    wo_free = [0, 0]
    xq_fills = [0, 0]
    xq_free = [0, 0]
    pw_free = [0, 0]
    trash_war = 0
    trans_last = 0
    pd2_rd = 0
    st1_war = 0
    ob_cnt = [0, 0, 0, 0]
    ytok = [(0, 0)] * 4

    def emit_quarter(qx):
        nonlocal atq_fills, atq_free, out_dma_count, trash_war, trans_last, pd2_rd, st1_war
        q4 = ExitStack()
        y_sb = sb(f"y_{qx}", [128, 4 * C], F32, q4)
        hbf = sb(f"hbf_{qx}", [128, 4 * C], BF16, q4)
        sums = sb(f"sums_{qx}", [128, 4 * NCB], F32, q4)
        sqs = sb(f"sqs_{qx}", [128, 4 * NCB], F32, q4)
        trash = sb(f"trash_{qx}", [128, 512], F32, q4)
        mst = sb(f"mst_{qx}", [128, 4], F32, q4)
        rst = sb(f"rst_{qx}", [128, 4], F32, q4)
        st1 = sb(f"st1_{qx}", [128, 6], F32, q4)
        gmid = sb(f"gmid_{qx}", [ER, 512], BF16, q4)

        qdve.w(s_dve, s_dve.n)
        qact.w(s_act, s_act.n)

        qsp.w(s_pe, atq_free)
        for kvh in range(HKV):
            for pri in range(2):
                qsp.do(lambda e, kvh=kvh, pri=pri: e.dma_start(
                    atq[:, (kvh * 2 + pri) * GW:(kvh * 2 + pri + 1) * GW],
                    s_at[kvh, 2 * qx + pri, :, :]), (s_aq, 16))
        atq_fills += 1
        tok_atq = 16 * 2 * HKV * atq_fills

        stt_tok = [[0] * NCB for _ in range(4)]
        sq_tok = [[0] * NCB for _ in range(4)]
        tokW = None
        for cb in range(NCB):
            wi = cb % 2
            qgp.w(s_pe, wo_free[wi])
            for c0 in range(0, NCH, csub):
                c1 = min(c0 + csub, NCH)
                qgp.do(lambda e, c0=c0, c1=c1, wi=wi, cb=cb: e.dma_start(
                    wob[wi][:, :].rearrange("p (c d) -> p c d", c=NCH)[:, c0:c1, :],
                    d_wo.rearrange("(c p) d -> p c d", p=128)[:, c0:c1, cb * CBW:(cb + 1) * CBW]),
                    (s_wo[wi], 16))
            wo_fills[wi] += 1
            tok_wo = 16 * (NCH // csub) * wo_fills[wi]
            for tb in range(4):
                pri, off = tb // 2, tb % 2
                xi = (cb * 4 + tb) % 2
                qsp.w(s_dve, xq_free[xi])
                qsp.do(lambda e, tb=tb, cb=cb, xi=xi: e.dma_start(
                    xqp[xi][:, :], d_xq[qx * 512 + tb * 128: qx * 512 + (tb + 1) * 128,
                                        cb * CBW:(cb + 1) * CBW]), (s_xq[xi], 16))
                xq_fills[xi] += 1
                tok_xq = 16 * xq_fills[xi]

                pwi = tb % 2
                qpe.w(s_aq, tok_atq)
                qpe.w(s_wo[wi], tok_wo)
                qpe.w(s_dve, pw_free[pwi])
                for h in range(H):
                    kvh, hig = h // G, h % G
                    scol = (kvh * 2 + pri) * GW + hig * 256 + off * 128
                    tokW = qpe.do(lambda e, h=h, scol=scol, wi=wi, pwi=pwi: e.matmul(
                        pw[pwi][:, :],
                        atq[:, scol:scol + 128],
                        wob[wi][:, :].rearrange("p (c d) -> p c d", c=NCH)[:, h, :],
                        start=(h == 0), stop=(h == H - 1)),
                        (s_pe, 1))
                par, cnt = ytok[tb]
                qdve.w(s_ob[par], cnt)
                qdve.w(s_pe, tokW)
                qdve.w(s_xq[xi], tok_xq)
                tokY = qdve.do(lambda e, tb=tb, cb=cb, pwi=pwi, xi=xi: e.scalar_tensor_tensor(
                    y_sb[:, tb * C + cb * CBW: tb * C + (cb + 1) * CBW],
                    pw[pwi][:, :], 0.0, xqp[xi][:, :], ALU.add, ALU.add,
                    accum_out=sums[:, tb * NCB + cb: tb * NCB + cb + 1]), (s_dve, 1))
                pw_free[pwi] = tokY
                xq_free[xi] = tokY
                stt_tok[tb][cb] = tokY
                qact.w(s_dve, tokY)
                qact.w(s_act, trash_war)
                tokQ2 = qact.do(lambda e, tb=tb, cb=cb: e.activation(
                    trash[:, 0:CBW], y_sb[:, tb * C + cb * CBW: tb * C + (cb + 1) * CBW],
                    AF.Square, accum_out=sqs[:, tb * NCB + cb: tb * NCB + cb + 1]), (s_act, 1))
                trash_war = tokQ2
                sq_tok[tb][cb] = tokQ2
            wo_free[wi] = tokW
        wo_last_pe = s_pe.n

        hbf_tok = [0] * 4
        norm_tok = [0] * 4
        st1_last = 0
        for tb in range(4):
            qdve.w(s_dve, stt_tok[tb][NCB - 1])
            qdve.w(s_act, sq_tok[tb][NCB - 1])
            qdve.w(s_dve, st1_last)
            qdve.w(s_act, st1_war)
            t1 = qdve.do(lambda e, tb=tb: e.reduce_sum(
                st1[:, 0:1], sums[:, tb * NCB:(tb + 1) * NCB], axis=AX.X), (s_dve, 1))
            t2 = qdve.do(lambda e, tb=tb: e.reduce_sum(
                st1[:, 1:2], sqs[:, tb * NCB:(tb + 1) * NCB], axis=AX.X), (s_dve, 1))
            qdve.w(s_dve, t2)
            t3 = qdve.do(lambda e, tb=tb: e.tensor_scalar_mul(
                mst[:, tb:tb + 1], st1[:, 0:1], 1.0 / C), (s_dve, 1))
            qdve.w(s_dve, t3)
            t4 = qdve.do(lambda e, tb=tb: e.tensor_tensor(
                st1[:, 2:3], mst[:, tb:tb + 1], mst[:, tb:tb + 1], ALU.mult), (s_dve, 1))
            qdve.w(s_dve, t4)
            t5 = qdve.do(lambda e: e.scalar_tensor_tensor(
                st1[:, 3:4], st1[:, 1:2], 1.0 / C, st1[:, 2:3], ALU.mult, ALU.subtract), (s_dve, 1))
            qdve.w(s_dve, t5)
            t6 = qdve.do(lambda e: e.tensor_scalar_add(st1[:, 4:5], st1[:, 3:4], EPS), (s_dve, 1))
            qact.w(s_dve, t6)
            t7 = qact.do(lambda e: e.activation(st1[:, 5:6], st1[:, 4:5], AF.Ln), (s_act, 1))
            qact.w(s_act, t7)
            t8 = qact.do(lambda e, tb=tb: e.activation(
                rst[:, tb:tb + 1], st1[:, 5:6], AF.Exp, scale=kon[:, 1:2]), (s_act, 1))
            qdve.w(s_act, t8)
            t9 = qdve.do(lambda e, tb=tb: e.tensor_scalar(
                y_sb[:, tb * C:(tb + 1) * C], y_sb[:, tb * C:(tb + 1) * C],
                mst[:, tb:tb + 1], rst[:, tb:tb + 1], ALU.subtract, ALU.mult), (s_dve, 1))
            norm_tok[tb] = t9
            st1_last = t6
            st1_war = t8
            qact.w(s_dve, t9)
            qact.w(s_pe, trans_last)
            hbf_tok[tb] = qact.do(lambda e, tb=tb: e.copy(
                hbf[:, tb * C:(tb + 1) * C], y_sb[:, tb * C:(tb + 1) * C]), (s_act, 1))

        RW2 = 512
        NR2 = C // RW2
        p4a = ExitStack()
        hT2 = [sb(f"hT_{qx}{i}", [128, NCH * 128], BF16, p4a) for i in range(2)]
        lg = sb(f"lg_{qx}", [32, 512], F32, p4a)
        tA = sb(f"tA_{qx}", [32, 512], F32, p4a)
        tB = sb(f"tB_{qx}", [32, 512], F32, p4a)
        tC = sb(f"tC_{qx}", [32, 512], F32, p4a)
        tD = sb(f"tD_{qx}", [32, 512], F32, p4a)
        gb = sb(f"gb_{qx}", [32, 512], BF16, p4a)
        g64 = sb(f"g64_{qx}", [ER, 512], BF16, p4a)
        sums2 = sb(f"sums2_{qx}", [128, 4 * NR2], F32, p4a)
        sqs2 = sb(f"sqs2_{qx}", [128, 4 * NR2], F32, p4a)
        ptr = [ps(f"ptr_{qx}{i}", [128, 512], BF16, p4a) for i in range(2)]
        pm68 = ps(f"pm68_{qx}", [EW, 512], F32, p4a)
        pg = ps(f"pg_{qx}", [ER, 512], F32, p4a)
        pd2 = [ps(f"pd2_{qx}{i}", [128, RW2], F32, p4a) for i in range(2)]

        perm1 = [(i + 1) % NE if i < NE else i for i in range(32)]
        perm2 = [(i + 2) % NE if i < NE else i for i in range(32)]
        ptr_free = [0, 0]
        hT_free = [0, 0]
        mid_tok = [0] * 4
        gate_tok = [0] * 4
        gmid_tok = [0] * 4
        pd2_free = [0, 0]
        lgz = qdve.do(lambda e: e.memset(lg[:, :], 0.0), (s_dve, 1))

        def emit_A(tb):
            nonlocal trans_last
            hi2 = tb % 2
            qpe.w(s_act, hbf_tok[tb])
            qpe.w(s_ld0, tok_ld0)
            qpe.w(s_dve, pd2_rd)
            ev_tok = 0
            NGRP = (NCH + 3) // 4
            for grp_i in range(NGRP):
                pi = (tb * NGRP + grp_i) % 2
                qpe.w(s_dve, ptr_free[pi])
                tokT = None
                c0g = grp_i * 4
                c1g = min(c0g + 4, NCH)
                for cc in range(c0g, c1g):
                    tokT = qpe.do(lambda e, tb=tb, cc=cc, pi=pi, c0g=c0g: e.transpose(
                        ptr[pi][:, (cc - c0g) * 128:(cc - c0g + 1) * 128],
                        hbf[:, tb * C + cc * 128: tb * C + (cc + 1) * 128],
                        ident_b[:, :]), (s_pe, 1))
                trans_last = tokT
                qdve.w(s_pe, tokT)
                qdve.w(s_pe, hT_free[hi2])
                ev_tok = qdve.do(lambda e, c0g=c0g, c1g=c1g, pi=pi, hi2=hi2: e.tensor_copy(
                    hT2[hi2][:, c0g * 128:c1g * 128], ptr[pi][:, 0:(c1g - c0g) * 128]), (s_dve, 1))
                ptr_free[pi] = ev_tok
            qpe.w(s_dve, ev_tok)
            tokM = None
            for cc in range(NCH):
                tokM = qpe.do(lambda e, cc=cc, tb=tb, hi2=hi2: e.matmul(
                    pm68[:, tb * 128:(tb + 1) * 128],
                    aw_b[:, :].rearrange("p (c e) -> p c e", c=NCH)[:, cc, :],
                    hT2[hi2][:, cc * 128:(cc + 1) * 128],
                    start=(cc == 0), stop=(cc == NCH - 1)), (s_pe, 1))
            hT_free[hi2] = tokM
            mid_tok[tb] = tokM

        def emit_gate(tb):
            s0, s1 = tb * 128, (tb + 1) * 128
            L = lambda t: t[:, s0:s1]
            qdve.w(s_dve, lgz)
            qdve.w(s_pe, mid_tok[tb])
            tg = qdve.do(lambda e: e.tensor_copy(lg[0:NE, s0:s1], pm68[ER:EW, s0:s1]), (s_dve, 1))
            qdve.w(s_dve, tg)
            tg = qdve.do(lambda e: e.stream_shuffle(L(tA), L(lg), perm1), (s_dve, 1))
            qdve.w(s_dve, tg)
            tg = qdve.do(lambda e: e.tensor_max(L(tB), L(lg), L(tA)), (s_dve, 1))
            qdve.w(s_dve, tg)
            tg = qdve.do(lambda e: e.stream_shuffle(L(tA), L(tB), perm2), (s_dve, 1))
            qdve.w(s_dve, tg)
            tg = qdve.do(lambda e: e.tensor_max(L(tB), L(tB), L(tA)), (s_dve, 1))
            qdve.w(s_dve, tg)
            tg = qdve.do(lambda e: e.tensor_tensor(L(tC), L(lg), L(tB), ALU.is_equal), (s_dve, 1))
            qdve.w(s_dve, tg)
            tg = qdve.do(lambda e: e.scalar_tensor_tensor(
                L(tC), L(tC), -1e30, L(lg), ALU.mult, ALU.add), (s_dve, 1))
            qdve.w(s_dve, tg)
            tg = qdve.do(lambda e: e.stream_shuffle(L(tA), L(tC), perm1), (s_dve, 1))
            qdve.w(s_dve, tg)
            tg = qdve.do(lambda e: e.tensor_max(L(tC), L(tC), L(tA)), (s_dve, 1))
            qdve.w(s_dve, tg)
            tg = qdve.do(lambda e: e.stream_shuffle(L(tA), L(tC), perm2), (s_dve, 1))
            qdve.w(s_dve, tg)
            tg = qdve.do(lambda e: e.tensor_max(L(tC), L(tC), L(tA)), (s_dve, 1))
            qdve.w(s_dve, tg)
            tg = qdve.do(lambda e: e.tensor_tensor(L(tD), L(lg), L(tC), ALU.is_ge), (s_dve, 1))
            qdve.w(s_dve, tg)
            tg = qdve.do(lambda e: e.tensor_tensor(L(tA), L(lg), L(tB), ALU.subtract), (s_dve, 1))
            qact.w(s_dve, tg)
            ta = qact.do(lambda e: e.activation(
                L(tB), L(tA), AF.Exp, bias=kon[0:32, 2:3], scale=kon[0:32, 3:4]), (s_act, 1))
            qdve.w(s_act, ta)
            tg = qdve.do(lambda e: e.tensor_tensor(L(tB), L(tB), L(tD), ALU.mult), (s_dve, 1))
            qdve.w(s_dve, tg)
            tg = qdve.do(lambda e: e.stream_shuffle(L(tA), L(tB), perm1), (s_dve, 1))
            qdve.w(s_dve, tg)
            tg = qdve.do(lambda e: e.tensor_add(L(tC), L(tB), L(tA)), (s_dve, 1))
            qdve.w(s_dve, tg)
            tg = qdve.do(lambda e: e.stream_shuffle(L(tA), L(tC), perm2), (s_dve, 1))
            qdve.w(s_dve, tg)
            tg = qdve.do(lambda e: e.tensor_add(L(tC), L(tC), L(tA)), (s_dve, 1))
            qdve.w(s_dve, tg)
            tg = qdve.do(lambda e: e.reciprocal(L(tA), L(tC)), (s_dve, 1))
            qdve.w(s_dve, tg)
            tg = qdve.do(lambda e, sc=g.SCALE: e.scalar_tensor_tensor(
                L(tD), L(tB), sc, L(tA), ALU.mult, ALU.mult), (s_dve, 1))
            qdve.w(s_dve, tg)
            gate_tok[tb] = qdve.do(lambda e: e.tensor_copy(L(gb), L(tD)), (s_dve, 1))

        def emit_B1(tb):
            s0, s1 = tb * 128, (tb + 1) * 128
            qpe.w(s_dve, gate_tok[tb])
            qpe.w(s_ld0, tok_ld0)
            tokG = qpe.do(lambda e, s0=s0, s1=s1: e.matmul(
                pg[:, s0:s1], em_b[:, :], gb[0:NE, s0:s1], start=True, stop=True), (s_pe, 1))
            qdve.w(s_pe, tokG)
            tg = qdve.do(lambda e, s0=s0, s1=s1: e.tensor_copy(g64[:, s0:s1], pg[:, s0:s1]), (s_dve, 1))
            qdve.w(s_dve, tg)
            gmid_tok[tb] = qdve.do(lambda e, s0=s0, s1=s1: e.tensor_tensor(
                gmid[:, s0:s1], pm68[0:ER, s0:s1], g64[:, s0:s1], ALU.mult), (s_dve, 1))

        def emit_B2(tb):
            nonlocal trash_war, pd2_rd, out_dma_count, st1_last, st1_war
            y2_last = 0
            sq2_last = 0
            for rnd in range(NR2):
                di = (tb * NR2 + rnd) % 2
                qpe.w(s_dve, gmid_tok[tb])
                qpe.w(s_ld0, tok_ld0)
                qpe.w(s_dve, pd2_free[di])
                tokB = qpe.do(lambda e, tb=tb, rnd=rnd, di=di: e.matmul(
                    pd2[di][:, :],
                    gmid[:, tb * 128:(tb + 1) * 128],
                    bS_b[:, rnd * RW2:(rnd + 1) * RW2],
                    start=True, stop=True), (s_pe, 1))
                qdve.w(s_pe, tokB)
                qdve.w(s_dve, norm_tok[tb])
                qdve.w(s_act, hbf_tok[tb])
                tokY2 = qdve.do(lambda e, tb=tb, rnd=rnd, di=di: e.scalar_tensor_tensor(
                    y_sb[:, tb * C + rnd * RW2: tb * C + (rnd + 1) * RW2],
                    pd2[di][:, :], 0.0, y_sb[:, tb * C + rnd * RW2: tb * C + (rnd + 1) * RW2],
                    ALU.add, ALU.add,
                    accum_out=sums2[:, tb * NR2 + rnd: tb * NR2 + rnd + 1]), (s_dve, 1))
                pd2_free[di] = tokY2
                pd2_rd = tokY2
                y2_last = tokY2
                qact.w(s_dve, tokY2)
                qact.w(s_act, trash_war)
                tokS2 = qact.do(lambda e, tb=tb, rnd=rnd: e.activation(
                    trash[:, 0:RW2],
                    y_sb[:, tb * C + rnd * RW2: tb * C + (rnd + 1) * RW2],
                    AF.Square, accum_out=sqs2[:, tb * NR2 + rnd: tb * NR2 + rnd + 1]), (s_act, 1))
                trash_war = tokS2
                sq2_last = tokS2
            qdve.w(s_dve, y2_last)
            qdve.w(s_act, sq2_last)
            qdve.w(s_dve, st1_last)
            qdve.w(s_act, st1_war)
            t1 = qdve.do(lambda e, tb=tb: e.reduce_sum(
                st1[:, 0:1], sums2[:, tb * NR2:(tb + 1) * NR2], axis=AX.X), (s_dve, 1))
            t2 = qdve.do(lambda e, tb=tb: e.reduce_sum(
                st1[:, 1:2], sqs2[:, tb * NR2:(tb + 1) * NR2], axis=AX.X), (s_dve, 1))
            qdve.w(s_dve, t2)
            t3 = qdve.do(lambda e, tb=tb: e.tensor_scalar_mul(
                mst[:, tb:tb + 1], st1[:, 0:1], 1.0 / C), (s_dve, 1))
            qdve.w(s_dve, t3)
            t4 = qdve.do(lambda e, tb=tb: e.tensor_tensor(
                st1[:, 2:3], mst[:, tb:tb + 1], mst[:, tb:tb + 1], ALU.mult), (s_dve, 1))
            qdve.w(s_dve, t4)
            t5 = qdve.do(lambda e: e.scalar_tensor_tensor(
                st1[:, 3:4], st1[:, 1:2], 1.0 / C, st1[:, 2:3], ALU.mult, ALU.subtract), (s_dve, 1))
            qdve.w(s_dve, t5)
            t6 = qdve.do(lambda e: e.tensor_scalar_add(st1[:, 4:5], st1[:, 3:4], EPS), (s_dve, 1))
            qact.w(s_dve, t6)
            t7 = qact.do(lambda e: e.activation(st1[:, 5:6], st1[:, 4:5], AF.Ln), (s_act, 1))
            qact.w(s_act, t7)
            t8 = qact.do(lambda e, tb=tb: e.activation(
                rst[:, tb:tb + 1], st1[:, 5:6], AF.Exp, scale=kon[:, 1:2]), (s_act, 1))
            st1_last = t6
            st1_war = t8
            qdve.w(s_act, t8)
            t9o = qdve.do(lambda e, tb=tb: e.tensor_scalar(
                y_sb[:, tb * C:(tb + 1) * C], y_sb[:, tb * C:(tb + 1) * C],
                mst[:, tb:tb + 1], rst[:, tb:tb + 1], ALU.subtract, ALU.mult), (s_dve, 1))
            qsp.w(s_dve, t9o)
            ob_cnt[tb] += 16
            qsp.do(lambda e, tb=tb: e.dma_start(
                d_out[qx * 512 + tb * 128: qx * 512 + (tb + 1) * 128, :],
                y_sb[:, tb * C:(tb + 1) * C]), (s_ob[tb], 16))
            ytok[tb] = (tb, ob_cnt[tb])
            out_dma_count += 1

        for tb in range(4):
            emit_A(tb)
            emit_gate(tb)
            if tb >= 1:
                emit_B1(tb - 1)
            if tb >= 2:
                emit_B2(tb - 2)
        emit_B1(3)
        emit_B2(2)
        emit_B2(3)
        p4a.close()
        atq_free = wo_last_pe
        q4.close()

    for qx in range(NQ4):
        emit_quarter(qx)


    p34.close()

    for _s in s_ob:
        qsp.w(_s, _s.n)
    return _finish(nc, ALLQ, es, pctx)


def _finish(nc, ALLQ, es, pctx):
    qpe, qact, qdve, qgp, qsp = ALLQ
    with nc.Block() as block:
        @block.tensor
        def _(e):
            _replay(qpe, e)

        @block.scalar
        def _(e):
            _replay(qact, e)

        @block.vector
        def _(e):
            _replay(qdve, e)

        @block.gpsimd
        def _(e):
            _replay(qgp, e)

        @block.sync
        def _(e):
            _replay(qsp, e)

    es.close()
    pctx.close()
    return nc



def stage_core(g, core, X, Wq, Wk, Wv, Wo, A, B, Wr):
    T, C = g.T, g.C
    b, half = core // 2, core % 2
    js = jlist(g, half)
    qcols = np.concatenate([np.arange(j * 128, (j + 1) * 128) for j in js])
    Xb = np.asarray(X[b], dtype=np.float32)
    xT = np.ascontiguousarray(Xb.T)
    xTq = np.ascontiguousarray(Xb.T[:, qcols])
    xq = np.ascontiguousarray(Xb[qcols, :])

    inv = 1.0 / THETA ** (np.arange(64, dtype=np.float64) / 64.0)
    posk = np.arange(T, dtype=np.float64)
    angk = np.outer(posk, inv)
    ct = np.cos(angk).T.astype(np.float32)
    st = np.sin(angk).T.astype(np.float32)
    ckk = np.ascontiguousarray(np.vstack([ct, ct]))
    skk = np.ascontiguousarray(np.vstack([-st, st]))
    angq = angk[qcols]
    cq = np.cos(angq).T.astype(np.float32)
    sq = np.sin(angq).T.astype(np.float32)
    sc = 1.0 / math.sqrt(g.DH)
    ckq = np.ascontiguousarray(np.vstack([cq, cq]) * sc)
    skq = np.ascontiguousarray(np.vstack([-sq, sq]) * sc)

    m = np.zeros((g.NPAIR, 4, 128, 256), dtype=np.float32)
    pk = np.arange(128)
    for pr in range(g.NPAIR):
        for off in range(4):
            kt = 4 * pr + off
            kidx = kt * 128 + pk
            for sl2 in range(2):
                j = js[2 * pr + sl2]
                tok = j * 128 + np.arange(128)
                m[pr, off, :, sl2 * 128:(sl2 + 1) * 128] = (kidx[:, None] <= tok[None, :])
    maskq = np.ascontiguousarray(m.reshape(g.NPAIR * 4 * 128, 256))

    wkh = np.ascontiguousarray(Wk[:, half * g.KVDh:(half + 1) * g.KVDh])
    wvh = np.ascontiguousarray(Wv[:, half * g.KVDh:(half + 1) * g.KVDh])
    return {
        "xT": xT, "xTq": xTq, "xq": xq,
        "wq": Wq, "wk": wkh, "wv": wvh, "wo": Wo,
        "awT": None, "bS": None, "emat": None,
        "ckk": ckk, "skk": skk, "ckq": ckq, "skq": skq,
        "maskq": maskq, "ident": None,
    }


def kernel(L0_attn_X, L0_attn_Wq, L0_attn_Wk, L0_attn_Wv, L0_attn_Wo,
           L0_ffn_A, L0_ffn_B, L0_ffn_Wr, _geo=None, _ncores=None):
    g = _geo or Geo()
    X = np.asarray(L0_attn_X, dtype=np.float32)
    Wq = np.ascontiguousarray(np.asarray(L0_attn_Wq, dtype=np.float32))
    Wk = np.ascontiguousarray(np.asarray(L0_attn_Wk, dtype=np.float32))
    Wv = np.ascontiguousarray(np.asarray(L0_attn_Wv, dtype=np.float32))
    Wo = np.ascontiguousarray(np.asarray(L0_attn_Wo, dtype=np.float32))
    A = np.asarray(L0_ffn_A, dtype=np.float32)
    B = np.asarray(L0_ffn_B, dtype=np.float32)
    Wr = np.asarray(L0_ffn_Wr, dtype=np.float32)

    n_cores = _ncores if _ncores is not None else 2 * X.shape[0]
    awT = np.ascontiguousarray(
        np.concatenate([A.reshape(g.ER, g.C), Wr], axis=0).T)
    bS = np.ascontiguousarray(B.transpose(0, 2, 1).reshape(g.ER, g.C))
    emat = np.ascontiguousarray(np.kron(np.eye(g.NE, dtype=np.float32),
                                        np.ones((1, g.R), dtype=np.float32)))
    ident = np.eye(128, dtype=np.float32)

    in_maps = []
    for core in range(n_cores):
        d = stage_core(g, core, X, Wq, Wk, Wv, Wo, A, B, Wr)
        d["awT"], d["bS"], d["emat"], d["ident"] = awT, bS, emat, ident
        in_maps.append(d)

    nc = build_nc(g, n_cores=n_cores)
    res = run_bass_kernel_spmd(nc, in_maps, core_ids=list(range(n_cores)))

    out = np.empty((X.shape[0], g.T, g.C), dtype=np.float32)
    for core in range(n_cores):
        b, half = core // 2, core % 2
        js = jlist(g, half)
        qcols = np.concatenate([np.arange(j * 128, (j + 1) * 128) for j in js])
        out[b, qcols, :] = res.results[core]["out"]
    return out


# version 15
# speedup vs baseline: 1.1493x; 1.0512x over previous
import math
import numpy as np
from contextlib import ExitStack

import concourse.bass as bass
import concourse.mybir as mybir
from concourse.bass import broadcast_tensor_aps
from concourse.bass_utils import run_bass_kernel_spmd

F32 = mybir.dt.float32
BF16 = mybir.dt.bfloat16
AF = mybir.ActivationFunctionType
ALU = mybir.AluOpType
AX = mybir.AxisListType

THETA = 10000.0
EPS = 1e-5
CSHIFT = 12.0
SUB = 99
FENCE_Q = True
DBG_SCRATCH = False
NT1CAP = 999
HKVCAP = 999
SUB3MODE = 0
def rotk_src(t):
    return t[:, :]


class Geo:
    def __init__(self, T=4096, C=4096, H=32, HKV=8, NE=4, R=16):
        self.T, self.C, self.H, self.HKV, self.NE, self.R = T, C, H, HKV, NE, R
        self.DH = 128
        self.G = H // HKV
        self.NCH = C // 128
        self.KVD = HKV * 128
        self.HKVh = HKV // 2
        self.KVDh = self.KVD // 2
        self.NS = T // 256
        self.NPAIR = self.NS // 2
        self.NQ4 = self.NS // 4
        self.QT = T // 2
        self.NKT = T // 128
        self.ER = NE * R
        self.GW = self.G * 256
        self.SCALE = R / math.sqrt(R)
        assert self.NS % 4 == 0 and self.C % 512 == 0 and self.QT % 512 == 0


def jlist(g, half):
    if half == 0:
        return [2 * s + (s & 1) for s in range(g.NS)]
    return [2 * s + 1 - (s & 1) for s in range(g.NS)]


class SemT:
    __slots__ = ("h", "n", "name")

    def __init__(self, h, name):
        self.h, self.n, self.name = h, 0, name


class Q:
    __slots__ = ("items", "hi")

    def __init__(self):
        self.items = []
        self.hi = {}

    def w(self, sem, val):
        if val is None or val <= 0:
            return
        if self.hi.get(id(sem), -1) >= val:
            return
        self.hi[id(sem)] = val
        self.items.append((0, sem, val))

    def do(self, fn, *incs):
        toks = []
        for s, k in incs:
            s.n += k
            toks.append(s.n)
        self.items.append((1, fn, [(s, k) for s, k in incs]))
        if len(toks) == 1:
            return toks[0]
        return toks


def _replay(q, eng):
    for it in q.items:
        if it[0] == 0:
            eng.wait_ge(it[1].h, it[2])
        else:
            inst = it[1](eng)
            for s, k in it[2]:
                inst = inst.then_inc(s.h, k)


def build_nc(g, _phases=99, n_cores=8):
    nc = bass.Bass("TRN2", target_bir_lowering=False)
    T, C, H, HKV, NE, R = g.T, g.C, g.H, g.HKV, g.NE, g.R
    HKVh, KVDh = g.HKVh, g.KVDh
    NCH, KVD, NS, NPAIR, NQ4, QT, NKT, ER, G, GW = (
        g.NCH, g.KVD, g.NS, g.NPAIR, g.NQ4, g.QT, g.NKT, g.ER, g.G, g.GW)
    CBW = 256
    NCB = C // CBW

    d_xT = nc.dram_tensor("xT", [C, T], F32, kind="ExternalInput").ap()
    d_xTq = nc.dram_tensor("xTq", [C, QT], F32, kind="ExternalInput").ap()
    d_xq = nc.dram_tensor("xq", [QT, C], F32, kind="ExternalInput").ap()
    d_wq = nc.dram_tensor("wq", [C, C], F32, kind="ExternalInput").ap()
    d_wk = nc.dram_tensor("wk", [C, KVDh], F32, kind="ExternalInput").ap()
    d_wv = nc.dram_tensor("wv", [C, KVDh], F32, kind="ExternalInput").ap()
    d_wo = nc.dram_tensor("wo", [C, C], F32, kind="ExternalInput").ap()
    d_aw = nc.dram_tensor("awT", [C, ER + NE], F32, kind="ExternalInput").ap()
    d_bS = nc.dram_tensor("bS", [ER, C], F32, kind="ExternalInput").ap()
    d_em = nc.dram_tensor("emat", [NE, ER], F32, kind="ExternalInput").ap()
    d_ckk = nc.dram_tensor("ckk", [128, T], F32, kind="ExternalInput").ap()
    d_skk = nc.dram_tensor("skk", [128, T], F32, kind="ExternalInput").ap()
    d_ckq = nc.dram_tensor("ckq", [128, QT], F32, kind="ExternalInput").ap()
    d_skq = nc.dram_tensor("skq", [128, QT], F32, kind="ExternalInput").ap()
    d_mask = nc.dram_tensor("maskq", [NPAIR * 4 * 128, 256], F32, kind="ExternalInput").ap()
    d_id = nc.dram_tensor("ident", [128, 128], F32, kind="ExternalInput").ap()
    d_out = nc.dram_tensor("out", [QT, C], F32, kind="ExternalOutput").ap()
    d_dby = nc.dram_tensor("dby", [QT, C], F32, kind="ExternalOutput").ap() if DBG_SCRATCH else None
    s_dby = None

    _sk = "ExternalOutput" if DBG_SCRATCH else "Internal"
    VCH = 4 if NKT % 4 == 0 else 1
    KTQ = NKT // VCH
    s_kh = nc.dram_tensor("scr_kh", [HKVh * 128, T], BF16, kind="Internal").ap()
    s_k = nc.dram_tensor("scr_k", [HKVh, 2, 128, T], BF16, kind=_sk).ap()
    s_vh = nc.dram_tensor("scr_vh", [NKT * 128, KVDh], BF16, kind="Internal").ap()
    s_wosc = nc.dram_tensor("scr_wo", [NCH, 128, C], BF16, kind="Internal").ap()
    s_v = nc.dram_tensor("scr_v", [VCH, 2, KTQ, 128, KVDh], BF16, kind=_sk).ap()
    s_q = nc.dram_tensor("scr_q", [HKV, 128, NPAIR * GW], BF16, kind=_sk).ap()
    s_at = nc.dram_tensor("scr_at", [HKV, NPAIR, 128, GW], BF16, kind=_sk).ap()

    es = ExitStack()
    semlist = []

    def S(name):
        h = es.enter_context(nc.semaphore(name))
        s = SemT(h, name)
        semlist.append(s)
        return s

    qpe, qact, qdve, qgp, qsp = Q(), Q(), Q(), Q(), Q()
    ALLQ = (qpe, qact, qdve, qgp, qsp)

    s_pe, s_act, s_dve = S("s_pe"), S("s_act"), S("s_dve")
    s_sp = S("s_sp")
    s_out = S("s_out")
    s_ld0 = S("s_ld0")
    s_ld1 = S("s_ld1")
    s_ld2 = S("s_ld2")
    s_ld3 = S("s_ld3")
    s_xt = [S("s_xt0"), S("s_xt1")]
    s_wq = [S("s_wq0"), S("s_wq1")]
    s_kv = [S("s_kv0"), S("s_kv1")]
    s_aq = S("s_aq")
    s_wos = S("s_wos")
    s_wo = [S("s_wo0"), S("s_wo1")]
    s_xq = [S("s_xq0"), S("s_xq1")]
    h_cc = es.enter_context(nc.semaphore("s_cc"))
    s_cc = SemT(h_cc, "s_cc")
    s_ko = [S("s_ko0"), S("s_ko1")]
    s_vo = [S("s_vo0"), S("s_vo1")]
    s_qo = [S("s_qo0"), S("s_qo1")]
    s_ab = [S("s_ab0"), S("s_ab1")]
    s_ob = [S("s_ob0"), S("s_ob1"), S("s_ob2"), S("s_ob3")]

    def barrier():
        for q in ALLQ:
            for s in semlist:
                q.w(s, s.n)

    def fence_compute():
        for q in (qpe, qact, qdve):
            for s in (s_pe, s_act, s_dve, s_wo[0], s_wo[1], s_xq[0], s_xq[1], s_aq):
                q.w(s, s.n)

    pctx = ExitStack()

    def sb(name, shape, dt, ctx):
        return ctx.enter_context(nc.sbuf_tensor(name, shape, dt))

    def ps(name, shape, dt, ctx):
        return ctx.enter_context(nc.psum_tensor(name, shape, dt))

    ident_b = sb("ident_b", [128, 128], BF16, pctx)
    ones_b = sb("ones_b", [128, 128], BF16, pctx)
    kon = sb("kon", [128, 8], F32, pctx)
    EW = ER + NE
    aw_b = sb("aw_b", [128, NCH * EW], BF16, pctx)
    bS_b = sb("bS_b", [ER, C], BF16, pctx)
    em_b = sb("em_b", [NE, ER], BF16, pctx)

    t_id = qgp.do(lambda e: e.dma_start(ident_b[:, :], d_id), (s_ld0, 16))
    t_aT = qgp.do(lambda e: e.dma_start(
        aw_b[:, :].rearrange("p (c e) -> p c e", c=NCH),
        d_aw.rearrange("(c p) e -> p c e", p=128)), (s_ld0, 16))
    t_bS = qgp.do(lambda e: e.dma_start(bS_b[:, :], d_bS), (s_ld0, 16))
    t_em = qgp.do(lambda e: e.dma_start(em_b[:, :], d_em), (s_ld0, 16))
    tok_ld0 = s_ld0.n
    qdve.do(lambda e: e.memset(ones_b[:, :], 1.0), (s_dve, 1))
    qdve.do(lambda e: e.memset(kon[:, 0:1], -CSHIFT), (s_dve, 1))
    qdve.do(lambda e: e.memset(kon[:, 1:2], -0.5), (s_dve, 1))
    qdve.do(lambda e: e.memset(kon[:, 2:3], 0.0), (s_dve, 1))
    qdve.do(lambda e: e.memset(kon[:, 3:4], 1.0), (s_dve, 1))
    tok_kon = qdve.do(lambda e: e.memset(kon[:, 4:5], -1.0), (s_dve, 1))

    p1a = ExitStack()
    wk_sb = sb("wk_sb", [128, NCH * KVDh], BF16, p1a)
    wv_sb = sb("wv_sb", [128, NCH * KVDh], BF16, p1a)
    ckk_b = sb("ckk_b", [128, T], BF16, p1a)
    skk_b = sb("skk_b", [128, T], BF16, p1a)
    xts = [sb(f"xts{i}", [128, NCH * 256], BF16, p1a) for i in range(2)]
    rot_sb = sb("rot_sb", [128, 256], F32, p1a)
    t1_sb = sb("t1_sb", [128, 256], F32, p1a)
    t2_sb = sb("t2_sb", [128, 256], F32, p1a)
    kob = [sb(f"kob{i}", [128, 256], BF16, p1a) for i in range(2)]
    vob = [sb(f"vob{i}", [128, KVDh], BF16, p1a) for i in range(2)]
    NVS = (KVDh + 511) // 512
    pk = [ps(f"pk{i}", [128, 256], F32, p1a) for i in range(4)]
    pv = [ps(f"pv{i}", [128, min(512, KVDh - i * 512)], F32, p1a) for i in range(NVS)]

    csub = max(1, NCH // 4)
    for c0 in range(0, NCH, csub):
        c1 = min(c0 + csub, NCH)
        qgp.do(lambda e, c0=c0, c1=c1: e.dma_start(
            wk_sb[:, :].rearrange("p (c d) -> p c d", c=NCH)[:, c0:c1, :],
            d_wk.rearrange("(c p) d -> p c d", p=128)[:, c0:c1, :]), (s_ld1, 16))
        qgp.do(lambda e, c0=c0, c1=c1: e.dma_start(
            wv_sb[:, :].rearrange("p (c d) -> p c d", c=NCH)[:, c0:c1, :],
            d_wv.rearrange("(c p) d -> p c d", p=128)[:, c0:c1, :]), (s_ld1, 16))
    qgp.do(lambda e: e.dma_start(ckk_b[:, :], d_ckk), (s_ld1, 16))
    qgp.do(lambda e: e.dma_start(skk_b[:, :], d_skk), (s_ld1, 16))
    tok_ld1 = s_ld1.n

    NT1 = T // 256
    pk_free = [(None, 0)] * 4
    pk_free_act = [0] * 4
    kob_free = [0, 0]
    vob_free = [0, 0]
    xt_fills = [0, 0]
    xt_free_tok = [0, 0]
    rot_rd = 0
    t1_rd = 0
    pv_free = [0] * NVS

    for tt in range(min(NT1, NT1CAP)):
        bi = tt % 2
        qgp.w(s_pe, xt_free_tok[bi])
        for c0 in range(0, NCH, csub):
            c1 = min(c0 + csub, NCH)
            qgp.do(lambda e, c0=c0, c1=c1, bi=bi, tt=tt: e.dma_start(
                xts[bi][:, :].rearrange("p (c t) -> p c t", c=NCH)[:, c0:c1, :],
                d_xT.rearrange("(c p) t -> p c t", p=128)[:, c0:c1, tt * 256:(tt + 1) * 256]),
                (s_xt[bi], 16))
        xt_fills[bi] += 1
        tok_xt = 16 * (NCH // csub) * xt_fills[bi]

        qpe.w(s_ld1, tok_ld1)
        qpe.w(s_xt[bi], tok_xt)
        for ks in range(min(HKVh, HKVCAP) if SUB >= 2 else 0):
            pb = ks % 4
            qpe.w(s_dve, pk_free[pb][1])
            qpe.w(s_act, pk_free_act[pb])
            tokS = None
            for c in range(NCH):
                tokS = qpe.do(lambda e, pb=pb, c=c, ks=ks, bi=bi: e.matmul(
                    pk[pb][:, :],
                    wk_sb[:, :].rearrange("p (c d) -> p c d", c=NCH)[:, c, ks * 128:(ks + 1) * 128],
                    xts[bi][:, :].rearrange("p (c t) -> p c t", c=NCH)[:, c, :],
                    start=(c == 0), stop=(c == NCH - 1)),
                    (s_pe, 1))
            if SUB < 3:
                continue
            qact.w(s_pe, tokS)
            qact.w(s_dve, rot_rd)
            qact.do(lambda e, pb=pb: e.copy(rot_sb[0:64, :], pk[pb][64:128, :]), (s_act, 1))
            tokR = qact.do(lambda e, pb=pb: e.copy(rot_sb[64:128, :], pk[pb][0:64, :]), (s_act, 1))
            pk_free_act[pb] = tokR
            if SUB < 4:
                continue
            qdve.w(s_pe, tokS)
            qdve.w(s_act, tokR)
            qdve.w(s_ld1, tok_ld1)
            qdve.w(s_dve, t1_rd)
            tok1 = qdve.do(lambda e, pb=pb, tt=tt: e.tensor_tensor(
                t1_sb[:, :], pk[pb][:, :], ckk_b[:, tt * 256:(tt + 1) * 256], ALU.mult), (s_dve, 1))
            pk_free[pb] = (s_dve, tok1)
            t1_rd = tok1
            if SUB < 5:
                continue
            qdve.w(s_act, tokR)
            tok2 = qdve.do(lambda e, tt=tt: e.tensor_tensor(
                t2_sb[:, :], rotk_src(rot_sb), skk_b[:, tt * 256:(tt + 1) * 256], ALU.mult), (s_dve, 1))
            rot_rd = tok2
            if SUB < 6:
                continue
            ko = ks % 2
            qdve.w(s_dve, tok2)
            qdve.w(s_ko[ko], kob_free[ko])
            tokK = qdve.do(lambda e, ko=ko: e.tensor_tensor(
                kob[ko][:, :], t1_sb[:, :], t2_sb[:, :], ALU.add), (s_dve, 1))
            t1_rd = tokK
            if SUB < 7:
                continue
            qsp.w(s_dve, tokK)
            kob_free[ko] = qsp.do(lambda e, ks=ks, tt=tt, ko=ko: e.dma_start(
                s_kh[ks * 128:(ks + 1) * 128, tt * 256:(tt + 1) * 256], kob[ko][:, :]), (s_ko[ko], 16))
        last_pe_tok = None
        for tb in range(2 if SUB >= 8 else 0):
            vo = tb % 2
            for i in range(NVS):
                qpe.w(s_dve, pv_free[i])
            for c in range(NCH):
                for i in range(NVS):
                    w0 = i * 512
                    w1 = min(w0 + 512, KVDh)
                    last_pe_tok = qpe.do(lambda e, i=i, c=c, tb=tb, bi=bi, w0=w0, w1=w1: e.matmul(
                        pv[i][:, :],
                        xts[bi][:, :].rearrange("p (c t) -> p c t", c=NCH)[:, c, tb * 128:(tb + 1) * 128],
                        wv_sb[:, :].rearrange("p (c d) -> p c d", c=NCH)[:, c, w0:w1],
                        start=(c == 0), stop=(c == NCH - 1)),
                        (s_pe, 1))
            if SUB < 9:
                continue
            qdve.w(s_pe, last_pe_tok)
            qdve.w(s_vo[vo], vob_free[vo])
            tokV = None
            for i in range(NVS):
                w0 = i * 512
                w1 = min(w0 + 512, KVDh)
                tokV = qdve.do(lambda e, i=i, vo=vo, w0=w0, w1=w1: e.tensor_copy(
                    vob[vo][:, w0:w1], pv[i][:, :]), (s_dve, 1))
                pv_free[i] = tokV
            if SUB >= 10:
                qsp.w(s_dve, tokV)
                vob_free[vo] = qsp.do(lambda e, tt=tt, tb=tb, vo=vo: e.dma_start(
                    s_vh[(tt * 2 + tb) * 128:(tt * 2 + tb + 1) * 128, :], vob[vo][:, :]), (s_vo[vo], 16))
        if last_pe_tok is not None:
            xt_free_tok[bi] = last_pe_tok

    RG = [[2 * i, 2 * i + 1] for i in range(max(1, n_cores // 2))]
    spill_toks = [(sx, sx.n) for sx in (s_ko[0], s_ko[1], s_vo[0], s_vo[1])]

    def mk_cc():
        ccs = []
        for ks in range(HKVh):
            ccs.append(lambda e, ks=ks: e.collective_compute(
                "AllGather", ALU.bypass, replica_groups=RG,
                ins=[s_kh[ks * 128:(ks + 1) * 128, :].opt()],
                outs=[s_k[ks].rearrange("g p t -> (g p) t").opt()]))
        RQ = KTQ * 128
        for q in range(VCH):
            ccs.append(lambda e, q=q: e.collective_compute(
                "AllGather", ALU.bypass, replica_groups=RG,
                ins=[s_vh[q * RQ:(q + 1) * RQ, :].opt()],
                outs=[s_v[q].rearrange("g k p d -> (g k p) d").opt()]))
        return ccs

    cc_list = mk_cc()
    cc_total = len(cc_list)

    def emit_cc_chunk():
        if not cc_list:
            return
        for sx, v in spill_toks:
            qgp.w(sx, v)
        qgp.do(cc_list.pop(0), (s_cc, 1))

    p1a.close()
    barrier()

    if _phases < 2:
        qsp.w(s_ko[0], s_ko[0].n); qsp.w(s_ko[1], s_ko[1].n)
        qsp.w(s_vo[0], s_vo[0].n); qsp.w(s_vo[1], s_vo[1].n)
        return _finish(nc, ALLQ, es, pctx)

    p1b = ExitStack()
    xtq_sb = sb("xtq_sb", [128, NCH * QT], BF16, p1b)
    wqs = [sb(f"wqs{i}", [128, NCH * 128], BF16, p1b) for i in range(2)]
    ckq_b = sb("ckq_b", [128, QT], BF16, p1b)
    skq_b = sb("skq_b", [128, QT], BF16, p1b)
    rotq = sb("rotq", [128, QT], F32, p1b)
    t1q = sb("t1q", [128, QT], F32, p1b)
    t2q = sb("t2q", [128, QT], F32, p1b)
    qob = [sb(f"qob{i}", [128, QT], BF16, p1b) for i in range(2)]
    pq2 = [ps(f"pq{i}", [128, min(2048, QT)], F32, p1b) for i in range(2)]
    NQB = QT // 512

    for c0 in range(0, NCH, csub):
        c1 = min(c0 + csub, NCH)
        qgp.do(lambda e, c0=c0, c1=c1: e.dma_start(
            xtq_sb[:, :].rearrange("p (c t) -> p c t", c=NCH)[:, c0:c1, :],
            d_xTq.rearrange("(c p) t -> p c t", p=128)[:, c0:c1, :]), (s_ld2, 16))
    qgp.do(lambda e: e.dma_start(ckq_b[:, :], d_ckq), (s_ld2, 16))
    qgp.do(lambda e: e.dma_start(skq_b[:, :], d_skq), (s_ld2, 16))
    tok_ld2 = s_ld2.n

    wq_fills = [0, 0]
    wq_free = [0, 0]
    pq_free_dve = [0, 0]
    pq_free_act = [0, 0]
    rotq_rd = 0
    t1q_rd = 0
    qob_free = [0, 0]

    for qs in range(H):
        bi = qs % 2
        pq = pq2[qs % 2]
        qgp.w(s_pe, wq_free[bi])
        for c0 in range(0, NCH, csub):
            c1 = min(c0 + csub, NCH)
            qgp.do(lambda e, c0=c0, c1=c1, bi=bi, qs=qs: e.dma_start(
                wqs[bi][:, :].rearrange("p (c d) -> p c d", c=NCH)[:, c0:c1, :],
                d_wq.rearrange("(c p) d -> p c d", p=128)[:, c0:c1, qs * 128:(qs + 1) * 128]),
                (s_wq[bi], 16))
        wq_fills[bi] += 1
        tok_w = 16 * (NCH // csub) * wq_fills[bi]
        if qs >= 2 and qs % 3 == 2:
            emit_cc_chunk()

        qpe.w(s_ld2, tok_ld2)
        qpe.w(s_wq[bi], tok_w)
        qpe.w(s_dve, pq_free_dve[qs % 2])
        qpe.w(s_act, pq_free_act[qs % 2])
        tokS = None
        for c in range(NCH):
            for nb in range(NQB):
                tokS = qpe.do(lambda e, c=c, nb=nb, bi=bi, pq=pq: e.matmul(
                    pq[:, nb * 512:(nb + 1) * 512],
                    wqs[bi][:, :].rearrange("p (c d) -> p c d", c=NCH)[:, c, :],
                    xtq_sb[:, :].rearrange("p (c t) -> p c t", c=NCH)[:, c, nb * 512:(nb + 1) * 512],
                    start=(c == 0), stop=(c == NCH - 1)),
                    (s_pe, 1))
        wq_free[bi] = tokS
        qact.w(s_pe, tokS)
        qact.w(s_dve, rotq_rd)
        qact.do(lambda e, pq=pq: e.copy(rotq[0:64, :], pq[64:128, :]), (s_act, 1))
        tokR = qact.do(lambda e, pq=pq: e.copy(rotq[64:128, :], pq[0:64, :]), (s_act, 1))
        pq_free_act[qs % 2] = tokR
        qdve.w(s_pe, tokS)
        qdve.w(s_act, tokR)
        qdve.w(s_ld2, tok_ld2)
        qdve.w(s_dve, t1q_rd)
        tok1 = qdve.do(lambda e, pq=pq: e.tensor_tensor(t1q[:, :], pq[:, :], ckq_b[:, :], ALU.mult), (s_dve, 1))
        pq_free_dve[qs % 2] = tok1
        qdve.w(s_act, tokR)
        tok2 = qdve.do(lambda e: e.tensor_tensor(t2q[:, :], rotq[:, :], skq_b[:, :], ALU.mult), (s_dve, 1))
        rotq_rd = tok2
        qo = qs % 2
        qdve.w(s_dve, tok2)
        qdve.w(s_qo[qo], qob_free[qo])
        tokQ = qdve.do(lambda e, qo=qo: e.tensor_tensor(qob[qo][:, :], t1q[:, :], t2q[:, :], ALU.add), (s_dve, 1))
        t1q_rd = tokQ
        kvh, hig = qs // G, qs % G
        qsp.w(s_dve, tokQ)
        qob_free[qo] = qsp.do(lambda e, kvh=kvh, hig=hig, qo=qo: e.dma_start(
            s_q[kvh, :, :].rearrange("p (pr g t) -> p pr g t", pr=NPAIR, g=G)[:, :, hig, :],
            qob[qo][:, :].rearrange("p (pr t) -> p pr t", pr=NPAIR)), (s_qo[qo], 16))

    while cc_list:
        emit_cc_chunk()

    p1b.close()
    barrier()

    if _phases < 3:
        qsp.w(s_qo[0], s_qo[0].n); qsp.w(s_qo[1], s_qo[1].n)
        return _finish(nc, ALLQ, es, pctx)

    p2 = ExitStack()
    kb = [sb(f"kb{i}", [128, T], BF16, p2) for i in range(2)]
    praw = sb("praw", [128, GW], BF16, p2)
    pd_sb = sb("pd_sb", [128, GW], F32, p2)
    pl_sb = sb("pl_sb", [128, GW], F32, p2)
    vb = [sb(f"vb{i}", [128, NKT * 128], BF16, p2) for i in range(2)]
    qb = [sb(f"qb{i}", [128, NPAIR * GW], BF16, p2) for i in range(2)]
    mask_b = sb("mask_b", [128, NPAIR * 4 * 256], BF16, p2)
    ptb = [sb(f"ptb{i}", [128, GW], BF16, p2) for i in range(3)]
    rd_sb = sb("rd_sb", [128, GW], F32, p2)
    atb = [sb(f"atb{i}", [128, GW], BF16, p2) for i in range(2)]
    pS = [ps(f"pS{i}", [128, GW], F32, p2) for i in range(2)]
    pO = ps("pO", [128, GW], F32, p2)
    pD = ps("pD", [128, GW], F32, p2)
    NSB = GW // 512

    msub = max(1, (NPAIR * 4) // 4)
    for m0 in range(0, NPAIR * 4, msub):
        m1 = min(m0 + msub, NPAIR * 4)
        qgp.do(lambda e, m0=m0, m1=m1: e.dma_start(
            mask_b[:, :].rearrange("p (m t) -> p m t", m=NPAIR * 4)[:, m0:m1, :],
            d_mask.rearrange("(m p) t -> p m t", p=128)[:, m0:m1, :]), (s_ld3, 16))
    tok_ld3 = s_ld3.n

    ln_tok = 0
    pO_free = 0
    pD_free = 0
    kv_fills = [0, 0]
    kv_free = [0, 0]
    atb_free = [(None, 0), (None, 0)]
    pOD_free = 0
    rd_rd = 0

    def kv_load(kvh):
        bi = kvh % 2
        qgp.w(s_cc, cc_total)
        qgp.w(s_pe, kv_free[bi])
        grp, loc = kvh // HKVh, kvh % HKVh
        qgp.do(lambda e, bi=bi, grp=grp, loc=loc: e.dma_start(
            kb[bi][:, :], s_k[loc, grp, :, :]), (s_kv[bi], 16))
        for q in range(VCH):
            qgp.do(lambda e, bi=bi, q=q, grp=grp, loc=loc: e.dma_start(
                vb[bi][:, :].rearrange("p (k d) -> p k d", k=NKT)[:, q * KTQ:(q + 1) * KTQ, :],
                s_v[q, grp].rearrange("k p d -> p k d")[:, :, loc * 128:(loc + 1) * 128]),
                (s_kv[bi], 16))
        qgp.do(lambda e, kvh=kvh, bi=bi: e.dma_start(qb[bi][:, :], s_q[kvh, :, :]), (s_kv[bi], 16))
        kv_fills[bi] += 1
        return 16 * (2 + VCH) * kv_fills[bi]

    gktc = [0]
    exp_slot = [0, 0]
    av_slot = [0, 0, 0]
    pending_tail = []
    last_pe = 0

    def emit_av(pair, ktp, bi, nkt, rdy_tok, po_of):
        nonlocal last_pe
        sem, tok = rdy_tok[ktp]
        qpe.w(sem, tok)
        po = po_of[ktp]
        t_last = None
        if ktp == 0:
            qpe.w(s_dve, pO_free)
        for sbk in range(NSB):
            qpe.do(lambda e, ktp=ktp, po=po, sbk=sbk, bi=bi, nkt=nkt: e.matmul(
                pO[:, sbk * 512:(sbk + 1) * 512],
                vb[bi][:, ktp * 128:(ktp + 1) * 128],
                ptb[po][:, sbk * 512:(sbk + 1) * 512],
                start=(ktp == 0), stop=(ktp == nkt - 1)),
                (s_pe, 1))
            if ktp == 0 and sbk == 0:
                qpe.w(s_dve, pD_free)
            t_last = qpe.do(lambda e, ktp=ktp, po=po, sbk=sbk, nkt=nkt: e.matmul(
                pD[:, sbk * 512:(sbk + 1) * 512],
                ones_b[:, :],
                ptb[po][:, sbk * 512:(sbk + 1) * 512],
                start=(ktp == 0), stop=(ktp == nkt - 1)),
                (s_pe, 1))
        av_slot[po] = t_last
        last_pe = t_last

    def emit_evac(kvh, pair):
        nonlocal rd_rd, ln_tok, pO_free, pD_free
        ao = pair % 2
        qdve.w(s_pe, last_pe)
        qdve.w(s_dve, rd_rd)
        qdve.w(s_act, ln_tok)
        tokCO = qdve.do(lambda e: e.tensor_copy(praw[:, :], pO[:, :]), (s_dve, 1))
        tokCP = qdve.do(lambda e: e.tensor_copy(pd_sb[:, :], pD[:, :]), (s_dve, 1))
        pO_free = tokCO
        pD_free = tokCP
        qact.w(s_dve, tokCP)
        t_ln = qact.do(lambda e: e.activation(pl_sb[:, :], pd_sb[:, :], AF.Ln), (s_act, 1))
        qact.w(s_act, t_ln)
        ln_tok = qact.do(lambda e: e.activation(
            rd_sb[:, :], pl_sb[:, :], AF.Exp, scale=kon[:, 4:5]), (s_act, 1))
        qdve.w(s_act, ln_tok)
        ao_sem, ao_tok = atb_free[ao]
        qdve.w(ao_sem, ao_tok) if ao_sem is not None else None
        tokAT = qdve.do(lambda e, ao=ao: e.tensor_tensor(
            atb[ao][:, :], praw[:, :], rd_sb[:, :], ALU.mult), (s_dve, 1))
        rd_rd = tokAT
        qsp.w(s_dve, tokAT)
        atb_free[ao] = (s_ab[ao], qsp.do(lambda e, kvh=kvh, pair=pair, ao=ao: e.dma_start(
            s_at[kvh, pair, :, :], atb[ao][:, :]), (s_ab[ao], 16)))

    wo_chunks = []
    wsub = max(1, NCH // 16)
    for c0 in range(0, NCH, wsub):
        c1 = min(c0 + wsub, NCH)
        wo_chunks.append(lambda e, c0=c0, c1=c1: e.dma_start(
            s_wosc[c0:c1].rearrange("c p d -> p c d"),
            d_wo.rearrange("(c p) d -> p c d", p=128)[:, c0:c1, :]))
    wo_total = 16 * len(wo_chunks)

    def emit_wo_chunks(n):
        while wo_chunks and n > 0:
            qgp.do(wo_chunks.pop(0), (s_wos, 16))
            n -= 1

    kv_tok = [0, 0]
    kv_tok[0] = kv_load(0)
    if HKV > 1:
        kv_tok[1] = kv_load(1)

    for kvh in range(HKV):
        bi = kvh % 2
        if kvh + 2 < HKV:
            pass
        qpe.w(s_kv[bi], kv_tok[bi])
        qpe.w(s_ld3, tok_ld3)
        qpe.w(s_dve, tok_kon)
        last_pe = None
        for pair in range(NPAIR):
            nkt = 4 * pair + 4
            exp_tok = {}
            rdy_tok = {}
            po_of = {}
            for kt in range(nkt):
                gkt = gktc[0]
                gktc[0] += 1
                sx = pS[gkt % 2]
                qpe.w(s_act, exp_slot[gkt % 2])
                tokS = None
                for sbk in range(NSB):
                    tokS = qpe.do(lambda e, sx=sx, kt=kt, sbk=sbk, pair=pair, bi=bi: e.matmul(
                        sx[:, sbk * 512:(sbk + 1) * 512],
                        kb[bi][:, kt * 128:(kt + 1) * 128],
                        qb[bi][:, pair * GW + sbk * 512: pair * GW + (sbk + 1) * 512],
                        start=True, stop=True),
                        (s_pe, 1))
                if kt == 1 and pending_tail:
                    pending_tail.pop(0)()
                po = gkt % 3
                po_of[kt] = po
                qact.w(s_pe, tokS)
                qact.w(s_dve, tok_kon)
                qact.w(s_pe, av_slot[po])
                tE = qact.do(lambda e, sx=sx, po=po: e.activation(
                    ptb[po][:, :], sx[:, :], AF.Exp, bias=kon[:, 0:1]), (s_act, 1))
                exp_tok[kt] = tE
                exp_slot[gkt % 2] = tE
                rdy_tok[kt] = (s_act, tE)
                if kt >= 4 * pair:
                    off = kt - 4 * pair
                    qdve.w(s_act, tE)
                    qdve.w(s_ld3, tok_ld3)
                    mcol = (pair * 4 + off) * 256
                    def mfn(e, po=po, mcol=mcol):
                        a3 = ptb[po][:, :].rearrange("p (g t) -> p g t", g=G)
                        m3 = mask_b[:, mcol:mcol + 256].rearrange("p (g t) -> p g t", g=1)
                        a3b, m3b = broadcast_tensor_aps(a3, m3)
                        return e.tensor_tensor(a3, a3b, m3b, ALU.mult)
                    tM = qdve.do(mfn, (s_dve, 1))
                    rdy_tok[kt] = (s_dve, tM)
                ktp = kt - 2
                if ktp >= 0:
                    emit_av(pair, ktp, bi, nkt, rdy_tok, po_of)
            def mk_tail(pair=pair, bi=bi, nkt=nkt, rdy_tok=rdy_tok, po_of=po_of, kvh=kvh):
                def tail():
                    emit_av(pair, nkt - 2, bi, nkt, rdy_tok, po_of)
                    emit_av(pair, nkt - 1, bi, nkt, rdy_tok, po_of)
                    emit_evac(kvh, pair)
                    if pair == NPAIR - 1:
                        kv_free[bi] = last_pe
                        if kvh + 2 < HKV:
                            kv_tok[bi] = kv_load(kvh + 2)
                return tail
            pending_tail.append(mk_tail())
        emit_wo_chunks(2)

    while wo_chunks:
        emit_wo_chunks(1)
    while pending_tail:
        pending_tail.pop(0)()

    p2.close()
    barrier()

    if _phases < 4:
        qsp.w(s_ab[0], s_ab[0].n); qsp.w(s_ab[1], s_ab[1].n)
        return _finish(nc, ALLQ, es, pctx)

    RW = min(1024, C)
    NRND = C // RW
    NSQ = RW // 512

    p34 = ExitStack()
    atq = sb("atq", [128, 2 * HKV * GW], BF16, p34)
    wob = [sb(f"wob{i}", [128, NCH * CBW], BF16, p34) for i in range(2)]
    xqp = [sb(f"xqp{i}", [128, CBW], F32, p34) for i in range(2)]
    pw = [ps(f"pw{i}", [128, CBW], F32, p34) for i in range(2)]

    out_dma_count = 0
    atq_fills = 0
    atq_free = 0
    wo_fills = [0, 0]
    wo_free = [0, 0]
    xq_fills = [0, 0]
    xq_free = [0, 0]
    pw_free = [0, 0]
    trash_war = 0
    trans_last = 0
    pd2_rd = 0
    st1_war = 0
    ob_cnt = [0, 0, 0, 0]
    ytok = [(0, 0)] * 4

    def emit_quarter(qx):
        nonlocal atq_fills, atq_free, out_dma_count, trash_war, trans_last, pd2_rd, st1_war
        q4 = ExitStack()
        y_sb = sb(f"y_{qx}", [128, 4 * C], F32, q4)
        hbf = sb(f"hbf_{qx}", [128, 4 * C], BF16, q4)
        sums = sb(f"sums_{qx}", [128, 4 * NCB], F32, q4)
        sqs = sb(f"sqs_{qx}", [128, 4 * NCB], F32, q4)
        trash = sb(f"trash_{qx}", [128, 512], F32, q4)
        mst = sb(f"mst_{qx}", [128, 4], F32, q4)
        rst = sb(f"rst_{qx}", [128, 4], F32, q4)
        st1 = sb(f"st1_{qx}", [128, 6], F32, q4)
        gmid = sb(f"gmid_{qx}", [ER, 512], BF16, q4)

        qdve.w(s_dve, s_dve.n)
        qact.w(s_act, s_act.n)

        qsp.w(s_pe, atq_free)
        for kvh in range(HKV):
            for pri in range(2):
                qsp.do(lambda e, kvh=kvh, pri=pri: e.dma_start(
                    atq[:, (kvh * 2 + pri) * GW:(kvh * 2 + pri + 1) * GW],
                    s_at[kvh, 2 * qx + pri, :, :]), (s_aq, 16))
        atq_fills += 1
        tok_atq = 16 * 2 * HKV * atq_fills

        stt_tok = [[0] * NCB for _ in range(4)]
        sq_tok = [[0] * NCB for _ in range(4)]
        tokW = None
        for cb in range(NCB):
            wi = cb % 2
            qsp.w(s_wos, wo_total)
            qsp.w(s_pe, wo_free[wi])
            for c0 in range(0, NCH, csub):
                c1 = min(c0 + csub, NCH)
                qsp.do(lambda e, c0=c0, c1=c1, wi=wi, cb=cb: e.dma_start(
                    wob[wi][:, :].rearrange("p (c d) -> p c d", c=NCH)[:, c0:c1, :],
                    s_wosc.rearrange("c p d -> p c d")[:, c0:c1, cb * CBW:(cb + 1) * CBW]),
                    (s_wo[wi], 16))
            wo_fills[wi] += 1
            tok_wo = 16 * (NCH // csub) * wo_fills[wi]
            for tb in range(4):
                pri, off = tb // 2, tb % 2
                xi = (cb * 4 + tb) % 2
                qsp.w(s_dve, xq_free[xi])
                qsp.do(lambda e, tb=tb, cb=cb, xi=xi: e.dma_start(
                    xqp[xi][:, :], d_xq[qx * 512 + tb * 128: qx * 512 + (tb + 1) * 128,
                                        cb * CBW:(cb + 1) * CBW]), (s_xq[xi], 16))
                xq_fills[xi] += 1
                tok_xq = 16 * xq_fills[xi]

                pwi = tb % 2
                qpe.w(s_aq, tok_atq)
                qpe.w(s_wo[wi], tok_wo)
                qpe.w(s_dve, pw_free[pwi])
                for h in range(H):
                    kvh, hig = h // G, h % G
                    scol = (kvh * 2 + pri) * GW + hig * 256 + off * 128
                    tokW = qpe.do(lambda e, h=h, scol=scol, wi=wi, pwi=pwi: e.matmul(
                        pw[pwi][:, :],
                        atq[:, scol:scol + 128],
                        wob[wi][:, :].rearrange("p (c d) -> p c d", c=NCH)[:, h, :],
                        start=(h == 0), stop=(h == H - 1)),
                        (s_pe, 1))
                par, cnt = ytok[tb]
                qdve.w(s_ob[par], cnt)
                qdve.w(s_pe, tokW)
                qdve.w(s_xq[xi], tok_xq)
                tokY = qdve.do(lambda e, tb=tb, cb=cb, pwi=pwi, xi=xi: e.scalar_tensor_tensor(
                    y_sb[:, tb * C + cb * CBW: tb * C + (cb + 1) * CBW],
                    pw[pwi][:, :], 0.0, xqp[xi][:, :], ALU.add, ALU.add,
                    accum_out=sums[:, tb * NCB + cb: tb * NCB + cb + 1]), (s_dve, 1))
                pw_free[pwi] = tokY
                xq_free[xi] = tokY
                stt_tok[tb][cb] = tokY
                qact.w(s_dve, tokY)
                qact.w(s_act, trash_war)
                tokQ2 = qact.do(lambda e, tb=tb, cb=cb: e.activation(
                    trash[:, 0:CBW], y_sb[:, tb * C + cb * CBW: tb * C + (cb + 1) * CBW],
                    AF.Square, accum_out=sqs[:, tb * NCB + cb: tb * NCB + cb + 1]), (s_act, 1))
                trash_war = tokQ2
                sq_tok[tb][cb] = tokQ2
            wo_free[wi] = tokW
        wo_last_pe = s_pe.n

        hbf_tok = [0] * 4
        norm_tok = [0] * 4
        st1_last = 0
        for tb in range(4):
            qdve.w(s_dve, stt_tok[tb][NCB - 1])
            qdve.w(s_act, sq_tok[tb][NCB - 1])
            qdve.w(s_dve, st1_last)
            qdve.w(s_act, st1_war)
            t1 = qdve.do(lambda e, tb=tb: e.reduce_sum(
                st1[:, 0:1], sums[:, tb * NCB:(tb + 1) * NCB], axis=AX.X), (s_dve, 1))
            t2 = qdve.do(lambda e, tb=tb: e.reduce_sum(
                st1[:, 1:2], sqs[:, tb * NCB:(tb + 1) * NCB], axis=AX.X), (s_dve, 1))
            qdve.w(s_dve, t2)
            t3 = qdve.do(lambda e, tb=tb: e.tensor_scalar_mul(
                mst[:, tb:tb + 1], st1[:, 0:1], 1.0 / C), (s_dve, 1))
            qdve.w(s_dve, t3)
            t4 = qdve.do(lambda e, tb=tb: e.tensor_tensor(
                st1[:, 2:3], mst[:, tb:tb + 1], mst[:, tb:tb + 1], ALU.mult), (s_dve, 1))
            qdve.w(s_dve, t4)
            t5 = qdve.do(lambda e: e.scalar_tensor_tensor(
                st1[:, 3:4], st1[:, 1:2], 1.0 / C, st1[:, 2:3], ALU.mult, ALU.subtract), (s_dve, 1))
            qdve.w(s_dve, t5)
            t6 = qdve.do(lambda e: e.tensor_scalar_add(st1[:, 4:5], st1[:, 3:4], EPS), (s_dve, 1))
            qact.w(s_dve, t6)
            t7 = qact.do(lambda e: e.activation(st1[:, 5:6], st1[:, 4:5], AF.Ln), (s_act, 1))
            qact.w(s_act, t7)
            t8 = qact.do(lambda e, tb=tb: e.activation(
                rst[:, tb:tb + 1], st1[:, 5:6], AF.Exp, scale=kon[:, 1:2]), (s_act, 1))
            qdve.w(s_act, t8)
            t9 = qdve.do(lambda e, tb=tb: e.tensor_scalar(
                y_sb[:, tb * C:(tb + 1) * C], y_sb[:, tb * C:(tb + 1) * C],
                mst[:, tb:tb + 1], rst[:, tb:tb + 1], ALU.subtract, ALU.mult), (s_dve, 1))
            norm_tok[tb] = t9
            st1_last = t6
            st1_war = t8
            qact.w(s_dve, t9)
            qact.w(s_pe, trans_last)
            hbf_tok[tb] = qact.do(lambda e, tb=tb: e.copy(
                hbf[:, tb * C:(tb + 1) * C], y_sb[:, tb * C:(tb + 1) * C]), (s_act, 1))

        RW2 = 512
        NR2 = C // RW2
        p4a = ExitStack()
        hT2 = [sb(f"hT_{qx}{i}", [128, NCH * 128], BF16, p4a) for i in range(2)]
        lg = sb(f"lg_{qx}", [32, 512], F32, p4a)
        tA = sb(f"tA_{qx}", [32, 512], F32, p4a)
        tB = sb(f"tB_{qx}", [32, 512], F32, p4a)
        tC = sb(f"tC_{qx}", [32, 512], F32, p4a)
        tD = sb(f"tD_{qx}", [32, 512], F32, p4a)
        gb = sb(f"gb_{qx}", [32, 512], BF16, p4a)
        g64 = sb(f"g64_{qx}", [ER, 512], BF16, p4a)
        sums2 = sb(f"sums2_{qx}", [128, 4 * NR2], F32, p4a)
        sqs2 = sb(f"sqs2_{qx}", [128, 4 * NR2], F32, p4a)
        ptr = [ps(f"ptr_{qx}{i}", [128, 512], BF16, p4a) for i in range(2)]
        pm68 = ps(f"pm68_{qx}", [EW, 512], F32, p4a)
        pg = ps(f"pg_{qx}", [ER, 512], F32, p4a)
        pd2 = [ps(f"pd2_{qx}{i}", [128, RW2], F32, p4a) for i in range(2)]

        perm1 = [(i + 1) % NE if i < NE else i for i in range(32)]
        perm2 = [(i + 2) % NE if i < NE else i for i in range(32)]
        ptr_free = [0, 0]
        hT_free = [0, 0]
        mid_tok = [0] * 4
        gate_tok = [0] * 4
        gmid_tok = [0] * 4
        pd2_free = [0, 0]
        lgz = qdve.do(lambda e: e.memset(lg[:, :], 0.0), (s_dve, 1))

        def emit_A(tb):
            nonlocal trans_last
            hi2 = tb % 2
            qpe.w(s_act, hbf_tok[tb])
            qpe.w(s_ld0, tok_ld0)
            qpe.w(s_dve, pd2_rd)
            ev_tok = 0
            NGRP = (NCH + 3) // 4
            for grp_i in range(NGRP):
                pi = (tb * NGRP + grp_i) % 2
                qpe.w(s_dve, ptr_free[pi])
                tokT = None
                c0g = grp_i * 4
                c1g = min(c0g + 4, NCH)
                for cc in range(c0g, c1g):
                    tokT = qpe.do(lambda e, tb=tb, cc=cc, pi=pi, c0g=c0g: e.transpose(
                        ptr[pi][:, (cc - c0g) * 128:(cc - c0g + 1) * 128],
                        hbf[:, tb * C + cc * 128: tb * C + (cc + 1) * 128],
                        ident_b[:, :]), (s_pe, 1))
                trans_last = tokT
                qdve.w(s_pe, tokT)
                qdve.w(s_pe, hT_free[hi2])
                ev_tok = qdve.do(lambda e, c0g=c0g, c1g=c1g, pi=pi, hi2=hi2: e.tensor_copy(
                    hT2[hi2][:, c0g * 128:c1g * 128], ptr[pi][:, 0:(c1g - c0g) * 128]), (s_dve, 1))
                ptr_free[pi] = ev_tok
            qpe.w(s_dve, ev_tok)
            tokM = None
            for cc in range(NCH):
                tokM = qpe.do(lambda e, cc=cc, tb=tb, hi2=hi2: e.matmul(
                    pm68[:, tb * 128:(tb + 1) * 128],
                    aw_b[:, :].rearrange("p (c e) -> p c e", c=NCH)[:, cc, :],
                    hT2[hi2][:, cc * 128:(cc + 1) * 128],
                    start=(cc == 0), stop=(cc == NCH - 1)), (s_pe, 1))
            hT_free[hi2] = tokM
            mid_tok[tb] = tokM

        def emit_gate(tb):
            s0, s1 = tb * 128, (tb + 1) * 128
            L = lambda t: t[:, s0:s1]
            qdve.w(s_dve, lgz)
            qdve.w(s_pe, mid_tok[tb])
            tg = qdve.do(lambda e: e.tensor_copy(lg[0:NE, s0:s1], pm68[ER:EW, s0:s1]), (s_dve, 1))
            qdve.w(s_dve, tg)
            tg = qdve.do(lambda e: e.stream_shuffle(L(tA), L(lg), perm1), (s_dve, 1))
            qdve.w(s_dve, tg)
            tg = qdve.do(lambda e: e.tensor_max(L(tB), L(lg), L(tA)), (s_dve, 1))
            qdve.w(s_dve, tg)
            tg = qdve.do(lambda e: e.stream_shuffle(L(tA), L(tB), perm2), (s_dve, 1))
            qdve.w(s_dve, tg)
            tg = qdve.do(lambda e: e.tensor_max(L(tB), L(tB), L(tA)), (s_dve, 1))
            qdve.w(s_dve, tg)
            tg = qdve.do(lambda e: e.tensor_tensor(L(tC), L(lg), L(tB), ALU.is_equal), (s_dve, 1))
            qdve.w(s_dve, tg)
            tg = qdve.do(lambda e: e.scalar_tensor_tensor(
                L(tC), L(tC), -1e30, L(lg), ALU.mult, ALU.add), (s_dve, 1))
            qdve.w(s_dve, tg)
            tg = qdve.do(lambda e: e.stream_shuffle(L(tA), L(tC), perm1), (s_dve, 1))
            qdve.w(s_dve, tg)
            tg = qdve.do(lambda e: e.tensor_max(L(tC), L(tC), L(tA)), (s_dve, 1))
            qdve.w(s_dve, tg)
            tg = qdve.do(lambda e: e.stream_shuffle(L(tA), L(tC), perm2), (s_dve, 1))
            qdve.w(s_dve, tg)
            tg = qdve.do(lambda e: e.tensor_max(L(tC), L(tC), L(tA)), (s_dve, 1))
            qdve.w(s_dve, tg)
            tg = qdve.do(lambda e: e.tensor_tensor(L(tD), L(lg), L(tC), ALU.is_ge), (s_dve, 1))
            qdve.w(s_dve, tg)
            tg = qdve.do(lambda e: e.tensor_tensor(L(tA), L(lg), L(tB), ALU.subtract), (s_dve, 1))
            qact.w(s_dve, tg)
            ta = qact.do(lambda e: e.activation(
                L(tB), L(tA), AF.Exp, bias=kon[0:32, 2:3], scale=kon[0:32, 3:4]), (s_act, 1))
            qdve.w(s_act, ta)
            tg = qdve.do(lambda e: e.tensor_tensor(L(tB), L(tB), L(tD), ALU.mult), (s_dve, 1))
            qdve.w(s_dve, tg)
            tg = qdve.do(lambda e: e.stream_shuffle(L(tA), L(tB), perm1), (s_dve, 1))
            qdve.w(s_dve, tg)
            tg = qdve.do(lambda e: e.tensor_add(L(tC), L(tB), L(tA)), (s_dve, 1))
            qdve.w(s_dve, tg)
            tg = qdve.do(lambda e: e.stream_shuffle(L(tA), L(tC), perm2), (s_dve, 1))
            qdve.w(s_dve, tg)
            tg = qdve.do(lambda e: e.tensor_add(L(tC), L(tC), L(tA)), (s_dve, 1))
            qdve.w(s_dve, tg)
            tg = qdve.do(lambda e: e.reciprocal(L(tA), L(tC)), (s_dve, 1))
            qdve.w(s_dve, tg)
            tg = qdve.do(lambda e, sc=g.SCALE: e.scalar_tensor_tensor(
                L(tD), L(tB), sc, L(tA), ALU.mult, ALU.mult), (s_dve, 1))
            qdve.w(s_dve, tg)
            gate_tok[tb] = qdve.do(lambda e: e.tensor_copy(L(gb), L(tD)), (s_dve, 1))

        def emit_B1(tb):
            s0, s1 = tb * 128, (tb + 1) * 128
            qpe.w(s_dve, gate_tok[tb])
            qpe.w(s_ld0, tok_ld0)
            tokG = qpe.do(lambda e, s0=s0, s1=s1: e.matmul(
                pg[:, s0:s1], em_b[:, :], gb[0:NE, s0:s1], start=True, stop=True), (s_pe, 1))
            qdve.w(s_pe, tokG)
            tg = qdve.do(lambda e, s0=s0, s1=s1: e.tensor_copy(g64[:, s0:s1], pg[:, s0:s1]), (s_dve, 1))
            qdve.w(s_dve, tg)
            gmid_tok[tb] = qdve.do(lambda e, s0=s0, s1=s1: e.tensor_tensor(
                gmid[:, s0:s1], pm68[0:ER, s0:s1], g64[:, s0:s1], ALU.mult), (s_dve, 1))

        def emit_B2(tb):
            nonlocal trash_war, pd2_rd, out_dma_count, st1_last, st1_war
            y2_last = 0
            sq2_last = 0
            for rnd in range(NR2):
                di = (tb * NR2 + rnd) % 2
                qpe.w(s_dve, gmid_tok[tb])
                qpe.w(s_ld0, tok_ld0)
                qpe.w(s_dve, pd2_free[di])
                tokB = qpe.do(lambda e, tb=tb, rnd=rnd, di=di: e.matmul(
                    pd2[di][:, :],
                    gmid[:, tb * 128:(tb + 1) * 128],
                    bS_b[:, rnd * RW2:(rnd + 1) * RW2],
                    start=True, stop=True), (s_pe, 1))
                qdve.w(s_pe, tokB)
                qdve.w(s_dve, norm_tok[tb])
                qdve.w(s_act, hbf_tok[tb])
                tokY2 = qdve.do(lambda e, tb=tb, rnd=rnd, di=di: e.scalar_tensor_tensor(
                    y_sb[:, tb * C + rnd * RW2: tb * C + (rnd + 1) * RW2],
                    pd2[di][:, :], 0.0, y_sb[:, tb * C + rnd * RW2: tb * C + (rnd + 1) * RW2],
                    ALU.add, ALU.add,
                    accum_out=sums2[:, tb * NR2 + rnd: tb * NR2 + rnd + 1]), (s_dve, 1))
                pd2_free[di] = tokY2
                pd2_rd = tokY2
                y2_last = tokY2
                qact.w(s_dve, tokY2)
                qact.w(s_act, trash_war)
                tokS2 = qact.do(lambda e, tb=tb, rnd=rnd: e.activation(
                    trash[:, 0:RW2],
                    y_sb[:, tb * C + rnd * RW2: tb * C + (rnd + 1) * RW2],
                    AF.Square, accum_out=sqs2[:, tb * NR2 + rnd: tb * NR2 + rnd + 1]), (s_act, 1))
                trash_war = tokS2
                sq2_last = tokS2
            qdve.w(s_dve, y2_last)
            qdve.w(s_act, sq2_last)
            qdve.w(s_dve, st1_last)
            qdve.w(s_act, st1_war)
            t1 = qdve.do(lambda e, tb=tb: e.reduce_sum(
                st1[:, 0:1], sums2[:, tb * NR2:(tb + 1) * NR2], axis=AX.X), (s_dve, 1))
            t2 = qdve.do(lambda e, tb=tb: e.reduce_sum(
                st1[:, 1:2], sqs2[:, tb * NR2:(tb + 1) * NR2], axis=AX.X), (s_dve, 1))
            qdve.w(s_dve, t2)
            t3 = qdve.do(lambda e, tb=tb: e.tensor_scalar_mul(
                mst[:, tb:tb + 1], st1[:, 0:1], 1.0 / C), (s_dve, 1))
            qdve.w(s_dve, t3)
            t4 = qdve.do(lambda e, tb=tb: e.tensor_tensor(
                st1[:, 2:3], mst[:, tb:tb + 1], mst[:, tb:tb + 1], ALU.mult), (s_dve, 1))
            qdve.w(s_dve, t4)
            t5 = qdve.do(lambda e: e.scalar_tensor_tensor(
                st1[:, 3:4], st1[:, 1:2], 1.0 / C, st1[:, 2:3], ALU.mult, ALU.subtract), (s_dve, 1))
            qdve.w(s_dve, t5)
            t6 = qdve.do(lambda e: e.tensor_scalar_add(st1[:, 4:5], st1[:, 3:4], EPS), (s_dve, 1))
            qact.w(s_dve, t6)
            t7 = qact.do(lambda e: e.activation(st1[:, 5:6], st1[:, 4:5], AF.Ln), (s_act, 1))
            qact.w(s_act, t7)
            t8 = qact.do(lambda e, tb=tb: e.activation(
                rst[:, tb:tb + 1], st1[:, 5:6], AF.Exp, scale=kon[:, 1:2]), (s_act, 1))
            st1_last = t6
            st1_war = t8
            qdve.w(s_act, t8)
            t9o = qdve.do(lambda e, tb=tb: e.tensor_scalar(
                y_sb[:, tb * C:(tb + 1) * C], y_sb[:, tb * C:(tb + 1) * C],
                mst[:, tb:tb + 1], rst[:, tb:tb + 1], ALU.subtract, ALU.mult), (s_dve, 1))
            qsp.w(s_dve, t9o)
            ob_cnt[tb] += 16
            qsp.do(lambda e, tb=tb: e.dma_start(
                d_out[qx * 512 + tb * 128: qx * 512 + (tb + 1) * 128, :],
                y_sb[:, tb * C:(tb + 1) * C]), (s_ob[tb], 16))
            ytok[tb] = (tb, ob_cnt[tb])
            out_dma_count += 1

        for tb in range(4):
            emit_A(tb)
            emit_gate(tb)
            if tb >= 1:
                emit_B1(tb - 1)
            if tb >= 2:
                emit_B2(tb - 2)
        emit_B1(3)
        emit_B2(2)
        emit_B2(3)
        p4a.close()
        atq_free = wo_last_pe
        q4.close()

    for qx in range(NQ4):
        emit_quarter(qx)


    p34.close()

    for _s in s_ob:
        qsp.w(_s, _s.n)
    return _finish(nc, ALLQ, es, pctx)


def _finish(nc, ALLQ, es, pctx):
    qpe, qact, qdve, qgp, qsp = ALLQ
    with nc.Block() as block:
        @block.tensor
        def _(e):
            _replay(qpe, e)

        @block.scalar
        def _(e):
            _replay(qact, e)

        @block.vector
        def _(e):
            _replay(qdve, e)

        @block.gpsimd
        def _(e):
            _replay(qgp, e)

        @block.sync
        def _(e):
            _replay(qsp, e)

    es.close()
    pctx.close()
    return nc



def stage_core(g, core, X, Wq, Wk, Wv, Wo, A, B, Wr):
    T, C = g.T, g.C
    b, half = core // 2, core % 2
    js = jlist(g, half)
    qcols = np.concatenate([np.arange(j * 128, (j + 1) * 128) for j in js])
    Xb = np.asarray(X[b], dtype=np.float32)
    xT = np.ascontiguousarray(Xb.T)
    xTq = np.ascontiguousarray(Xb.T[:, qcols])
    xq = np.ascontiguousarray(Xb[qcols, :])

    inv = 1.0 / THETA ** (np.arange(64, dtype=np.float64) / 64.0)
    posk = np.arange(T, dtype=np.float64)
    angk = np.outer(posk, inv)
    ct = np.cos(angk).T.astype(np.float32)
    st = np.sin(angk).T.astype(np.float32)
    ckk = np.ascontiguousarray(np.vstack([ct, ct]))
    skk = np.ascontiguousarray(np.vstack([-st, st]))
    angq = angk[qcols]
    cq = np.cos(angq).T.astype(np.float32)
    sq = np.sin(angq).T.astype(np.float32)
    sc = 1.0 / math.sqrt(g.DH)
    ckq = np.ascontiguousarray(np.vstack([cq, cq]) * sc)
    skq = np.ascontiguousarray(np.vstack([-sq, sq]) * sc)

    m = np.zeros((g.NPAIR, 4, 128, 256), dtype=np.float32)
    pk = np.arange(128)
    for pr in range(g.NPAIR):
        for off in range(4):
            kt = 4 * pr + off
            kidx = kt * 128 + pk
            for sl2 in range(2):
                j = js[2 * pr + sl2]
                tok = j * 128 + np.arange(128)
                m[pr, off, :, sl2 * 128:(sl2 + 1) * 128] = (kidx[:, None] <= tok[None, :])
    maskq = np.ascontiguousarray(m.reshape(g.NPAIR * 4 * 128, 256))

    wkh = np.ascontiguousarray(Wk[:, half * g.KVDh:(half + 1) * g.KVDh])
    wvh = np.ascontiguousarray(Wv[:, half * g.KVDh:(half + 1) * g.KVDh])
    return {
        "xT": xT, "xTq": xTq, "xq": xq,
        "wq": Wq, "wk": wkh, "wv": wvh, "wo": Wo,
        "awT": None, "bS": None, "emat": None,
        "ckk": ckk, "skk": skk, "ckq": ckq, "skq": skq,
        "maskq": maskq, "ident": None,
    }


def kernel(L0_attn_X, L0_attn_Wq, L0_attn_Wk, L0_attn_Wv, L0_attn_Wo,
           L0_ffn_A, L0_ffn_B, L0_ffn_Wr, _geo=None, _ncores=None):
    g = _geo or Geo()
    X = np.asarray(L0_attn_X, dtype=np.float32)
    Wq = np.ascontiguousarray(np.asarray(L0_attn_Wq, dtype=np.float32))
    Wk = np.ascontiguousarray(np.asarray(L0_attn_Wk, dtype=np.float32))
    Wv = np.ascontiguousarray(np.asarray(L0_attn_Wv, dtype=np.float32))
    Wo = np.ascontiguousarray(np.asarray(L0_attn_Wo, dtype=np.float32))
    A = np.asarray(L0_ffn_A, dtype=np.float32)
    B = np.asarray(L0_ffn_B, dtype=np.float32)
    Wr = np.asarray(L0_ffn_Wr, dtype=np.float32)

    n_cores = _ncores if _ncores is not None else 2 * X.shape[0]
    awT = np.ascontiguousarray(
        np.concatenate([A.reshape(g.ER, g.C), Wr], axis=0).T)
    bS = np.ascontiguousarray(B.transpose(0, 2, 1).reshape(g.ER, g.C))
    emat = np.ascontiguousarray(np.kron(np.eye(g.NE, dtype=np.float32),
                                        np.ones((1, g.R), dtype=np.float32)))
    ident = np.eye(128, dtype=np.float32)

    in_maps = []
    for core in range(n_cores):
        d = stage_core(g, core, X, Wq, Wk, Wv, Wo, A, B, Wr)
        d["awT"], d["bS"], d["emat"], d["ident"] = awT, bS, emat, ident
        in_maps.append(d)

    nc = build_nc(g, n_cores=n_cores)
    res = run_bass_kernel_spmd(nc, in_maps, core_ids=list(range(n_cores)))

    out = np.empty((X.shape[0], g.T, g.C), dtype=np.float32)
    for core in range(n_cores):
        b, half = core // 2, core % 2
        js = jlist(g, half)
        qcols = np.concatenate([np.arange(j * 128, (j + 1) * 128) for j in js])
        out[b, qcols, :] = res.results[core]["out"]
    return out


# version 16
# speedup vs baseline: 1.1834x; 1.0297x over previous
import math
import numpy as np
from contextlib import ExitStack

import concourse.bass as bass
import concourse.mybir as mybir
from concourse.bass import broadcast_tensor_aps
from concourse.bass_utils import run_bass_kernel_spmd

F32 = mybir.dt.float32
BF16 = mybir.dt.bfloat16
AF = mybir.ActivationFunctionType
ALU = mybir.AluOpType
AX = mybir.AxisListType

THETA = 10000.0
EPS = 1e-5
CSHIFT = 12.0
SUB = 99
FENCE_Q = True
DBG_SCRATCH = False
NT1CAP = 999
HKVCAP = 999
SUB3MODE = 0
def rotk_src(t):
    return t[:, :]


class Geo:
    def __init__(self, T=4096, C=4096, H=32, HKV=8, NE=4, R=16):
        self.T, self.C, self.H, self.HKV, self.NE, self.R = T, C, H, HKV, NE, R
        self.DH = 128
        self.G = H // HKV
        self.NCH = C // 128
        self.KVD = HKV * 128
        self.HKVh = HKV // 2
        self.KVDh = self.KVD // 2
        self.NS = T // 256
        self.NPAIR = self.NS // 2
        self.NQ4 = self.NS // 4
        self.QT = T // 2
        self.NKT = T // 128
        self.ER = NE * R
        self.GW = self.G * 256
        self.SCALE = R / math.sqrt(R)
        assert self.NS % 4 == 0 and self.C % 512 == 0 and self.QT % 512 == 0


def jlist(g, half):
    if half == 0:
        return [2 * s + (s & 1) for s in range(g.NS)]
    return [2 * s + 1 - (s & 1) for s in range(g.NS)]


class SemT:
    __slots__ = ("h", "n", "name")

    def __init__(self, h, name):
        self.h, self.n, self.name = h, 0, name


class Q:
    __slots__ = ("items", "hi")

    def __init__(self):
        self.items = []
        self.hi = {}

    def w(self, sem, val):
        if val is None or val <= 0:
            return
        if self.hi.get(id(sem), -1) >= val:
            return
        self.hi[id(sem)] = val
        self.items.append((0, sem, val))

    def do(self, fn, *incs):
        toks = []
        for s, k in incs:
            s.n += k
            toks.append(s.n)
        self.items.append((1, fn, [(s, k) for s, k in incs]))
        if len(toks) == 1:
            return toks[0]
        return toks


def _replay(q, eng):
    for it in q.items:
        if it[0] == 0:
            eng.wait_ge(it[1].h, it[2])
        else:
            inst = it[1](eng)
            for s, k in it[2]:
                inst = inst.then_inc(s.h, k)


def build_nc(g, _phases=99, n_cores=8):
    nc = bass.Bass("TRN2", target_bir_lowering=False)
    T, C, H, HKV, NE, R = g.T, g.C, g.H, g.HKV, g.NE, g.R
    HKVh, KVDh = g.HKVh, g.KVDh
    NCH, KVD, NS, NPAIR, NQ4, QT, NKT, ER, G, GW = (
        g.NCH, g.KVD, g.NS, g.NPAIR, g.NQ4, g.QT, g.NKT, g.ER, g.G, g.GW)
    CBW = 256
    NCB = C // CBW

    d_xT = nc.dram_tensor("xT", [C, T], F32, kind="ExternalInput").ap()
    d_xTq = nc.dram_tensor("xTq", [C, QT], F32, kind="ExternalInput").ap()
    d_xq = nc.dram_tensor("xq", [QT, C], F32, kind="ExternalInput").ap()
    d_wq = nc.dram_tensor("wq", [C, C], F32, kind="ExternalInput").ap()
    d_wk = nc.dram_tensor("wk", [C, KVDh], F32, kind="ExternalInput").ap()
    d_wv = nc.dram_tensor("wv", [C, KVDh], F32, kind="ExternalInput").ap()
    d_wo = nc.dram_tensor("wo", [C, C], F32, kind="ExternalInput").ap()
    d_aw = nc.dram_tensor("awT", [C, ER + NE], F32, kind="ExternalInput").ap()
    d_bS = nc.dram_tensor("bS", [ER, C], F32, kind="ExternalInput").ap()
    d_em = nc.dram_tensor("emat", [NE, ER], F32, kind="ExternalInput").ap()
    d_ckk = nc.dram_tensor("ckk", [128, T], F32, kind="ExternalInput").ap()
    d_skk = nc.dram_tensor("skk", [128, T], F32, kind="ExternalInput").ap()
    d_ckq = nc.dram_tensor("ckq", [128, QT], F32, kind="ExternalInput").ap()
    d_skq = nc.dram_tensor("skq", [128, QT], F32, kind="ExternalInput").ap()
    d_mask = nc.dram_tensor("maskq", [NPAIR * 4 * 128, 256], F32, kind="ExternalInput").ap()
    d_id = nc.dram_tensor("ident", [128, 128], F32, kind="ExternalInput").ap()
    d_out = nc.dram_tensor("out", [QT, C], F32, kind="ExternalOutput").ap()
    d_dby = nc.dram_tensor("dby", [QT, C], F32, kind="ExternalOutput").ap() if DBG_SCRATCH else None
    s_dby = None

    _sk = "ExternalOutput" if DBG_SCRATCH else "Internal"
    VCH = 4 if NKT % 4 == 0 else 1
    KTQ = NKT // VCH
    s_kh = nc.dram_tensor("scr_kh", [HKVh * 128, T], BF16, kind="Internal").ap()
    s_k = nc.dram_tensor("scr_k", [HKVh, 2, 128, T], BF16, kind=_sk).ap()
    s_vh = nc.dram_tensor("scr_vh", [NKT * 128, KVDh], BF16, kind="Internal").ap()
    s_wosc = nc.dram_tensor("scr_wo", [NCH, 128, C], BF16, kind="Internal").ap()
    s_v = nc.dram_tensor("scr_v", [VCH, 2, KTQ, 128, KVDh], BF16, kind=_sk).ap()
    s_q = nc.dram_tensor("scr_q", [HKV, 128, NPAIR * GW], BF16, kind=_sk).ap()
    s_at = nc.dram_tensor("scr_at", [HKV, NPAIR, 128, GW], BF16, kind=_sk).ap()

    es = ExitStack()
    semlist = []

    def S(name):
        h = es.enter_context(nc.semaphore(name))
        s = SemT(h, name)
        semlist.append(s)
        return s

    qpe, qact, qdve, qgp, qsp = Q(), Q(), Q(), Q(), Q()
    ALLQ = (qpe, qact, qdve, qgp, qsp)

    s_pe, s_act, s_dve = S("s_pe"), S("s_act"), S("s_dve")
    s_sp = S("s_sp")
    s_out = S("s_out")
    s_ld0 = S("s_ld0")
    s_ld1 = S("s_ld1")
    s_ld2 = S("s_ld2")
    s_ld3 = S("s_ld3")
    s_xt = [S("s_xt0"), S("s_xt1")]
    s_wq = [S("s_wq0"), S("s_wq1")]
    s_kv = [S("s_kv0"), S("s_kv1")]
    s_aq = S("s_aq")
    s_wos = S("s_wos")
    s_wo = [S("s_wo0"), S("s_wo1")]
    s_xq = [S("s_xq0"), S("s_xq1")]
    h_cc = es.enter_context(nc.semaphore("s_cc"))
    s_cc = SemT(h_cc, "s_cc")
    s_ko = [S("s_ko0"), S("s_ko1")]
    s_vo = [S("s_vo0"), S("s_vo1")]
    s_qo = [S("s_qo0"), S("s_qo1")]
    s_ab = [S("s_ab0"), S("s_ab1")]
    s_ob = [S("s_ob0"), S("s_ob1"), S("s_ob2"), S("s_ob3")]

    def barrier():
        for q in ALLQ:
            for s in semlist:
                q.w(s, s.n)

    def fence_compute():
        for q in (qpe, qact, qdve):
            for s in (s_pe, s_act, s_dve, s_wo[0], s_wo[1], s_xq[0], s_xq[1], s_aq):
                q.w(s, s.n)

    pctx = ExitStack()

    def sb(name, shape, dt, ctx):
        return ctx.enter_context(nc.sbuf_tensor(name, shape, dt))

    def ps(name, shape, dt, ctx):
        return ctx.enter_context(nc.psum_tensor(name, shape, dt))

    ident_b = sb("ident_b", [128, 128], BF16, pctx)
    ones_b = sb("ones_b", [128, 128], BF16, pctx)
    kon = sb("kon", [128, 8], F32, pctx)
    EW = ER + NE
    aw_b = sb("aw_b", [128, NCH * EW], BF16, pctx)
    bS_b = sb("bS_b", [ER, C], BF16, pctx)
    em_b = sb("em_b", [NE, ER], BF16, pctx)

    t_id = qgp.do(lambda e: e.dma_start(ident_b[:, :], d_id), (s_ld0, 16))
    t_aT = qgp.do(lambda e: e.dma_start(
        aw_b[:, :].rearrange("p (c e) -> p c e", c=NCH),
        d_aw.rearrange("(c p) e -> p c e", p=128)), (s_ld0, 16))
    t_bS = qgp.do(lambda e: e.dma_start(bS_b[:, :], d_bS), (s_ld0, 16))
    t_em = qgp.do(lambda e: e.dma_start(em_b[:, :], d_em), (s_ld0, 16))
    tok_ld0 = s_ld0.n
    qdve.do(lambda e: e.memset(ones_b[:, :], 1.0), (s_dve, 1))
    qdve.do(lambda e: e.memset(kon[:, 0:1], -CSHIFT), (s_dve, 1))
    qdve.do(lambda e: e.memset(kon[:, 1:2], -0.5), (s_dve, 1))
    qdve.do(lambda e: e.memset(kon[:, 2:3], 0.0), (s_dve, 1))
    qdve.do(lambda e: e.memset(kon[:, 3:4], 1.0), (s_dve, 1))
    tok_kon = qdve.do(lambda e: e.memset(kon[:, 4:5], -1.0), (s_dve, 1))

    p1a = ExitStack()
    wk_sb = sb("wk_sb", [128, NCH * KVDh], BF16, p1a)
    wv_sb = sb("wv_sb", [128, NCH * KVDh], BF16, p1a)
    ckk_b = sb("ckk_b", [128, T], BF16, p1a)
    skk_b = sb("skk_b", [128, T], BF16, p1a)
    xts = [sb(f"xts{i}", [128, NCH * 256], BF16, p1a) for i in range(2)]
    rot_sb = sb("rot_sb", [128, 256], F32, p1a)
    t1_sb = sb("t1_sb", [128, 256], F32, p1a)
    t2_sb = sb("t2_sb", [128, 256], F32, p1a)
    kob = [sb(f"kob{i}", [128, 256], BF16, p1a) for i in range(2)]
    vob = [sb(f"vob{i}", [128, KVDh], BF16, p1a) for i in range(2)]
    NVS = (KVDh + 511) // 512
    pk = [ps(f"pk{i}", [128, 256], F32, p1a) for i in range(4)]
    pv = [ps(f"pv{i}", [128, min(512, KVDh - i * 512)], F32, p1a) for i in range(NVS)]

    csub = max(1, NCH // 4)
    for c0 in range(0, NCH, csub):
        c1 = min(c0 + csub, NCH)
        qgp.do(lambda e, c0=c0, c1=c1: e.dma_start(
            wk_sb[:, :].rearrange("p (c d) -> p c d", c=NCH)[:, c0:c1, :],
            d_wk.rearrange("(c p) d -> p c d", p=128)[:, c0:c1, :]), (s_ld1, 16))
        qgp.do(lambda e, c0=c0, c1=c1: e.dma_start(
            wv_sb[:, :].rearrange("p (c d) -> p c d", c=NCH)[:, c0:c1, :],
            d_wv.rearrange("(c p) d -> p c d", p=128)[:, c0:c1, :]), (s_ld1, 16))
    qgp.do(lambda e: e.dma_start(ckk_b[:, :], d_ckk), (s_ld1, 16))
    qgp.do(lambda e: e.dma_start(skk_b[:, :], d_skk), (s_ld1, 16))
    tok_ld1 = s_ld1.n

    NT1 = T // 256
    pk_free = [(None, 0)] * 4
    pk_free_act = [0] * 4
    kob_free = [0, 0]
    vob_free = [0, 0]
    xt_fills = [0, 0]
    xt_free_tok = [0, 0]
    rot_rd = 0
    t1_rd = 0
    pv_free = [0] * NVS

    for tt in range(min(NT1, NT1CAP)):
        bi = tt % 2
        qgp.w(s_pe, xt_free_tok[bi])
        for c0 in range(0, NCH, csub):
            c1 = min(c0 + csub, NCH)
            qgp.do(lambda e, c0=c0, c1=c1, bi=bi, tt=tt: e.dma_start(
                xts[bi][:, :].rearrange("p (c t) -> p c t", c=NCH)[:, c0:c1, :],
                d_xT.rearrange("(c p) t -> p c t", p=128)[:, c0:c1, tt * 256:(tt + 1) * 256]),
                (s_xt[bi], 16))
        xt_fills[bi] += 1
        tok_xt = 16 * (NCH // csub) * xt_fills[bi]

        qpe.w(s_ld1, tok_ld1)
        qpe.w(s_xt[bi], tok_xt)
        for ks in range(min(HKVh, HKVCAP) if SUB >= 2 else 0):
            pb = ks % 4
            qpe.w(s_dve, pk_free[pb][1])
            qpe.w(s_act, pk_free_act[pb])
            tokS = None
            for c in range(NCH):
                tokS = qpe.do(lambda e, pb=pb, c=c, ks=ks, bi=bi: e.matmul(
                    pk[pb][:, :],
                    wk_sb[:, :].rearrange("p (c d) -> p c d", c=NCH)[:, c, ks * 128:(ks + 1) * 128],
                    xts[bi][:, :].rearrange("p (c t) -> p c t", c=NCH)[:, c, :],
                    start=(c == 0), stop=(c == NCH - 1)),
                    (s_pe, 1))
            if SUB < 3:
                continue
            qact.w(s_pe, tokS)
            qact.w(s_dve, rot_rd)
            qact.do(lambda e, pb=pb: e.copy(rot_sb[0:64, :], pk[pb][64:128, :]), (s_act, 1))
            tokR = qact.do(lambda e, pb=pb: e.copy(rot_sb[64:128, :], pk[pb][0:64, :]), (s_act, 1))
            pk_free_act[pb] = tokR
            if SUB < 4:
                continue
            qdve.w(s_pe, tokS)
            qdve.w(s_act, tokR)
            qdve.w(s_ld1, tok_ld1)
            qdve.w(s_dve, t1_rd)
            tok1 = qdve.do(lambda e, pb=pb, tt=tt: e.tensor_tensor(
                t1_sb[:, :], pk[pb][:, :], ckk_b[:, tt * 256:(tt + 1) * 256], ALU.mult), (s_dve, 1))
            pk_free[pb] = (s_dve, tok1)
            t1_rd = tok1
            if SUB < 5:
                continue
            qdve.w(s_act, tokR)
            tok2 = qdve.do(lambda e, tt=tt: e.tensor_tensor(
                t2_sb[:, :], rotk_src(rot_sb), skk_b[:, tt * 256:(tt + 1) * 256], ALU.mult), (s_dve, 1))
            rot_rd = tok2
            if SUB < 6:
                continue
            ko = ks % 2
            qdve.w(s_dve, tok2)
            qdve.w(s_ko[ko], kob_free[ko])
            tokK = qdve.do(lambda e, ko=ko: e.tensor_tensor(
                kob[ko][:, :], t1_sb[:, :], t2_sb[:, :], ALU.add), (s_dve, 1))
            t1_rd = tokK
            if SUB < 7:
                continue
            qsp.w(s_dve, tokK)
            kob_free[ko] = qsp.do(lambda e, ks=ks, tt=tt, ko=ko: e.dma_start(
                s_kh[ks * 128:(ks + 1) * 128, tt * 256:(tt + 1) * 256], kob[ko][:, :]), (s_ko[ko], 16))
        last_pe_tok = None
        for tb in range(2 if SUB >= 8 else 0):
            vo = tb % 2
            for i in range(NVS):
                qpe.w(s_dve, pv_free[i])
            for c in range(NCH):
                for i in range(NVS):
                    w0 = i * 512
                    w1 = min(w0 + 512, KVDh)
                    last_pe_tok = qpe.do(lambda e, i=i, c=c, tb=tb, bi=bi, w0=w0, w1=w1: e.matmul(
                        pv[i][:, :],
                        xts[bi][:, :].rearrange("p (c t) -> p c t", c=NCH)[:, c, tb * 128:(tb + 1) * 128],
                        wv_sb[:, :].rearrange("p (c d) -> p c d", c=NCH)[:, c, w0:w1],
                        start=(c == 0), stop=(c == NCH - 1)),
                        (s_pe, 1))
            if SUB < 9:
                continue
            qdve.w(s_pe, last_pe_tok)
            qdve.w(s_vo[vo], vob_free[vo])
            tokV = None
            for i in range(NVS):
                w0 = i * 512
                w1 = min(w0 + 512, KVDh)
                tokV = qdve.do(lambda e, i=i, vo=vo, w0=w0, w1=w1: e.tensor_copy(
                    vob[vo][:, w0:w1], pv[i][:, :]), (s_dve, 1))
                pv_free[i] = tokV
            if SUB >= 10:
                qsp.w(s_dve, tokV)
                vob_free[vo] = qsp.do(lambda e, tt=tt, tb=tb, vo=vo: e.dma_start(
                    s_vh[(tt * 2 + tb) * 128:(tt * 2 + tb + 1) * 128, :], vob[vo][:, :]), (s_vo[vo], 16))
        if last_pe_tok is not None:
            xt_free_tok[bi] = last_pe_tok

    RG = [[2 * i, 2 * i + 1] for i in range(max(1, n_cores // 2))]
    spill_toks = [(sx, sx.n) for sx in (s_ko[0], s_ko[1], s_vo[0], s_vo[1])]

    def mk_cc():
        ccs = []
        for ks in range(HKVh):
            ccs.append(lambda e, ks=ks: e.collective_compute(
                "AllGather", ALU.bypass, replica_groups=RG,
                ins=[s_kh[ks * 128:(ks + 1) * 128, :].opt()],
                outs=[s_k[ks].rearrange("g p t -> (g p) t").opt()]))
        RQ = KTQ * 128
        for q in range(VCH):
            ccs.append(lambda e, q=q: e.collective_compute(
                "AllGather", ALU.bypass, replica_groups=RG,
                ins=[s_vh[q * RQ:(q + 1) * RQ, :].opt()],
                outs=[s_v[q].rearrange("g k p d -> (g k p) d").opt()]))
        return ccs

    cc_list = mk_cc()
    cc_total = len(cc_list)

    def emit_cc_chunk():
        if not cc_list:
            return
        for sx, v in spill_toks:
            qgp.w(sx, v)
        qgp.do(cc_list.pop(0), (s_cc, 1))

    p1a.close()
    barrier()

    if _phases < 2:
        qsp.w(s_ko[0], s_ko[0].n); qsp.w(s_ko[1], s_ko[1].n)
        qsp.w(s_vo[0], s_vo[0].n); qsp.w(s_vo[1], s_vo[1].n)
        return _finish(nc, ALLQ, es, pctx)

    p1b = ExitStack()
    xtq_sb = sb("xtq_sb", [128, NCH * QT], BF16, p1b)
    wqs = [sb(f"wqs{i}", [128, NCH * 128], BF16, p1b) for i in range(2)]
    ckq_b = sb("ckq_b", [128, QT], BF16, p1b)
    skq_b = sb("skq_b", [128, QT], BF16, p1b)
    rotq = sb("rotq", [128, QT], F32, p1b)
    t1q = sb("t1q", [128, QT], F32, p1b)
    t2q = sb("t2q", [128, QT], F32, p1b)
    qob = [sb(f"qob{i}", [128, QT], BF16, p1b) for i in range(2)]
    pq2 = [ps(f"pq{i}", [128, min(2048, QT)], F32, p1b) for i in range(2)]
    NQB = QT // 512

    for c0 in range(0, NCH, csub):
        c1 = min(c0 + csub, NCH)
        qgp.do(lambda e, c0=c0, c1=c1: e.dma_start(
            xtq_sb[:, :].rearrange("p (c t) -> p c t", c=NCH)[:, c0:c1, :],
            d_xTq.rearrange("(c p) t -> p c t", p=128)[:, c0:c1, :]), (s_ld2, 16))
    qgp.do(lambda e: e.dma_start(ckq_b[:, :], d_ckq), (s_ld2, 16))
    qgp.do(lambda e: e.dma_start(skq_b[:, :], d_skq), (s_ld2, 16))
    tok_ld2 = s_ld2.n

    wq_fills = [0, 0]
    wq_free = [0, 0]
    pq_free_dve = [0, 0]
    pq_free_act = [0, 0]
    rotq_rd = 0
    t1q_rd = 0
    qob_free = [0, 0]

    for qs in range(H):
        bi = qs % 2
        pq = pq2[qs % 2]
        qgp.w(s_pe, wq_free[bi])
        for c0 in range(0, NCH, csub):
            c1 = min(c0 + csub, NCH)
            qgp.do(lambda e, c0=c0, c1=c1, bi=bi, qs=qs: e.dma_start(
                wqs[bi][:, :].rearrange("p (c d) -> p c d", c=NCH)[:, c0:c1, :],
                d_wq.rearrange("(c p) d -> p c d", p=128)[:, c0:c1, qs * 128:(qs + 1) * 128]),
                (s_wq[bi], 16))
        wq_fills[bi] += 1
        tok_w = 16 * (NCH // csub) * wq_fills[bi]
        if qs >= 2 and qs % 3 == 2:
            emit_cc_chunk()

        qpe.w(s_ld2, tok_ld2)
        qpe.w(s_wq[bi], tok_w)
        qpe.w(s_dve, pq_free_dve[qs % 2])
        qpe.w(s_act, pq_free_act[qs % 2])
        tokS = None
        for c in range(NCH):
            for nb in range(NQB):
                tokS = qpe.do(lambda e, c=c, nb=nb, bi=bi, pq=pq: e.matmul(
                    pq[:, nb * 512:(nb + 1) * 512],
                    wqs[bi][:, :].rearrange("p (c d) -> p c d", c=NCH)[:, c, :],
                    xtq_sb[:, :].rearrange("p (c t) -> p c t", c=NCH)[:, c, nb * 512:(nb + 1) * 512],
                    start=(c == 0), stop=(c == NCH - 1)),
                    (s_pe, 1))
        wq_free[bi] = tokS
        qact.w(s_pe, tokS)
        qact.w(s_dve, rotq_rd)
        qact.do(lambda e, pq=pq: e.copy(rotq[0:64, :], pq[64:128, :]), (s_act, 1))
        tokR = qact.do(lambda e, pq=pq: e.copy(rotq[64:128, :], pq[0:64, :]), (s_act, 1))
        pq_free_act[qs % 2] = tokR
        qdve.w(s_pe, tokS)
        qdve.w(s_act, tokR)
        qdve.w(s_ld2, tok_ld2)
        qdve.w(s_dve, t1q_rd)
        tok1 = qdve.do(lambda e, pq=pq: e.tensor_tensor(t1q[:, :], pq[:, :], ckq_b[:, :], ALU.mult), (s_dve, 1))
        pq_free_dve[qs % 2] = tok1
        qdve.w(s_act, tokR)
        tok2 = qdve.do(lambda e: e.tensor_tensor(t2q[:, :], rotq[:, :], skq_b[:, :], ALU.mult), (s_dve, 1))
        rotq_rd = tok2
        qo = qs % 2
        qdve.w(s_dve, tok2)
        qdve.w(s_qo[qo], qob_free[qo])
        tokQ = qdve.do(lambda e, qo=qo: e.tensor_tensor(qob[qo][:, :], t1q[:, :], t2q[:, :], ALU.add), (s_dve, 1))
        t1q_rd = tokQ
        kvh, hig = qs // G, qs % G
        qsp.w(s_dve, tokQ)
        qob_free[qo] = qsp.do(lambda e, kvh=kvh, hig=hig, qo=qo: e.dma_start(
            s_q[kvh, :, :].rearrange("p (pr g t) -> p pr g t", pr=NPAIR, g=G)[:, :, hig, :],
            qob[qo][:, :].rearrange("p (pr t) -> p pr t", pr=NPAIR)), (s_qo[qo], 16))

    while cc_list:
        emit_cc_chunk()

    p1b.close()
    barrier()

    if _phases < 3:
        qsp.w(s_qo[0], s_qo[0].n); qsp.w(s_qo[1], s_qo[1].n)
        return _finish(nc, ALLQ, es, pctx)

    p2 = ExitStack()
    kb = [sb(f"kb{i}", [128, T], BF16, p2) for i in range(2)]
    praw = sb("praw", [128, GW], BF16, p2)
    pd_sb = sb("pd_sb", [128, GW], F32, p2)
    pl_sb = sb("pl_sb", [128, GW], F32, p2)
    vb = [sb(f"vb{i}", [128, NKT * 128], BF16, p2) for i in range(2)]
    qb = [sb(f"qb{i}", [128, NPAIR * GW], BF16, p2) for i in range(2)]
    mask_b = sb("mask_b", [128, NPAIR * 4 * 256], BF16, p2)
    ptb = [sb(f"ptb{i}", [128, GW], BF16, p2) for i in range(3)]
    rd_sb = sb("rd_sb", [128, GW], F32, p2)
    atb = [sb(f"atb{i}", [128, GW], BF16, p2) for i in range(2)]
    pS = [ps(f"pS{i}", [128, GW], F32, p2) for i in range(2)]
    pO = ps("pO", [128, GW], F32, p2)
    pD = ps("pD", [128, GW], F32, p2)
    NSB = GW // 512

    msub = max(1, (NPAIR * 4) // 4)
    for m0 in range(0, NPAIR * 4, msub):
        m1 = min(m0 + msub, NPAIR * 4)
        qgp.do(lambda e, m0=m0, m1=m1: e.dma_start(
            mask_b[:, :].rearrange("p (m t) -> p m t", m=NPAIR * 4)[:, m0:m1, :],
            d_mask.rearrange("(m p) t -> p m t", p=128)[:, m0:m1, :]), (s_ld3, 16))
    tok_ld3 = s_ld3.n

    ln_tok = 0
    pO_free = 0
    pD_free = 0
    kv_fills = [0, 0]
    kv_free = [0, 0]
    atb_free = [(None, 0), (None, 0)]
    pOD_free = 0
    rd_rd = 0

    def kv_load(kvh):
        bi = kvh % 2
        qgp.w(s_cc, cc_total)
        qgp.w(s_pe, kv_free[bi])
        grp, loc = kvh // HKVh, kvh % HKVh
        qgp.do(lambda e, bi=bi, grp=grp, loc=loc: e.dma_start(
            kb[bi][:, :], s_k[loc, grp, :, :]), (s_kv[bi], 16))
        for q in range(VCH):
            qgp.do(lambda e, bi=bi, q=q, grp=grp, loc=loc: e.dma_start(
                vb[bi][:, :].rearrange("p (k d) -> p k d", k=NKT)[:, q * KTQ:(q + 1) * KTQ, :],
                s_v[q, grp].rearrange("k p d -> p k d")[:, :, loc * 128:(loc + 1) * 128]),
                (s_kv[bi], 16))
        qgp.do(lambda e, kvh=kvh, bi=bi: e.dma_start(qb[bi][:, :], s_q[kvh, :, :]), (s_kv[bi], 16))
        kv_fills[bi] += 1
        return 16 * (2 + VCH) * kv_fills[bi]

    gktc = [0]
    exp_slot = [0, 0]
    av_slot = [0, 0, 0]
    pending_tail = []
    last_pe = 0

    def emit_av(pair, ktp, bi, nkt, rdy_tok, po_of):
        nonlocal last_pe
        sem, tok = rdy_tok[ktp]
        qpe.w(sem, tok)
        po = po_of[ktp]
        t_last = None
        if ktp == 0:
            qpe.w(s_dve, pO_free)
        for sbk in range(NSB):
            qpe.do(lambda e, ktp=ktp, po=po, sbk=sbk, bi=bi, nkt=nkt: e.matmul(
                pO[:, sbk * 512:(sbk + 1) * 512],
                vb[bi][:, ktp * 128:(ktp + 1) * 128],
                ptb[po][:, sbk * 512:(sbk + 1) * 512],
                start=(ktp == 0), stop=(ktp == nkt - 1)),
                (s_pe, 1))
            if ktp == 0 and sbk == 0:
                qpe.w(s_dve, pD_free)
            t_last = qpe.do(lambda e, ktp=ktp, po=po, sbk=sbk, nkt=nkt: e.matmul(
                pD[:, sbk * 512:(sbk + 1) * 512],
                ones_b[:, :],
                ptb[po][:, sbk * 512:(sbk + 1) * 512],
                start=(ktp == 0), stop=(ktp == nkt - 1)),
                (s_pe, 1))
        av_slot[po] = t_last
        last_pe = t_last

    def emit_evac(kvh, pair):
        nonlocal rd_rd, ln_tok, pO_free, pD_free
        ao = pair % 2
        qdve.w(s_pe, last_pe)
        qdve.w(s_dve, rd_rd)
        qdve.w(s_act, ln_tok)
        tokCO = qdve.do(lambda e: e.tensor_copy(praw[:, :], pO[:, :]), (s_dve, 1))
        tokCP = qdve.do(lambda e: e.tensor_copy(pd_sb[:, :], pD[:, :]), (s_dve, 1))
        pO_free = tokCO
        pD_free = tokCP
        qact.w(s_dve, tokCP)
        t_ln = qact.do(lambda e: e.activation(pl_sb[:, :], pd_sb[:, :], AF.Ln), (s_act, 1))
        qact.w(s_act, t_ln)
        ln_tok = qact.do(lambda e: e.activation(
            rd_sb[:, :], pl_sb[:, :], AF.Exp, scale=kon[:, 4:5]), (s_act, 1))
        qdve.w(s_act, ln_tok)
        ao_sem, ao_tok = atb_free[ao]
        qdve.w(ao_sem, ao_tok) if ao_sem is not None else None
        tokAT = qdve.do(lambda e, ao=ao: e.tensor_tensor(
            atb[ao][:, :], praw[:, :], rd_sb[:, :], ALU.mult), (s_dve, 1))
        rd_rd = tokAT
        qsp.w(s_dve, tokAT)
        atb_free[ao] = (s_ab[ao], qsp.do(lambda e, kvh=kvh, pair=pair, ao=ao: e.dma_start(
            s_at[kvh, pair, :, :], atb[ao][:, :]), (s_ab[ao], 16)))

    wo_chunks = []
    wsub = max(1, NCH // 16)
    for c0 in range(0, NCH, wsub):
        c1 = min(c0 + wsub, NCH)
        wo_chunks.append(lambda e, c0=c0, c1=c1: e.dma_start(
            s_wosc[c0:c1].rearrange("c p d -> p c d"),
            d_wo.rearrange("(c p) d -> p c d", p=128)[:, c0:c1, :]))
    wo_total = 16 * len(wo_chunks)

    def emit_wo_chunks(n):
        while wo_chunks and n > 0:
            qgp.do(wo_chunks.pop(0), (s_wos, 16))
            n -= 1

    kv_tok = [0, 0]
    kv_tok[0] = kv_load(0)
    if HKV > 1:
        kv_tok[1] = kv_load(1)

    for kvh in range(HKV):
        bi = kvh % 2
        if kvh + 2 < HKV:
            pass
        qpe.w(s_kv[bi], kv_tok[bi])
        qpe.w(s_ld3, tok_ld3)
        qpe.w(s_dve, tok_kon)
        last_pe = None
        for pair in range(NPAIR):
            nkt = 4 * pair + 4
            exp_tok = {}
            rdy_tok = {}
            po_of = {}
            for kt in range(nkt):
                gkt = gktc[0]
                gktc[0] += 1
                sx = pS[gkt % 2]
                qpe.w(s_act, exp_slot[gkt % 2])
                tokS = None
                for sbk in range(NSB):
                    tokS = qpe.do(lambda e, sx=sx, kt=kt, sbk=sbk, pair=pair, bi=bi: e.matmul(
                        sx[:, sbk * 512:(sbk + 1) * 512],
                        kb[bi][:, kt * 128:(kt + 1) * 128],
                        qb[bi][:, pair * GW + sbk * 512: pair * GW + (sbk + 1) * 512],
                        start=True, stop=True),
                        (s_pe, 1))
                if kt == 1 and pending_tail:
                    pending_tail.pop(0)()
                po = gkt % 3
                po_of[kt] = po
                qact.w(s_pe, tokS)
                qact.w(s_dve, tok_kon)
                qact.w(s_pe, av_slot[po])
                tE = qact.do(lambda e, sx=sx, po=po: e.activation(
                    ptb[po][:, :], sx[:, :], AF.Exp, bias=kon[:, 0:1]), (s_act, 1))
                exp_tok[kt] = tE
                exp_slot[gkt % 2] = tE
                rdy_tok[kt] = (s_act, tE)
                if kt >= 4 * pair:
                    off = kt - 4 * pair
                    qdve.w(s_act, tE)
                    qdve.w(s_ld3, tok_ld3)
                    mcol = (pair * 4 + off) * 256
                    def mfn(e, po=po, mcol=mcol):
                        a3 = ptb[po][:, :].rearrange("p (g t) -> p g t", g=G)
                        m3 = mask_b[:, mcol:mcol + 256].rearrange("p (g t) -> p g t", g=1)
                        a3b, m3b = broadcast_tensor_aps(a3, m3)
                        return e.tensor_tensor(a3, a3b, m3b, ALU.mult)
                    tM = qdve.do(mfn, (s_dve, 1))
                    rdy_tok[kt] = (s_dve, tM)
                ktp = kt - 2
                if ktp >= 0:
                    emit_av(pair, ktp, bi, nkt, rdy_tok, po_of)
            def mk_tail(pair=pair, bi=bi, nkt=nkt, rdy_tok=rdy_tok, po_of=po_of, kvh=kvh):
                def tail():
                    emit_av(pair, nkt - 2, bi, nkt, rdy_tok, po_of)
                    emit_av(pair, nkt - 1, bi, nkt, rdy_tok, po_of)
                    emit_evac(kvh, pair)
                    if pair == NPAIR - 1:
                        kv_free[bi] = last_pe
                        if kvh + 2 < HKV:
                            kv_tok[bi] = kv_load(kvh + 2)
                return tail
            pending_tail.append(mk_tail())
        emit_wo_chunks(2)

    while wo_chunks:
        emit_wo_chunks(1)
    while pending_tail:
        pending_tail.pop(0)()

    p2.close()
    barrier()

    if _phases < 4:
        qsp.w(s_ab[0], s_ab[0].n); qsp.w(s_ab[1], s_ab[1].n)
        return _finish(nc, ALLQ, es, pctx)

    RW = min(1024, C)
    NRND = C // RW
    NSQ = RW // 512

    p34 = ExitStack()
    atq = sb("atq", [128, 2 * HKV * GW], BF16, p34)
    wob = [sb(f"wob{i}", [128, NCH * CBW], BF16, p34) for i in range(2)]
    xqp = [sb(f"xqp{i}", [128, CBW], F32, p34) for i in range(2)]
    pw = [ps(f"pw{i}", [128, CBW], F32, p34) for i in range(2)]

    out_dma_count = 0
    atq_fills = 0
    atq_free = 0
    wo_fills = [0, 0]
    wo_free = [0, 0]
    xq_fills = [0, 0]
    xq_free = [0, 0]
    pw_free = [0, 0]
    trash_war = 0
    trans_last = 0
    pd2_rd = 0
    st1_war = 0
    ob_cnt = [0, 0, 0, 0]
    ytok = [(0, 0)] * 4

    def emit_quarter(qx):
        nonlocal atq_fills, atq_free, out_dma_count, trash_war, trans_last, pd2_rd, st1_war
        q4 = ExitStack()
        y_sb = sb(f"y_{qx}", [128, 4 * C], F32, q4)
        hbf = sb(f"hbf_{qx}", [128, 4 * C], BF16, q4)
        sums = sb(f"sums_{qx}", [128, 4 * NCB], F32, q4)
        sqs = sb(f"sqs_{qx}", [128, 4 * NCB], F32, q4)
        trash = sb(f"trash_{qx}", [128, 512], F32, q4)
        mst = sb(f"mst_{qx}", [128, 4], F32, q4)
        rst = sb(f"rst_{qx}", [128, 4], F32, q4)
        st1 = sb(f"st1_{qx}", [128, 6], F32, q4)
        gmid = sb(f"gmid_{qx}", [ER, 512], BF16, q4)

        qdve.w(s_dve, s_dve.n)
        qact.w(s_act, s_act.n)

        qsp.w(s_pe, atq_free)
        for kvh in range(HKV):
            for pri in range(2):
                qsp.do(lambda e, kvh=kvh, pri=pri: e.dma_start(
                    atq[:, (kvh * 2 + pri) * GW:(kvh * 2 + pri + 1) * GW],
                    s_at[kvh, 2 * qx + pri, :, :]), (s_aq, 16))
        atq_fills += 1
        tok_atq = 16 * 2 * HKV * atq_fills

        stt_tok = [[0] * NCB for _ in range(4)]
        sq_tok = [[0] * NCB for _ in range(4)]
        tokW = None
        for cb in range(NCB):
            wi = cb % 2
            qsp.w(s_wos, wo_total)
            qsp.w(s_pe, wo_free[wi])
            for c0 in range(0, NCH, csub):
                c1 = min(c0 + csub, NCH)
                qsp.do(lambda e, c0=c0, c1=c1, wi=wi, cb=cb: e.dma_start(
                    wob[wi][:, :].rearrange("p (c d) -> p c d", c=NCH)[:, c0:c1, :],
                    s_wosc.rearrange("c p d -> p c d")[:, c0:c1, cb * CBW:(cb + 1) * CBW]),
                    (s_wo[wi], 16))
            wo_fills[wi] += 1
            tok_wo = 16 * (NCH // csub) * wo_fills[wi]
            for tb in range(4):
                pri, off = tb // 2, tb % 2
                xi = (cb * 4 + tb) % 2
                qgp.w(s_dve, xq_free[xi])
                qgp.do(lambda e, tb=tb, cb=cb, xi=xi: e.dma_start(
                    xqp[xi][:, :], d_xq[qx * 512 + tb * 128: qx * 512 + (tb + 1) * 128,
                                        cb * CBW:(cb + 1) * CBW]), (s_xq[xi], 16))
                xq_fills[xi] += 1
                tok_xq = 16 * xq_fills[xi]

                pwi = tb % 2
                qpe.w(s_aq, tok_atq)
                qpe.w(s_wo[wi], tok_wo)
                qpe.w(s_dve, pw_free[pwi])
                for h in range(H):
                    kvh, hig = h // G, h % G
                    scol = (kvh * 2 + pri) * GW + hig * 256 + off * 128
                    tokW = qpe.do(lambda e, h=h, scol=scol, wi=wi, pwi=pwi: e.matmul(
                        pw[pwi][:, :],
                        atq[:, scol:scol + 128],
                        wob[wi][:, :].rearrange("p (c d) -> p c d", c=NCH)[:, h, :],
                        start=(h == 0), stop=(h == H - 1)),
                        (s_pe, 1))
                par, cnt = ytok[tb]
                qdve.w(s_ob[par], cnt)
                qdve.w(s_pe, tokW)
                qdve.w(s_xq[xi], tok_xq)
                tokY = qdve.do(lambda e, tb=tb, cb=cb, pwi=pwi, xi=xi: e.scalar_tensor_tensor(
                    y_sb[:, tb * C + cb * CBW: tb * C + (cb + 1) * CBW],
                    pw[pwi][:, :], 0.0, xqp[xi][:, :], ALU.add, ALU.add,
                    accum_out=sums[:, tb * NCB + cb: tb * NCB + cb + 1]), (s_dve, 1))
                pw_free[pwi] = tokY
                xq_free[xi] = tokY
                stt_tok[tb][cb] = tokY
                qact.w(s_dve, tokY)
                qact.w(s_act, trash_war)
                tokQ2 = qact.do(lambda e, tb=tb, cb=cb: e.activation(
                    trash[:, 0:CBW], y_sb[:, tb * C + cb * CBW: tb * C + (cb + 1) * CBW],
                    AF.Square, accum_out=sqs[:, tb * NCB + cb: tb * NCB + cb + 1]), (s_act, 1))
                trash_war = tokQ2
                sq_tok[tb][cb] = tokQ2
            wo_free[wi] = tokW
        wo_last_pe = s_pe.n

        hbf_tok = [0] * 4
        norm_tok = [0] * 4
        st1_last = 0
        for tb in range(4):
            qdve.w(s_dve, stt_tok[tb][NCB - 1])
            qdve.w(s_act, sq_tok[tb][NCB - 1])
            qdve.w(s_dve, st1_last)
            qdve.w(s_act, st1_war)
            t1 = qdve.do(lambda e, tb=tb: e.reduce_sum(
                st1[:, 0:1], sums[:, tb * NCB:(tb + 1) * NCB], axis=AX.X), (s_dve, 1))
            t2 = qdve.do(lambda e, tb=tb: e.reduce_sum(
                st1[:, 1:2], sqs[:, tb * NCB:(tb + 1) * NCB], axis=AX.X), (s_dve, 1))
            qdve.w(s_dve, t2)
            t3 = qdve.do(lambda e, tb=tb: e.tensor_scalar_mul(
                mst[:, tb:tb + 1], st1[:, 0:1], 1.0 / C), (s_dve, 1))
            qdve.w(s_dve, t3)
            t4 = qdve.do(lambda e, tb=tb: e.tensor_tensor(
                st1[:, 2:3], mst[:, tb:tb + 1], mst[:, tb:tb + 1], ALU.mult), (s_dve, 1))
            qdve.w(s_dve, t4)
            t5 = qdve.do(lambda e: e.scalar_tensor_tensor(
                st1[:, 3:4], st1[:, 1:2], 1.0 / C, st1[:, 2:3], ALU.mult, ALU.subtract), (s_dve, 1))
            qdve.w(s_dve, t5)
            t6 = qdve.do(lambda e: e.tensor_scalar_add(st1[:, 4:5], st1[:, 3:4], EPS), (s_dve, 1))
            qact.w(s_dve, t6)
            t7 = qact.do(lambda e: e.activation(st1[:, 5:6], st1[:, 4:5], AF.Ln), (s_act, 1))
            qact.w(s_act, t7)
            t8 = qact.do(lambda e, tb=tb: e.activation(
                rst[:, tb:tb + 1], st1[:, 5:6], AF.Exp, scale=kon[:, 1:2]), (s_act, 1))
            qdve.w(s_act, t8)
            t9 = qdve.do(lambda e, tb=tb: e.tensor_scalar(
                y_sb[:, tb * C:(tb + 1) * C], y_sb[:, tb * C:(tb + 1) * C],
                mst[:, tb:tb + 1], rst[:, tb:tb + 1], ALU.subtract, ALU.mult), (s_dve, 1))
            norm_tok[tb] = t9
            st1_last = t6
            st1_war = t8
            qact.w(s_dve, t9)
            qact.w(s_pe, trans_last)
            hbf_tok[tb] = qact.do(lambda e, tb=tb: e.copy(
                hbf[:, tb * C:(tb + 1) * C], y_sb[:, tb * C:(tb + 1) * C]), (s_act, 1))

        RW2 = 512
        NR2 = C // RW2
        p4a = ExitStack()
        hT2 = [sb(f"hT_{qx}{i}", [128, NCH * 128], BF16, p4a) for i in range(2)]
        lg = sb(f"lg_{qx}", [32, 512], F32, p4a)
        tA = sb(f"tA_{qx}", [32, 512], F32, p4a)
        tB = sb(f"tB_{qx}", [32, 512], F32, p4a)
        tC = sb(f"tC_{qx}", [32, 512], F32, p4a)
        tD = sb(f"tD_{qx}", [32, 512], F32, p4a)
        gb = sb(f"gb_{qx}", [32, 512], BF16, p4a)
        g64 = sb(f"g64_{qx}", [ER, 512], BF16, p4a)
        sums2 = sb(f"sums2_{qx}", [128, 4 * NR2], F32, p4a)
        sqs2 = sb(f"sqs2_{qx}", [128, 4 * NR2], F32, p4a)
        ptr = [ps(f"ptr_{qx}{i}", [128, 512], BF16, p4a) for i in range(2)]
        pm68 = ps(f"pm68_{qx}", [EW, 512], F32, p4a)
        pg = ps(f"pg_{qx}", [ER, 512], F32, p4a)
        pd2 = [ps(f"pd2_{qx}{i}", [128, RW2], F32, p4a) for i in range(2)]

        perm1 = [(i + 1) % NE if i < NE else i for i in range(32)]
        perm2 = [(i + 2) % NE if i < NE else i for i in range(32)]
        ptr_free = [0, 0]
        hT_free = [0, 0]
        mid_tok = [0] * 4
        gate_tok = [0] * 4
        gmid_tok = [0] * 4
        pd2_free = [0, 0]
        lgz = qdve.do(lambda e: e.memset(lg[:, :], 0.0), (s_dve, 1))

        def emit_A(tb):
            nonlocal trans_last
            hi2 = tb % 2
            qpe.w(s_act, hbf_tok[tb])
            qpe.w(s_ld0, tok_ld0)
            qpe.w(s_dve, pd2_rd)
            ev_tok = 0
            NGRP = (NCH + 3) // 4
            for grp_i in range(NGRP):
                pi = (tb * NGRP + grp_i) % 2
                qpe.w(s_dve, ptr_free[pi])
                tokT = None
                c0g = grp_i * 4
                c1g = min(c0g + 4, NCH)
                for cc in range(c0g, c1g):
                    tokT = qpe.do(lambda e, tb=tb, cc=cc, pi=pi, c0g=c0g: e.transpose(
                        ptr[pi][:, (cc - c0g) * 128:(cc - c0g + 1) * 128],
                        hbf[:, tb * C + cc * 128: tb * C + (cc + 1) * 128],
                        ident_b[:, :]), (s_pe, 1))
                trans_last = tokT
                qdve.w(s_pe, tokT)
                qdve.w(s_pe, hT_free[hi2])
                ev_tok = qdve.do(lambda e, c0g=c0g, c1g=c1g, pi=pi, hi2=hi2: e.tensor_copy(
                    hT2[hi2][:, c0g * 128:c1g * 128], ptr[pi][:, 0:(c1g - c0g) * 128]), (s_dve, 1))
                ptr_free[pi] = ev_tok
            qpe.w(s_dve, ev_tok)
            tokM = None
            for cc in range(NCH):
                tokM = qpe.do(lambda e, cc=cc, tb=tb, hi2=hi2: e.matmul(
                    pm68[:, tb * 128:(tb + 1) * 128],
                    aw_b[:, :].rearrange("p (c e) -> p c e", c=NCH)[:, cc, :],
                    hT2[hi2][:, cc * 128:(cc + 1) * 128],
                    start=(cc == 0), stop=(cc == NCH - 1)), (s_pe, 1))
            hT_free[hi2] = tokM
            mid_tok[tb] = tokM

        def emit_gate(tb):
            s0, s1 = tb * 128, (tb + 1) * 128
            L = lambda t: t[:, s0:s1]
            qdve.w(s_dve, lgz)
            qdve.w(s_pe, mid_tok[tb])
            tg = qdve.do(lambda e: e.tensor_copy(lg[0:NE, s0:s1], pm68[ER:EW, s0:s1]), (s_dve, 1))
            qdve.w(s_dve, tg)
            tg = qdve.do(lambda e: e.stream_shuffle(L(tA), L(lg), perm1), (s_dve, 1))
            qdve.w(s_dve, tg)
            tg = qdve.do(lambda e: e.tensor_max(L(tB), L(lg), L(tA)), (s_dve, 1))
            qdve.w(s_dve, tg)
            tg = qdve.do(lambda e: e.stream_shuffle(L(tA), L(tB), perm2), (s_dve, 1))
            qdve.w(s_dve, tg)
            tg = qdve.do(lambda e: e.tensor_max(L(tB), L(tB), L(tA)), (s_dve, 1))
            qdve.w(s_dve, tg)
            tg = qdve.do(lambda e: e.tensor_tensor(L(tC), L(lg), L(tB), ALU.is_equal), (s_dve, 1))
            qdve.w(s_dve, tg)
            tg = qdve.do(lambda e: e.scalar_tensor_tensor(
                L(tC), L(tC), -1e30, L(lg), ALU.mult, ALU.add), (s_dve, 1))
            qdve.w(s_dve, tg)
            tg = qdve.do(lambda e: e.stream_shuffle(L(tA), L(tC), perm1), (s_dve, 1))
            qdve.w(s_dve, tg)
            tg = qdve.do(lambda e: e.tensor_max(L(tC), L(tC), L(tA)), (s_dve, 1))
            qdve.w(s_dve, tg)
            tg = qdve.do(lambda e: e.stream_shuffle(L(tA), L(tC), perm2), (s_dve, 1))
            qdve.w(s_dve, tg)
            tg = qdve.do(lambda e: e.tensor_max(L(tC), L(tC), L(tA)), (s_dve, 1))
            qdve.w(s_dve, tg)
            tg = qdve.do(lambda e: e.tensor_tensor(L(tD), L(lg), L(tC), ALU.is_ge), (s_dve, 1))
            qdve.w(s_dve, tg)
            tg = qdve.do(lambda e: e.tensor_tensor(L(tA), L(lg), L(tB), ALU.subtract), (s_dve, 1))
            qact.w(s_dve, tg)
            ta = qact.do(lambda e: e.activation(
                L(tB), L(tA), AF.Exp, bias=kon[0:32, 2:3], scale=kon[0:32, 3:4]), (s_act, 1))
            qdve.w(s_act, ta)
            tg = qdve.do(lambda e: e.tensor_tensor(L(tB), L(tB), L(tD), ALU.mult), (s_dve, 1))
            qdve.w(s_dve, tg)
            tg = qdve.do(lambda e: e.stream_shuffle(L(tA), L(tB), perm1), (s_dve, 1))
            qdve.w(s_dve, tg)
            tg = qdve.do(lambda e: e.tensor_add(L(tC), L(tB), L(tA)), (s_dve, 1))
            qdve.w(s_dve, tg)
            tg = qdve.do(lambda e: e.stream_shuffle(L(tA), L(tC), perm2), (s_dve, 1))
            qdve.w(s_dve, tg)
            tg = qdve.do(lambda e: e.tensor_add(L(tC), L(tC), L(tA)), (s_dve, 1))
            qdve.w(s_dve, tg)
            tg = qdve.do(lambda e: e.reciprocal(L(tA), L(tC)), (s_dve, 1))
            qdve.w(s_dve, tg)
            tg = qdve.do(lambda e, sc=g.SCALE: e.scalar_tensor_tensor(
                L(tD), L(tB), sc, L(tA), ALU.mult, ALU.mult), (s_dve, 1))
            qdve.w(s_dve, tg)
            gate_tok[tb] = qdve.do(lambda e: e.tensor_copy(L(gb), L(tD)), (s_dve, 1))

        def emit_B1(tb):
            s0, s1 = tb * 128, (tb + 1) * 128
            qpe.w(s_dve, gate_tok[tb])
            qpe.w(s_ld0, tok_ld0)
            tokG = qpe.do(lambda e, s0=s0, s1=s1: e.matmul(
                pg[:, s0:s1], em_b[:, :], gb[0:NE, s0:s1], start=True, stop=True), (s_pe, 1))
            qdve.w(s_pe, tokG)
            tg = qdve.do(lambda e, s0=s0, s1=s1: e.tensor_copy(g64[:, s0:s1], pg[:, s0:s1]), (s_dve, 1))
            qdve.w(s_dve, tg)
            gmid_tok[tb] = qdve.do(lambda e, s0=s0, s1=s1: e.tensor_tensor(
                gmid[:, s0:s1], pm68[0:ER, s0:s1], g64[:, s0:s1], ALU.mult), (s_dve, 1))

        def emit_B2(tb):
            nonlocal trash_war, pd2_rd, out_dma_count, st1_last, st1_war
            y2_last = 0
            sq2_last = 0
            for rnd in range(NR2):
                di = (tb * NR2 + rnd) % 2
                qpe.w(s_dve, gmid_tok[tb])
                qpe.w(s_ld0, tok_ld0)
                qpe.w(s_dve, pd2_free[di])
                tokB = qpe.do(lambda e, tb=tb, rnd=rnd, di=di: e.matmul(
                    pd2[di][:, :],
                    gmid[:, tb * 128:(tb + 1) * 128],
                    bS_b[:, rnd * RW2:(rnd + 1) * RW2],
                    start=True, stop=True), (s_pe, 1))
                qdve.w(s_pe, tokB)
                qdve.w(s_dve, norm_tok[tb])
                qdve.w(s_act, hbf_tok[tb])
                tokY2 = qdve.do(lambda e, tb=tb, rnd=rnd, di=di: e.scalar_tensor_tensor(
                    y_sb[:, tb * C + rnd * RW2: tb * C + (rnd + 1) * RW2],
                    pd2[di][:, :], 0.0, y_sb[:, tb * C + rnd * RW2: tb * C + (rnd + 1) * RW2],
                    ALU.add, ALU.add,
                    accum_out=sums2[:, tb * NR2 + rnd: tb * NR2 + rnd + 1]), (s_dve, 1))
                pd2_free[di] = tokY2
                pd2_rd = tokY2
                y2_last = tokY2
                qact.w(s_dve, tokY2)
                qact.w(s_act, trash_war)
                tokS2 = qact.do(lambda e, tb=tb, rnd=rnd: e.activation(
                    trash[:, 0:RW2],
                    y_sb[:, tb * C + rnd * RW2: tb * C + (rnd + 1) * RW2],
                    AF.Square, accum_out=sqs2[:, tb * NR2 + rnd: tb * NR2 + rnd + 1]), (s_act, 1))
                trash_war = tokS2
                sq2_last = tokS2
            qdve.w(s_dve, y2_last)
            qdve.w(s_act, sq2_last)
            qdve.w(s_dve, st1_last)
            qdve.w(s_act, st1_war)
            t1 = qdve.do(lambda e, tb=tb: e.reduce_sum(
                st1[:, 0:1], sums2[:, tb * NR2:(tb + 1) * NR2], axis=AX.X), (s_dve, 1))
            t2 = qdve.do(lambda e, tb=tb: e.reduce_sum(
                st1[:, 1:2], sqs2[:, tb * NR2:(tb + 1) * NR2], axis=AX.X), (s_dve, 1))
            qdve.w(s_dve, t2)
            t3 = qdve.do(lambda e, tb=tb: e.tensor_scalar_mul(
                mst[:, tb:tb + 1], st1[:, 0:1], 1.0 / C), (s_dve, 1))
            qdve.w(s_dve, t3)
            t4 = qdve.do(lambda e, tb=tb: e.tensor_tensor(
                st1[:, 2:3], mst[:, tb:tb + 1], mst[:, tb:tb + 1], ALU.mult), (s_dve, 1))
            qdve.w(s_dve, t4)
            t5 = qdve.do(lambda e: e.scalar_tensor_tensor(
                st1[:, 3:4], st1[:, 1:2], 1.0 / C, st1[:, 2:3], ALU.mult, ALU.subtract), (s_dve, 1))
            qdve.w(s_dve, t5)
            t6 = qdve.do(lambda e: e.tensor_scalar_add(st1[:, 4:5], st1[:, 3:4], EPS), (s_dve, 1))
            qact.w(s_dve, t6)
            t7 = qact.do(lambda e: e.activation(st1[:, 5:6], st1[:, 4:5], AF.Ln), (s_act, 1))
            qact.w(s_act, t7)
            t8 = qact.do(lambda e, tb=tb: e.activation(
                rst[:, tb:tb + 1], st1[:, 5:6], AF.Exp, scale=kon[:, 1:2]), (s_act, 1))
            st1_last = t6
            st1_war = t8
            qdve.w(s_act, t8)
            t9o = qdve.do(lambda e, tb=tb: e.tensor_scalar(
                y_sb[:, tb * C:(tb + 1) * C], y_sb[:, tb * C:(tb + 1) * C],
                mst[:, tb:tb + 1], rst[:, tb:tb + 1], ALU.subtract, ALU.mult), (s_dve, 1))
            qsp.w(s_dve, t9o)
            ob_cnt[tb] += 16
            qsp.do(lambda e, tb=tb: e.dma_start(
                d_out[qx * 512 + tb * 128: qx * 512 + (tb + 1) * 128, :],
                y_sb[:, tb * C:(tb + 1) * C]), (s_ob[tb], 16))
            ytok[tb] = (tb, ob_cnt[tb])
            out_dma_count += 1

        for tb in range(4):
            emit_A(tb)
            emit_gate(tb)
            if tb >= 1:
                emit_B1(tb - 1)
            if tb >= 2:
                emit_B2(tb - 2)
        emit_B1(3)
        emit_B2(2)
        emit_B2(3)
        p4a.close()
        atq_free = wo_last_pe
        q4.close()

    for qx in range(NQ4):
        emit_quarter(qx)


    p34.close()

    for _s in s_ob:
        qsp.w(_s, _s.n)
    return _finish(nc, ALLQ, es, pctx)


def _finish(nc, ALLQ, es, pctx):
    qpe, qact, qdve, qgp, qsp = ALLQ
    with nc.Block() as block:
        @block.tensor
        def _(e):
            _replay(qpe, e)

        @block.scalar
        def _(e):
            _replay(qact, e)

        @block.vector
        def _(e):
            _replay(qdve, e)

        @block.gpsimd
        def _(e):
            _replay(qgp, e)

        @block.sync
        def _(e):
            _replay(qsp, e)

    es.close()
    pctx.close()
    return nc



def stage_core(g, core, X, Wq, Wk, Wv, Wo, A, B, Wr):
    T, C = g.T, g.C
    b, half = core // 2, core % 2
    js = jlist(g, half)
    qcols = np.concatenate([np.arange(j * 128, (j + 1) * 128) for j in js])
    Xb = np.asarray(X[b], dtype=np.float32)
    xT = np.ascontiguousarray(Xb.T)
    xTq = np.ascontiguousarray(Xb.T[:, qcols])
    xq = np.ascontiguousarray(Xb[qcols, :])

    inv = 1.0 / THETA ** (np.arange(64, dtype=np.float64) / 64.0)
    posk = np.arange(T, dtype=np.float64)
    angk = np.outer(posk, inv)
    ct = np.cos(angk).T.astype(np.float32)
    st = np.sin(angk).T.astype(np.float32)
    ckk = np.ascontiguousarray(np.vstack([ct, ct]))
    skk = np.ascontiguousarray(np.vstack([-st, st]))
    angq = angk[qcols]
    cq = np.cos(angq).T.astype(np.float32)
    sq = np.sin(angq).T.astype(np.float32)
    sc = 1.0 / math.sqrt(g.DH)
    ckq = np.ascontiguousarray(np.vstack([cq, cq]) * sc)
    skq = np.ascontiguousarray(np.vstack([-sq, sq]) * sc)

    m = np.zeros((g.NPAIR, 4, 128, 256), dtype=np.float32)
    pk = np.arange(128)
    for pr in range(g.NPAIR):
        for off in range(4):
            kt = 4 * pr + off
            kidx = kt * 128 + pk
            for sl2 in range(2):
                j = js[2 * pr + sl2]
                tok = j * 128 + np.arange(128)
                m[pr, off, :, sl2 * 128:(sl2 + 1) * 128] = (kidx[:, None] <= tok[None, :])
    maskq = np.ascontiguousarray(m.reshape(g.NPAIR * 4 * 128, 256))

    wkh = np.ascontiguousarray(Wk[:, half * g.KVDh:(half + 1) * g.KVDh])
    wvh = np.ascontiguousarray(Wv[:, half * g.KVDh:(half + 1) * g.KVDh])
    return {
        "xT": xT, "xTq": xTq, "xq": xq,
        "wq": Wq, "wk": wkh, "wv": wvh, "wo": Wo,
        "awT": None, "bS": None, "emat": None,
        "ckk": ckk, "skk": skk, "ckq": ckq, "skq": skq,
        "maskq": maskq, "ident": None,
    }


def kernel(L0_attn_X, L0_attn_Wq, L0_attn_Wk, L0_attn_Wv, L0_attn_Wo,
           L0_ffn_A, L0_ffn_B, L0_ffn_Wr, _geo=None, _ncores=None):
    g = _geo or Geo()
    X = np.asarray(L0_attn_X, dtype=np.float32)
    Wq = np.ascontiguousarray(np.asarray(L0_attn_Wq, dtype=np.float32))
    Wk = np.ascontiguousarray(np.asarray(L0_attn_Wk, dtype=np.float32))
    Wv = np.ascontiguousarray(np.asarray(L0_attn_Wv, dtype=np.float32))
    Wo = np.ascontiguousarray(np.asarray(L0_attn_Wo, dtype=np.float32))
    A = np.asarray(L0_ffn_A, dtype=np.float32)
    B = np.asarray(L0_ffn_B, dtype=np.float32)
    Wr = np.asarray(L0_ffn_Wr, dtype=np.float32)

    n_cores = _ncores if _ncores is not None else 2 * X.shape[0]
    awT = np.ascontiguousarray(
        np.concatenate([A.reshape(g.ER, g.C), Wr], axis=0).T)
    bS = np.ascontiguousarray(B.transpose(0, 2, 1).reshape(g.ER, g.C))
    emat = np.ascontiguousarray(np.kron(np.eye(g.NE, dtype=np.float32),
                                        np.ones((1, g.R), dtype=np.float32)))
    ident = np.eye(128, dtype=np.float32)

    in_maps = []
    for core in range(n_cores):
        d = stage_core(g, core, X, Wq, Wk, Wv, Wo, A, B, Wr)
        d["awT"], d["bS"], d["emat"], d["ident"] = awT, bS, emat, ident
        in_maps.append(d)

    nc = build_nc(g, n_cores=n_cores)
    res = run_bass_kernel_spmd(nc, in_maps, core_ids=list(range(n_cores)))

    out = np.empty((X.shape[0], g.T, g.C), dtype=np.float32)
    for core in range(n_cores):
        b, half = core // 2, core % 2
        js = jlist(g, half)
        qcols = np.concatenate([np.arange(j * 128, (j + 1) * 128) for j in js])
        out[b, qcols, :] = res.results[core]["out"]
    return out


# version 17
# speedup vs baseline: 1.2322x; 1.0412x over previous
import math
import numpy as np
from contextlib import ExitStack

import concourse.bass as bass
import concourse.mybir as mybir
from concourse.bass import broadcast_tensor_aps
from concourse.bass_utils import run_bass_kernel_spmd

F32 = mybir.dt.float32
BF16 = mybir.dt.bfloat16
AF = mybir.ActivationFunctionType
ALU = mybir.AluOpType
AX = mybir.AxisListType

THETA = 10000.0
EPS = 1e-5
CSHIFT = 12.0
SUB = 99
FENCE_Q = True
DBG_SCRATCH = False
NT1CAP = 999
HKVCAP = 999
SUB3MODE = 0
def rotk_src(t):
    return t[:, :]


class Geo:
    def __init__(self, T=4096, C=4096, H=32, HKV=8, NE=4, R=16):
        self.T, self.C, self.H, self.HKV, self.NE, self.R = T, C, H, HKV, NE, R
        self.DH = 128
        self.G = H // HKV
        self.NCH = C // 128
        self.KVD = HKV * 128
        self.HKVh = HKV // 2
        self.KVDh = self.KVD // 2
        self.NS = T // 256
        self.NPAIR = self.NS // 2
        self.NQ4 = self.NS // 4
        self.QT = T // 2
        self.NKT = T // 128
        self.ER = NE * R
        self.GW = self.G * 256
        self.SCALE = R / math.sqrt(R)
        assert self.NS % 4 == 0 and self.C % 512 == 0 and self.QT % 512 == 0


def jlist(g, half):
    if half == 0:
        return [2 * s + (s & 1) for s in range(g.NS)]
    return [2 * s + 1 - (s & 1) for s in range(g.NS)]


class SemT:
    __slots__ = ("h", "n", "name")

    def __init__(self, h, name):
        self.h, self.n, self.name = h, 0, name


class Q:
    __slots__ = ("items", "hi")

    def __init__(self):
        self.items = []
        self.hi = {}

    def w(self, sem, val):
        if val is None or val <= 0:
            return
        if self.hi.get(id(sem), -1) >= val:
            return
        self.hi[id(sem)] = val
        self.items.append((0, sem, val))

    def do(self, fn, *incs):
        toks = []
        for s, k in incs:
            s.n += k
            toks.append(s.n)
        self.items.append((1, fn, [(s, k) for s, k in incs]))
        if len(toks) == 1:
            return toks[0]
        return toks


def _replay(q, eng):
    for it in q.items:
        if it[0] == 0:
            eng.wait_ge(it[1].h, it[2])
        else:
            inst = it[1](eng)
            for s, k in it[2]:
                inst = inst.then_inc(s.h, k)


def build_nc(g, _phases=99, n_cores=8):
    nc = bass.Bass("TRN2", target_bir_lowering=False)
    T, C, H, HKV, NE, R = g.T, g.C, g.H, g.HKV, g.NE, g.R
    HKVh, KVDh = g.HKVh, g.KVDh
    NCH, KVD, NS, NPAIR, NQ4, QT, NKT, ER, G, GW = (
        g.NCH, g.KVD, g.NS, g.NPAIR, g.NQ4, g.QT, g.NKT, g.ER, g.G, g.GW)
    CBW = 256
    NCB = C // CBW

    d_xT = nc.dram_tensor("xT", [C, T], F32, kind="ExternalInput").ap()
    d_xTq = nc.dram_tensor("xTq", [C, QT], F32, kind="ExternalInput").ap()
    d_xq = nc.dram_tensor("xq", [QT, C], F32, kind="ExternalInput").ap()
    d_wq = nc.dram_tensor("wq", [C, C], F32, kind="ExternalInput").ap()
    d_wk = nc.dram_tensor("wk", [C, KVDh], F32, kind="ExternalInput").ap()
    d_wv = nc.dram_tensor("wv", [C, KVDh], F32, kind="ExternalInput").ap()
    d_wo = nc.dram_tensor("wo", [C, C], F32, kind="ExternalInput").ap()
    d_aw = nc.dram_tensor("awT", [C, ER + NE], F32, kind="ExternalInput").ap()
    d_bS = nc.dram_tensor("bS", [ER, C], F32, kind="ExternalInput").ap()
    d_em = nc.dram_tensor("emat", [NE, ER], F32, kind="ExternalInput").ap()
    d_ckk = nc.dram_tensor("ckk", [128, T], F32, kind="ExternalInput").ap()
    d_skk = nc.dram_tensor("skk", [128, T], F32, kind="ExternalInput").ap()
    d_ckq = nc.dram_tensor("ckq", [128, QT], F32, kind="ExternalInput").ap()
    d_skq = nc.dram_tensor("skq", [128, QT], F32, kind="ExternalInput").ap()
    d_mask = nc.dram_tensor("maskq", [NPAIR * 4 * 128, 256], F32, kind="ExternalInput").ap()
    d_id = nc.dram_tensor("ident", [128, 128], F32, kind="ExternalInput").ap()
    d_out = nc.dram_tensor("out", [QT, C], F32, kind="ExternalOutput").ap()
    d_dby = nc.dram_tensor("dby", [QT, C], F32, kind="ExternalOutput").ap() if DBG_SCRATCH else None
    s_dby = None

    _sk = "ExternalOutput" if DBG_SCRATCH else "Internal"
    VCH = 4 if NKT % 4 == 0 else 1
    KTQ = NKT // VCH
    s_kh = nc.dram_tensor("scr_kh", [HKVh * 128, T], BF16, kind="Internal").ap()
    s_k = nc.dram_tensor("scr_k", [HKVh, 2, 128, T], BF16, kind=_sk).ap()
    s_vh = nc.dram_tensor("scr_vh", [NKT * 128, KVDh], BF16, kind="Internal").ap()
    s_wosc = nc.dram_tensor("scr_wo", [NCH, 128, C], BF16, kind="Internal").ap()
    s_v = nc.dram_tensor("scr_v", [VCH, 2, KTQ, 128, KVDh], BF16, kind=_sk).ap()
    s_q = nc.dram_tensor("scr_q", [HKV, 128, NPAIR * GW], BF16, kind=_sk).ap()
    s_at = nc.dram_tensor("scr_at", [HKV, NPAIR, 128, GW], BF16, kind=_sk).ap()

    es = ExitStack()
    semlist = []

    def S(name):
        h = es.enter_context(nc.semaphore(name))
        s = SemT(h, name)
        semlist.append(s)
        return s

    qpe, qact, qdve, qgp, qsp = Q(), Q(), Q(), Q(), Q()
    ALLQ = (qpe, qact, qdve, qgp, qsp)

    s_pe, s_act, s_dve = S("s_pe"), S("s_act"), S("s_dve")
    s_sp = S("s_sp")
    s_out = S("s_out")
    s_ld0 = S("s_ld0")
    s_ld1 = S("s_ld1")
    s_ld2 = S("s_ld2")
    s_ld3 = S("s_ld3")
    s_xt = [S("s_xt0"), S("s_xt1")]
    s_wq = [S("s_wq0"), S("s_wq1")]
    s_xtq = [S(f"s_xtq{i}") for i in range(4)]
    s_kv = [S("s_kv0"), S("s_kv1")]
    s_aq = S("s_aq")
    s_wos = S("s_wos")
    s_wo = [S("s_wo0"), S("s_wo1")]
    s_xq = [S("s_xq0"), S("s_xq1")]
    h_cc = es.enter_context(nc.semaphore("s_cc"))
    s_cc = SemT(h_cc, "s_cc")
    s_ko = [S("s_ko0"), S("s_ko1")]
    s_vo = [S("s_vo0"), S("s_vo1")]
    s_qo = [S("s_qo0"), S("s_qo1")]
    s_ab = [S("s_ab0"), S("s_ab1")]
    s_ob = [S("s_ob0"), S("s_ob1"), S("s_ob2"), S("s_ob3")]

    def barrier():
        for q in ALLQ:
            for s in semlist:
                q.w(s, s.n)

    def fence_compute():
        for q in (qpe, qact, qdve):
            for s in (s_pe, s_act, s_dve, s_wo[0], s_wo[1], s_xq[0], s_xq[1], s_aq):
                q.w(s, s.n)

    pctx = ExitStack()

    def sb(name, shape, dt, ctx):
        return ctx.enter_context(nc.sbuf_tensor(name, shape, dt))

    def ps(name, shape, dt, ctx):
        return ctx.enter_context(nc.psum_tensor(name, shape, dt))

    ident_b = sb("ident_b", [128, 128], BF16, pctx)
    ones_b = sb("ones_b", [128, 128], BF16, pctx)
    kon = sb("kon", [128, 8], F32, pctx)
    EW = ER + NE
    aw_b = sb("aw_b", [128, NCH * EW], BF16, pctx)
    bS_b = sb("bS_b", [ER, C], BF16, pctx)
    em_b = sb("em_b", [NE, ER], BF16, pctx)

    t_id = qgp.do(lambda e: e.dma_start(ident_b[:, :], d_id), (s_ld0, 16))
    t_aT = qgp.do(lambda e: e.dma_start(
        aw_b[:, :].rearrange("p (c e) -> p c e", c=NCH),
        d_aw.rearrange("(c p) e -> p c e", p=128)), (s_ld0, 16))
    t_bS = qgp.do(lambda e: e.dma_start(bS_b[:, :], d_bS), (s_ld0, 16))
    t_em = qgp.do(lambda e: e.dma_start(em_b[:, :], d_em), (s_ld0, 16))
    tok_ld0 = s_ld0.n
    qdve.do(lambda e: e.memset(ones_b[:, :], 1.0), (s_dve, 1))
    qdve.do(lambda e: e.memset(kon[:, 0:1], -CSHIFT), (s_dve, 1))
    qdve.do(lambda e: e.memset(kon[:, 1:2], -0.5), (s_dve, 1))
    qdve.do(lambda e: e.memset(kon[:, 2:3], 0.0), (s_dve, 1))
    qdve.do(lambda e: e.memset(kon[:, 3:4], 1.0), (s_dve, 1))
    tok_kon = qdve.do(lambda e: e.memset(kon[:, 4:5], -1.0), (s_dve, 1))

    p1a = ExitStack()
    wk_sb = sb("wk_sb", [128, NCH * KVDh], BF16, p1a)
    wv_sb = sb("wv_sb", [128, NCH * KVDh], BF16, p1a)
    ckk_b = sb("ckk_b", [128, T], BF16, p1a)
    skk_b = sb("skk_b", [128, T], BF16, p1a)
    xts = [sb(f"xts{i}", [128, NCH * 256], BF16, p1a) for i in range(2)]
    rot_sb = sb("rot_sb", [128, 256], F32, p1a)
    t1_sb = sb("t1_sb", [128, 256], F32, p1a)
    t2_sb = sb("t2_sb", [128, 256], F32, p1a)
    kob = [sb(f"kob{i}", [128, 256], BF16, p1a) for i in range(2)]
    vob = [sb(f"vob{i}", [128, KVDh], BF16, p1a) for i in range(2)]
    NVS = (KVDh + 511) // 512
    pk = [ps(f"pk{i}", [128, 256], F32, p1a) for i in range(4)]
    pv = [ps(f"pv{i}", [128, min(512, KVDh - i * 512)], F32, p1a) for i in range(NVS)]

    csub = max(1, NCH // 4)
    for c0 in range(0, NCH, csub):
        c1 = min(c0 + csub, NCH)
        qgp.do(lambda e, c0=c0, c1=c1: e.dma_start(
            wk_sb[:, :].rearrange("p (c d) -> p c d", c=NCH)[:, c0:c1, :],
            d_wk.rearrange("(c p) d -> p c d", p=128)[:, c0:c1, :]), (s_ld1, 16))
        qgp.do(lambda e, c0=c0, c1=c1: e.dma_start(
            wv_sb[:, :].rearrange("p (c d) -> p c d", c=NCH)[:, c0:c1, :],
            d_wv.rearrange("(c p) d -> p c d", p=128)[:, c0:c1, :]), (s_ld1, 16))
    qgp.do(lambda e: e.dma_start(ckk_b[:, :], d_ckk), (s_ld1, 16))
    qgp.do(lambda e: e.dma_start(skk_b[:, :], d_skk), (s_ld1, 16))
    tok_ld1 = s_ld1.n

    NT1 = T // 256
    pk_free = [(None, 0)] * 4
    pk_free_act = [0] * 4
    kob_free = [0, 0]
    vob_free = [0, 0]
    xt_fills = [0, 0]
    xt_free_tok = [0, 0]
    rot_rd = 0
    t1_rd = 0
    pv_free = [0] * NVS

    for tt in range(min(NT1, NT1CAP)):
        bi = tt % 2
        qgp.w(s_pe, xt_free_tok[bi])
        for c0 in range(0, NCH, csub):
            c1 = min(c0 + csub, NCH)
            qgp.do(lambda e, c0=c0, c1=c1, bi=bi, tt=tt: e.dma_start(
                xts[bi][:, :].rearrange("p (c t) -> p c t", c=NCH)[:, c0:c1, :],
                d_xT.rearrange("(c p) t -> p c t", p=128)[:, c0:c1, tt * 256:(tt + 1) * 256]),
                (s_xt[bi], 16))
        xt_fills[bi] += 1
        tok_xt = 16 * (NCH // csub) * xt_fills[bi]

        qpe.w(s_ld1, tok_ld1)
        qpe.w(s_xt[bi], tok_xt)
        for ks in range(min(HKVh, HKVCAP) if SUB >= 2 else 0):
            pb = ks % 4
            qpe.w(s_dve, pk_free[pb][1])
            qpe.w(s_act, pk_free_act[pb])
            tokS = None
            for c in range(NCH):
                tokS = qpe.do(lambda e, pb=pb, c=c, ks=ks, bi=bi: e.matmul(
                    pk[pb][:, :],
                    wk_sb[:, :].rearrange("p (c d) -> p c d", c=NCH)[:, c, ks * 128:(ks + 1) * 128],
                    xts[bi][:, :].rearrange("p (c t) -> p c t", c=NCH)[:, c, :],
                    start=(c == 0), stop=(c == NCH - 1)),
                    (s_pe, 1))
            if SUB < 3:
                continue
            qact.w(s_pe, tokS)
            qact.w(s_dve, rot_rd)
            qact.do(lambda e, pb=pb: e.copy(rot_sb[0:64, :], pk[pb][64:128, :]), (s_act, 1))
            tokR = qact.do(lambda e, pb=pb: e.copy(rot_sb[64:128, :], pk[pb][0:64, :]), (s_act, 1))
            pk_free_act[pb] = tokR
            if SUB < 4:
                continue
            qdve.w(s_pe, tokS)
            qdve.w(s_act, tokR)
            qdve.w(s_ld1, tok_ld1)
            qdve.w(s_dve, t1_rd)
            tok1 = qdve.do(lambda e, pb=pb, tt=tt: e.tensor_tensor(
                t1_sb[:, :], pk[pb][:, :], ckk_b[:, tt * 256:(tt + 1) * 256], ALU.mult), (s_dve, 1))
            pk_free[pb] = (s_dve, tok1)
            t1_rd = tok1
            if SUB < 5:
                continue
            qdve.w(s_act, tokR)
            tok2 = qdve.do(lambda e, tt=tt: e.tensor_tensor(
                t2_sb[:, :], rotk_src(rot_sb), skk_b[:, tt * 256:(tt + 1) * 256], ALU.mult), (s_dve, 1))
            rot_rd = tok2
            if SUB < 6:
                continue
            ko = ks % 2
            qdve.w(s_dve, tok2)
            qdve.w(s_ko[ko], kob_free[ko])
            tokK = qdve.do(lambda e, ko=ko: e.tensor_tensor(
                kob[ko][:, :], t1_sb[:, :], t2_sb[:, :], ALU.add), (s_dve, 1))
            t1_rd = tokK
            if SUB < 7:
                continue
            qsp.w(s_dve, tokK)
            kob_free[ko] = qsp.do(lambda e, ks=ks, tt=tt, ko=ko: e.dma_start(
                s_kh[ks * 128:(ks + 1) * 128, tt * 256:(tt + 1) * 256], kob[ko][:, :]), (s_ko[ko], 16))
        last_pe_tok = None
        for tb in range(2 if SUB >= 8 else 0):
            vo = tb % 2
            for i in range(NVS):
                qpe.w(s_dve, pv_free[i])
            for c in range(NCH):
                for i in range(NVS):
                    w0 = i * 512
                    w1 = min(w0 + 512, KVDh)
                    last_pe_tok = qpe.do(lambda e, i=i, c=c, tb=tb, bi=bi, w0=w0, w1=w1: e.matmul(
                        pv[i][:, :],
                        xts[bi][:, :].rearrange("p (c t) -> p c t", c=NCH)[:, c, tb * 128:(tb + 1) * 128],
                        wv_sb[:, :].rearrange("p (c d) -> p c d", c=NCH)[:, c, w0:w1],
                        start=(c == 0), stop=(c == NCH - 1)),
                        (s_pe, 1))
            if SUB < 9:
                continue
            qdve.w(s_pe, last_pe_tok)
            qdve.w(s_vo[vo], vob_free[vo])
            tokV = None
            for i in range(NVS):
                w0 = i * 512
                w1 = min(w0 + 512, KVDh)
                tokV = qdve.do(lambda e, i=i, vo=vo, w0=w0, w1=w1: e.tensor_copy(
                    vob[vo][:, w0:w1], pv[i][:, :]), (s_dve, 1))
                pv_free[i] = tokV
            if SUB >= 10:
                qsp.w(s_dve, tokV)
                vob_free[vo] = qsp.do(lambda e, tt=tt, tb=tb, vo=vo: e.dma_start(
                    s_vh[(tt * 2 + tb) * 128:(tt * 2 + tb + 1) * 128, :], vob[vo][:, :]), (s_vo[vo], 16))
        if last_pe_tok is not None:
            xt_free_tok[bi] = last_pe_tok

    RG = [[2 * i, 2 * i + 1] for i in range(max(1, n_cores // 2))]
    spill_toks = [(sx, sx.n) for sx in (s_ko[0], s_ko[1], s_vo[0], s_vo[1])]

    def mk_cc():
        ccs = []
        for ks in range(HKVh):
            ccs.append(lambda e, ks=ks: e.collective_compute(
                "AllGather", ALU.bypass, replica_groups=RG,
                ins=[s_kh[ks * 128:(ks + 1) * 128, :].opt()],
                outs=[s_k[ks].rearrange("g p t -> (g p) t").opt()]))
        RQ = KTQ * 128
        for q in range(VCH):
            ccs.append(lambda e, q=q: e.collective_compute(
                "AllGather", ALU.bypass, replica_groups=RG,
                ins=[s_vh[q * RQ:(q + 1) * RQ, :].opt()],
                outs=[s_v[q].rearrange("g k p d -> (g k p) d").opt()]))
        return ccs

    cc_list = mk_cc()
    cc_total = len(cc_list)

    def emit_cc_chunk():
        if not cc_list:
            return
        for sx, v in spill_toks:
            qgp.w(sx, v)
        qgp.do(cc_list.pop(0), (s_cc, 1))

    p1a.close()
    barrier()

    if _phases < 2:
        qsp.w(s_ko[0], s_ko[0].n); qsp.w(s_ko[1], s_ko[1].n)
        qsp.w(s_vo[0], s_vo[0].n); qsp.w(s_vo[1], s_vo[1].n)
        return _finish(nc, ALLQ, es, pctx)

    p1b = ExitStack()
    xtq_sb = sb("xtq_sb", [128, NCH * QT], BF16, p1b)
    wqs = [sb(f"wqs{i}", [128, NCH * 128], BF16, p1b) for i in range(2)]
    ckq_b = sb("ckq_b", [128, QT], BF16, p1b)
    skq_b = sb("skq_b", [128, QT], BF16, p1b)
    rotq = sb("rotq", [128, QT], F32, p1b)
    t1q = sb("t1q", [128, QT], F32, p1b)
    t2q = sb("t2q", [128, QT], F32, p1b)
    qob = [sb(f"qob{i}", [128, QT], BF16, p1b) for i in range(2)]
    pq2 = [ps(f"pq{i}", [128, min(2048, QT)], F32, p1b) for i in range(2)]
    NQB = QT // 512

    for si_, c0 in enumerate(range(0, NCH, csub)):
        c1 = min(c0 + csub, NCH)
        qgp.do(lambda e, c0=c0, c1=c1: e.dma_start(
            xtq_sb[:, :].rearrange("p (c t) -> p c t", c=NCH)[:, c0:c1, :],
            d_xTq.rearrange("(c p) t -> p c t", p=128)[:, c0:c1, :]), (s_xtq[min(si_, 3)], 16))
    qgp.do(lambda e: e.dma_start(ckq_b[:, :], d_ckq), (s_ld2, 16))
    qgp.do(lambda e: e.dma_start(skq_b[:, :], d_skq), (s_ld2, 16))
    tok_ld2 = s_ld2.n

    wq_fills = [0, 0]
    wq_free = [0, 0]
    pq_free_dve = [0, 0]
    pq_free_act = [0, 0]
    rotq_rd = 0
    t1q_rd = 0
    qob_free = [0, 0]

    for qs in range(H):
        bi = qs % 2
        pq = pq2[qs % 2]
        qgp.w(s_pe, wq_free[bi])
        for c0 in range(0, NCH, csub):
            c1 = min(c0 + csub, NCH)
            qgp.do(lambda e, c0=c0, c1=c1, bi=bi, qs=qs: e.dma_start(
                wqs[bi][:, :].rearrange("p (c d) -> p c d", c=NCH)[:, c0:c1, :],
                d_wq.rearrange("(c p) d -> p c d", p=128)[:, c0:c1, qs * 128:(qs + 1) * 128]),
                (s_wq[bi], 16))
        wq_fills[bi] += 1
        tok_w = 16 * (NCH // csub) * wq_fills[bi]
        if qs >= 2 and qs % 3 == 2:
            emit_cc_chunk()

        qpe.w(s_wq[bi], tok_w)
        qpe.w(s_dve, pq_free_dve[qs % 2])
        qpe.w(s_act, pq_free_act[qs % 2])
        tokS = None
        for c in range(NCH):
            qpe.w(s_xtq[min(c // csub, 3)], 16)
            for nb in range(NQB):
                tokS = qpe.do(lambda e, c=c, nb=nb, bi=bi, pq=pq: e.matmul(
                    pq[:, nb * 512:(nb + 1) * 512],
                    wqs[bi][:, :].rearrange("p (c d) -> p c d", c=NCH)[:, c, :],
                    xtq_sb[:, :].rearrange("p (c t) -> p c t", c=NCH)[:, c, nb * 512:(nb + 1) * 512],
                    start=(c == 0), stop=(c == NCH - 1)),
                    (s_pe, 1))
        wq_free[bi] = tokS
        qact.w(s_pe, tokS)
        qact.w(s_dve, rotq_rd)
        qact.do(lambda e, pq=pq: e.copy(rotq[0:64, :], pq[64:128, :]), (s_act, 1))
        tokR = qact.do(lambda e, pq=pq: e.copy(rotq[64:128, :], pq[0:64, :]), (s_act, 1))
        pq_free_act[qs % 2] = tokR
        qdve.w(s_pe, tokS)
        qdve.w(s_act, tokR)
        qdve.w(s_ld2, tok_ld2)
        qdve.w(s_dve, t1q_rd)
        tok1 = qdve.do(lambda e, pq=pq: e.tensor_tensor(t1q[:, :], pq[:, :], ckq_b[:, :], ALU.mult), (s_dve, 1))
        pq_free_dve[qs % 2] = tok1
        qdve.w(s_act, tokR)
        tok2 = qdve.do(lambda e: e.tensor_tensor(t2q[:, :], rotq[:, :], skq_b[:, :], ALU.mult), (s_dve, 1))
        rotq_rd = tok2
        qo = qs % 2
        qdve.w(s_dve, tok2)
        qdve.w(s_qo[qo], qob_free[qo])
        tokQ = qdve.do(lambda e, qo=qo: e.tensor_tensor(qob[qo][:, :], t1q[:, :], t2q[:, :], ALU.add), (s_dve, 1))
        t1q_rd = tokQ
        kvh, hig = qs // G, qs % G
        qsp.w(s_dve, tokQ)
        qob_free[qo] = qsp.do(lambda e, kvh=kvh, hig=hig, qo=qo: e.dma_start(
            s_q[kvh, :, :].rearrange("p (pr g t) -> p pr g t", pr=NPAIR, g=G)[:, :, hig, :],
            qob[qo][:, :].rearrange("p (pr t) -> p pr t", pr=NPAIR)), (s_qo[qo], 16))

    while cc_list:
        emit_cc_chunk()

    p1b.close()
    barrier()

    if _phases < 3:
        qsp.w(s_qo[0], s_qo[0].n); qsp.w(s_qo[1], s_qo[1].n)
        return _finish(nc, ALLQ, es, pctx)

    p2 = ExitStack()
    kb = [sb(f"kb{i}", [128, T], BF16, p2) for i in range(2)]
    praw = sb("praw", [128, GW], BF16, p2)
    pd_sb = sb("pd_sb", [128, GW], F32, p2)
    pl_sb = sb("pl_sb", [128, GW], F32, p2)
    vb = [sb(f"vb{i}", [128, NKT * 128], BF16, p2) for i in range(2)]
    qb = [sb(f"qb{i}", [128, NPAIR * GW], BF16, p2) for i in range(2)]
    mask_b = sb("mask_b", [128, NPAIR * 4 * 256], BF16, p2)
    ptb = [sb(f"ptb{i}", [128, GW], BF16, p2) for i in range(3)]
    rd_sb = sb("rd_sb", [128, GW], F32, p2)
    atb = [sb(f"atb{i}", [128, GW], BF16, p2) for i in range(2)]
    pS = [ps(f"pS{i}", [128, GW], F32, p2) for i in range(2)]
    pO = ps("pO", [128, GW], F32, p2)
    pD = ps("pD", [128, GW], F32, p2)
    NSB = GW // 512

    msub = max(1, (NPAIR * 4) // 4)
    for m0 in range(0, NPAIR * 4, msub):
        m1 = min(m0 + msub, NPAIR * 4)
        qgp.do(lambda e, m0=m0, m1=m1: e.dma_start(
            mask_b[:, :].rearrange("p (m t) -> p m t", m=NPAIR * 4)[:, m0:m1, :],
            d_mask.rearrange("(m p) t -> p m t", p=128)[:, m0:m1, :]), (s_ld3, 16))
    tok_ld3 = s_ld3.n

    ln_tok = 0
    pO_free = 0
    pD_free = 0
    kv_fills = [0, 0]
    kv_free = [0, 0]
    atb_free = [(None, 0), (None, 0)]
    pOD_free = 0
    rd_rd = 0

    def kv_load(kvh):
        bi = kvh % 2
        qgp.w(s_cc, cc_total)
        qgp.w(s_pe, kv_free[bi])
        grp, loc = kvh // HKVh, kvh % HKVh
        qgp.do(lambda e, bi=bi, grp=grp, loc=loc: e.dma_start(
            kb[bi][:, :], s_k[loc, grp, :, :]), (s_kv[bi], 16))
        for q in range(VCH):
            qgp.do(lambda e, bi=bi, q=q, grp=grp, loc=loc: e.dma_start(
                vb[bi][:, :].rearrange("p (k d) -> p k d", k=NKT)[:, q * KTQ:(q + 1) * KTQ, :],
                s_v[q, grp].rearrange("k p d -> p k d")[:, :, loc * 128:(loc + 1) * 128]),
                (s_kv[bi], 16))
        qgp.do(lambda e, kvh=kvh, bi=bi: e.dma_start(qb[bi][:, :], s_q[kvh, :, :]), (s_kv[bi], 16))
        kv_fills[bi] += 1
        return 16 * (2 + VCH) * kv_fills[bi]

    gktc = [0]
    exp_slot = [0, 0]
    av_slot = [0, 0, 0]
    pending_tail = []
    last_pe = 0

    def emit_av(pair, ktp, bi, nkt, rdy_tok, po_of):
        nonlocal last_pe
        sem, tok = rdy_tok[ktp]
        qpe.w(sem, tok)
        po = po_of[ktp]
        t_last = None
        if ktp == 0:
            qpe.w(s_dve, pO_free)
        for sbk in range(NSB):
            qpe.do(lambda e, ktp=ktp, po=po, sbk=sbk, bi=bi, nkt=nkt: e.matmul(
                pO[:, sbk * 512:(sbk + 1) * 512],
                vb[bi][:, ktp * 128:(ktp + 1) * 128],
                ptb[po][:, sbk * 512:(sbk + 1) * 512],
                start=(ktp == 0), stop=(ktp == nkt - 1)),
                (s_pe, 1))
            if ktp == 0 and sbk == 0:
                qpe.w(s_dve, pD_free)
            t_last = qpe.do(lambda e, ktp=ktp, po=po, sbk=sbk, nkt=nkt: e.matmul(
                pD[:, sbk * 512:(sbk + 1) * 512],
                ones_b[:, :],
                ptb[po][:, sbk * 512:(sbk + 1) * 512],
                start=(ktp == 0), stop=(ktp == nkt - 1)),
                (s_pe, 1))
        av_slot[po] = t_last
        last_pe = t_last

    def emit_evac(kvh, pair):
        nonlocal rd_rd, ln_tok, pO_free, pD_free
        ao = pair % 2
        qdve.w(s_pe, last_pe)
        qdve.w(s_dve, rd_rd)
        qdve.w(s_act, ln_tok)
        tokCO = qdve.do(lambda e: e.tensor_copy(praw[:, :], pO[:, :]), (s_dve, 1))
        tokCP = qdve.do(lambda e: e.tensor_copy(pd_sb[:, :], pD[:, :]), (s_dve, 1))
        pO_free = tokCO
        pD_free = tokCP
        qact.w(s_dve, tokCP)
        t_ln = qact.do(lambda e: e.activation(pl_sb[:, :], pd_sb[:, :], AF.Ln), (s_act, 1))
        qact.w(s_act, t_ln)
        ln_tok = qact.do(lambda e: e.activation(
            rd_sb[:, :], pl_sb[:, :], AF.Exp, scale=kon[:, 4:5]), (s_act, 1))
        qdve.w(s_act, ln_tok)
        ao_sem, ao_tok = atb_free[ao]
        qdve.w(ao_sem, ao_tok) if ao_sem is not None else None
        tokAT = qdve.do(lambda e, ao=ao: e.tensor_tensor(
            atb[ao][:, :], praw[:, :], rd_sb[:, :], ALU.mult), (s_dve, 1))
        rd_rd = tokAT
        qsp.w(s_dve, tokAT)
        atb_free[ao] = (s_ab[ao], qsp.do(lambda e, kvh=kvh, pair=pair, ao=ao: e.dma_start(
            s_at[kvh, pair, :, :], atb[ao][:, :]), (s_ab[ao], 16)))

    wo_chunks = []
    wsub = max(1, NCH // 16)
    for c0 in range(0, NCH, wsub):
        c1 = min(c0 + wsub, NCH)
        wo_chunks.append(lambda e, c0=c0, c1=c1: e.dma_start(
            s_wosc[c0:c1].rearrange("c p d -> p c d"),
            d_wo.rearrange("(c p) d -> p c d", p=128)[:, c0:c1, :]))
    wo_total = 16 * len(wo_chunks)

    def emit_wo_chunks(n):
        while wo_chunks and n > 0:
            qgp.do(wo_chunks.pop(0), (s_wos, 16))
            n -= 1

    kv_tok = [0, 0]
    kv_tok[0] = kv_load(0)
    if HKV > 1:
        kv_tok[1] = kv_load(1)

    for kvh in range(HKV):
        bi = kvh % 2
        if kvh + 2 < HKV:
            pass
        qpe.w(s_kv[bi], kv_tok[bi])
        qpe.w(s_ld3, tok_ld3)
        qpe.w(s_dve, tok_kon)
        last_pe = None
        for pair in range(NPAIR):
            nkt = 4 * pair + 4
            exp_tok = {}
            rdy_tok = {}
            po_of = {}
            for kt in range(nkt):
                gkt = gktc[0]
                gktc[0] += 1
                sx = pS[gkt % 2]
                qpe.w(s_act, exp_slot[gkt % 2])
                tokS = None
                for sbk in range(NSB):
                    tokS = qpe.do(lambda e, sx=sx, kt=kt, sbk=sbk, pair=pair, bi=bi: e.matmul(
                        sx[:, sbk * 512:(sbk + 1) * 512],
                        kb[bi][:, kt * 128:(kt + 1) * 128],
                        qb[bi][:, pair * GW + sbk * 512: pair * GW + (sbk + 1) * 512],
                        start=True, stop=True),
                        (s_pe, 1))
                if kt == 1 and pending_tail:
                    pending_tail.pop(0)()
                po = gkt % 3
                po_of[kt] = po
                qact.w(s_pe, tokS)
                qact.w(s_dve, tok_kon)
                qact.w(s_pe, av_slot[po])
                tE = qact.do(lambda e, sx=sx, po=po: e.activation(
                    ptb[po][:, :], sx[:, :], AF.Exp, bias=kon[:, 0:1]), (s_act, 1))
                exp_tok[kt] = tE
                exp_slot[gkt % 2] = tE
                rdy_tok[kt] = (s_act, tE)
                if kt >= 4 * pair:
                    off = kt - 4 * pair
                    qdve.w(s_act, tE)
                    qdve.w(s_ld3, tok_ld3)
                    mcol = (pair * 4 + off) * 256
                    def mfn(e, po=po, mcol=mcol):
                        a3 = ptb[po][:, :].rearrange("p (g t) -> p g t", g=G)
                        m3 = mask_b[:, mcol:mcol + 256].rearrange("p (g t) -> p g t", g=1)
                        a3b, m3b = broadcast_tensor_aps(a3, m3)
                        return e.tensor_tensor(a3, a3b, m3b, ALU.mult)
                    tM = qdve.do(mfn, (s_dve, 1))
                    rdy_tok[kt] = (s_dve, tM)
                ktp = kt - 2
                if ktp >= 0:
                    emit_av(pair, ktp, bi, nkt, rdy_tok, po_of)
            def mk_tail(pair=pair, bi=bi, nkt=nkt, rdy_tok=rdy_tok, po_of=po_of, kvh=kvh):
                def tail():
                    emit_av(pair, nkt - 2, bi, nkt, rdy_tok, po_of)
                    emit_av(pair, nkt - 1, bi, nkt, rdy_tok, po_of)
                    emit_evac(kvh, pair)
                    if pair == NPAIR - 1:
                        kv_free[bi] = last_pe
                        if kvh + 2 < HKV:
                            kv_tok[bi] = kv_load(kvh + 2)
                return tail
            pending_tail.append(mk_tail())
        emit_wo_chunks(2)

    while wo_chunks:
        emit_wo_chunks(1)
    while pending_tail:
        pending_tail.pop(0)()

    p2.close()
    barrier()

    if _phases < 4:
        qsp.w(s_ab[0], s_ab[0].n); qsp.w(s_ab[1], s_ab[1].n)
        return _finish(nc, ALLQ, es, pctx)

    RW = min(1024, C)
    NRND = C // RW
    NSQ = RW // 512

    p34 = ExitStack()
    atq = sb("atq", [128, 2 * HKV * GW], BF16, p34)
    wob = [sb(f"wob{i}", [128, NCH * CBW], BF16, p34) for i in range(2)]
    xqp = [sb(f"xqp{i}", [128, CBW], F32, p34) for i in range(2)]
    pw = [ps(f"pw{i}", [128, CBW], F32, p34) for i in range(2)]

    out_dma_count = 0
    atq_fills = 0
    atq_free = 0
    wo_fills = [0, 0]
    wo_free = [0, 0]
    xq_fills = [0, 0]
    xq_free = [0, 0]
    pw_free = [0, 0]
    trash_war = 0
    trans_last = 0
    pd2_rd = 0
    st1_war = 0
    ob_cnt = [0, 0, 0, 0]
    ytok = [(0, 0)] * 4

    def emit_quarter(qx):
        nonlocal atq_fills, atq_free, out_dma_count, trash_war, trans_last, pd2_rd, st1_war
        q4 = ExitStack()
        y_sb = sb(f"y_{qx}", [128, 4 * C], F32, q4)
        hbf = sb(f"hbf_{qx}", [128, 4 * C], BF16, q4)
        sums = sb(f"sums_{qx}", [128, 4 * NCB], F32, q4)
        sqs = sb(f"sqs_{qx}", [128, 4 * NCB], F32, q4)
        trash = sb(f"trash_{qx}", [128, 512], F32, q4)
        mst = sb(f"mst_{qx}", [128, 4], F32, q4)
        rst = sb(f"rst_{qx}", [128, 4], F32, q4)
        st1 = sb(f"st1_{qx}", [128, 6], F32, q4)
        gmid = sb(f"gmid_{qx}", [ER, 512], BF16, q4)

        qdve.w(s_dve, s_dve.n)
        qact.w(s_act, s_act.n)

        qsp.w(s_pe, atq_free)
        for kvh in range(HKV):
            for pri in range(2):
                qsp.do(lambda e, kvh=kvh, pri=pri: e.dma_start(
                    atq[:, (kvh * 2 + pri) * GW:(kvh * 2 + pri + 1) * GW],
                    s_at[kvh, 2 * qx + pri, :, :]), (s_aq, 16))
        atq_fills += 1
        tok_atq = 16 * 2 * HKV * atq_fills

        stt_tok = [[0] * NCB for _ in range(4)]
        sq_tok = [[0] * NCB for _ in range(4)]
        tokW = None
        for cb in range(NCB):
            wi = cb % 2
            qsp.w(s_wos, wo_total)
            qsp.w(s_pe, wo_free[wi])
            for c0 in range(0, NCH, csub):
                c1 = min(c0 + csub, NCH)
                qsp.do(lambda e, c0=c0, c1=c1, wi=wi, cb=cb: e.dma_start(
                    wob[wi][:, :].rearrange("p (c d) -> p c d", c=NCH)[:, c0:c1, :],
                    s_wosc.rearrange("c p d -> p c d")[:, c0:c1, cb * CBW:(cb + 1) * CBW]),
                    (s_wo[wi], 16))
            wo_fills[wi] += 1
            tok_wo = 16 * (NCH // csub) * wo_fills[wi]
            for tb in range(4):
                pri, off = tb // 2, tb % 2
                xi = (cb * 4 + tb) % 2
                qgp.w(s_dve, xq_free[xi])
                qgp.do(lambda e, tb=tb, cb=cb, xi=xi: e.dma_start(
                    xqp[xi][:, :], d_xq[qx * 512 + tb * 128: qx * 512 + (tb + 1) * 128,
                                        cb * CBW:(cb + 1) * CBW]), (s_xq[xi], 16))
                xq_fills[xi] += 1
                tok_xq = 16 * xq_fills[xi]

                pwi = tb % 2
                qpe.w(s_aq, tok_atq)
                qpe.w(s_wo[wi], tok_wo)
                qpe.w(s_dve, pw_free[pwi])
                for h in range(H):
                    kvh, hig = h // G, h % G
                    scol = (kvh * 2 + pri) * GW + hig * 256 + off * 128
                    tokW = qpe.do(lambda e, h=h, scol=scol, wi=wi, pwi=pwi: e.matmul(
                        pw[pwi][:, :],
                        atq[:, scol:scol + 128],
                        wob[wi][:, :].rearrange("p (c d) -> p c d", c=NCH)[:, h, :],
                        start=(h == 0), stop=(h == H - 1)),
                        (s_pe, 1))
                par, cnt = ytok[tb]
                qdve.w(s_ob[par], cnt)
                qdve.w(s_pe, tokW)
                qdve.w(s_xq[xi], tok_xq)
                tokY = qdve.do(lambda e, tb=tb, cb=cb, pwi=pwi, xi=xi: e.scalar_tensor_tensor(
                    y_sb[:, tb * C + cb * CBW: tb * C + (cb + 1) * CBW],
                    pw[pwi][:, :], 0.0, xqp[xi][:, :], ALU.add, ALU.add,
                    accum_out=sums[:, tb * NCB + cb: tb * NCB + cb + 1]), (s_dve, 1))
                pw_free[pwi] = tokY
                xq_free[xi] = tokY
                stt_tok[tb][cb] = tokY
                qact.w(s_dve, tokY)
                qact.w(s_act, trash_war)
                tokQ2 = qact.do(lambda e, tb=tb, cb=cb: e.activation(
                    trash[:, 0:CBW], y_sb[:, tb * C + cb * CBW: tb * C + (cb + 1) * CBW],
                    AF.Square, accum_out=sqs[:, tb * NCB + cb: tb * NCB + cb + 1]), (s_act, 1))
                trash_war = tokQ2
                sq_tok[tb][cb] = tokQ2
            wo_free[wi] = tokW
        wo_last_pe = s_pe.n

        hbf_tok = [0] * 4
        norm_tok = [0] * 4
        st1_last = 0
        for tb in range(4):
            qdve.w(s_dve, stt_tok[tb][NCB - 1])
            qdve.w(s_act, sq_tok[tb][NCB - 1])
            qdve.w(s_dve, st1_last)
            qdve.w(s_act, st1_war)
            t1 = qdve.do(lambda e, tb=tb: e.reduce_sum(
                st1[:, 0:1], sums[:, tb * NCB:(tb + 1) * NCB], axis=AX.X), (s_dve, 1))
            t2 = qdve.do(lambda e, tb=tb: e.reduce_sum(
                st1[:, 1:2], sqs[:, tb * NCB:(tb + 1) * NCB], axis=AX.X), (s_dve, 1))
            qdve.w(s_dve, t2)
            t3 = qdve.do(lambda e, tb=tb: e.tensor_scalar_mul(
                mst[:, tb:tb + 1], st1[:, 0:1], 1.0 / C), (s_dve, 1))
            qdve.w(s_dve, t3)
            t4 = qdve.do(lambda e, tb=tb: e.tensor_tensor(
                st1[:, 2:3], mst[:, tb:tb + 1], mst[:, tb:tb + 1], ALU.mult), (s_dve, 1))
            qdve.w(s_dve, t4)
            t5 = qdve.do(lambda e: e.scalar_tensor_tensor(
                st1[:, 3:4], st1[:, 1:2], 1.0 / C, st1[:, 2:3], ALU.mult, ALU.subtract), (s_dve, 1))
            qdve.w(s_dve, t5)
            t6 = qdve.do(lambda e: e.tensor_scalar_add(st1[:, 4:5], st1[:, 3:4], EPS), (s_dve, 1))
            qact.w(s_dve, t6)
            t7 = qact.do(lambda e: e.activation(st1[:, 5:6], st1[:, 4:5], AF.Ln), (s_act, 1))
            qact.w(s_act, t7)
            t8 = qact.do(lambda e, tb=tb: e.activation(
                rst[:, tb:tb + 1], st1[:, 5:6], AF.Exp, scale=kon[:, 1:2]), (s_act, 1))
            qdve.w(s_act, t8)
            tmr = qdve.do(lambda e, tb=tb: e.scalar_tensor_tensor(
                st1[:, 0:1], mst[:, tb:tb + 1], -1.0, rst[:, tb:tb + 1],
                ALU.mult, ALU.mult), (s_dve, 1))
            qact.w(s_dve, tmr)
            qact.w(s_pe, trans_last)
            hbf_tok[tb] = qact.do(lambda e, tb=tb: e.activation(
                hbf[:, tb * C:(tb + 1) * C], y_sb[:, tb * C:(tb + 1) * C],
                AF.Identity, bias=st1[:, 0:1], scale=rst[:, tb:tb + 1]), (s_act, 1))
            qdve.w(s_act, hbf_tok[tb])
            t9 = qdve.do(lambda e, tb=tb: e.tensor_scalar(
                y_sb[:, tb * C:(tb + 1) * C], y_sb[:, tb * C:(tb + 1) * C],
                mst[:, tb:tb + 1], rst[:, tb:tb + 1], ALU.subtract, ALU.mult), (s_dve, 1))
            norm_tok[tb] = t9
            st1_last = t9
            st1_war = t8

        RW2 = 512
        NR2 = C // RW2
        p4a = ExitStack()
        hT2 = [sb(f"hT_{qx}{i}", [128, NCH * 128], BF16, p4a) for i in range(2)]
        lg = sb(f"lg_{qx}", [32, 512], F32, p4a)
        tA = sb(f"tA_{qx}", [32, 512], F32, p4a)
        tB = sb(f"tB_{qx}", [32, 512], F32, p4a)
        tC = sb(f"tC_{qx}", [32, 512], F32, p4a)
        tD = sb(f"tD_{qx}", [32, 512], F32, p4a)
        gb = sb(f"gb_{qx}", [32, 512], BF16, p4a)
        g64 = sb(f"g64_{qx}", [ER, 512], BF16, p4a)
        sums2 = sb(f"sums2_{qx}", [128, 4 * NR2], F32, p4a)
        sqs2 = sb(f"sqs2_{qx}", [128, 4 * NR2], F32, p4a)
        ptr = [ps(f"ptr_{qx}{i}", [128, 512], BF16, p4a) for i in range(2)]
        pm68 = ps(f"pm68_{qx}", [EW, 512], F32, p4a)
        pg = ps(f"pg_{qx}", [ER, 512], F32, p4a)
        pd2 = [ps(f"pd2_{qx}{i}", [128, RW2], F32, p4a) for i in range(2)]

        perm1 = [(i + 1) % NE if i < NE else i for i in range(32)]
        perm2 = [(i + 2) % NE if i < NE else i for i in range(32)]
        ptr_free = [0, 0]
        hT_free = [0, 0]
        mid_tok = [0] * 4
        gate_tok = [0] * 4
        gmid_tok = [0] * 4
        pd2_free = [0, 0]
        lgz = qdve.do(lambda e: e.memset(lg[:, :], 0.0), (s_dve, 1))

        def emit_A(tb):
            nonlocal trans_last
            hi2 = tb % 2
            qpe.w(s_act, hbf_tok[tb])
            qpe.w(s_ld0, tok_ld0)
            qpe.w(s_dve, pd2_rd)
            ev_tok = 0
            NGRP = (NCH + 3) // 4
            for grp_i in range(NGRP):
                pi = (tb * NGRP + grp_i) % 2
                qpe.w(s_dve, ptr_free[pi])
                tokT = None
                c0g = grp_i * 4
                c1g = min(c0g + 4, NCH)
                for cc in range(c0g, c1g):
                    tokT = qpe.do(lambda e, tb=tb, cc=cc, pi=pi, c0g=c0g: e.transpose(
                        ptr[pi][:, (cc - c0g) * 128:(cc - c0g + 1) * 128],
                        hbf[:, tb * C + cc * 128: tb * C + (cc + 1) * 128],
                        ident_b[:, :]), (s_pe, 1))
                trans_last = tokT
                qdve.w(s_pe, tokT)
                qdve.w(s_pe, hT_free[hi2])
                ev_tok = qdve.do(lambda e, c0g=c0g, c1g=c1g, pi=pi, hi2=hi2: e.tensor_copy(
                    hT2[hi2][:, c0g * 128:c1g * 128], ptr[pi][:, 0:(c1g - c0g) * 128]), (s_dve, 1))
                ptr_free[pi] = ev_tok
            qpe.w(s_dve, ev_tok)
            tokM = None
            for cc in range(NCH):
                tokM = qpe.do(lambda e, cc=cc, tb=tb, hi2=hi2: e.matmul(
                    pm68[:, tb * 128:(tb + 1) * 128],
                    aw_b[:, :].rearrange("p (c e) -> p c e", c=NCH)[:, cc, :],
                    hT2[hi2][:, cc * 128:(cc + 1) * 128],
                    start=(cc == 0), stop=(cc == NCH - 1)), (s_pe, 1))
            hT_free[hi2] = tokM
            mid_tok[tb] = tokM

        def emit_gate(tb):
            s0, s1 = tb * 128, (tb + 1) * 128
            L = lambda t: t[:, s0:s1]
            qdve.w(s_dve, lgz)
            qdve.w(s_pe, mid_tok[tb])
            tg = qdve.do(lambda e: e.tensor_copy(lg[0:NE, s0:s1], pm68[ER:EW, s0:s1]), (s_dve, 1))
            qdve.w(s_dve, tg)
            tg = qdve.do(lambda e: e.stream_shuffle(L(tA), L(lg), perm1), (s_dve, 1))
            qdve.w(s_dve, tg)
            tg = qdve.do(lambda e: e.tensor_max(L(tB), L(lg), L(tA)), (s_dve, 1))
            qdve.w(s_dve, tg)
            tg = qdve.do(lambda e: e.stream_shuffle(L(tA), L(tB), perm2), (s_dve, 1))
            qdve.w(s_dve, tg)
            tg = qdve.do(lambda e: e.tensor_max(L(tB), L(tB), L(tA)), (s_dve, 1))
            qdve.w(s_dve, tg)
            tg = qdve.do(lambda e: e.tensor_tensor(L(tC), L(lg), L(tB), ALU.is_equal), (s_dve, 1))
            qdve.w(s_dve, tg)
            tg = qdve.do(lambda e: e.scalar_tensor_tensor(
                L(tC), L(tC), -1e30, L(lg), ALU.mult, ALU.add), (s_dve, 1))
            qdve.w(s_dve, tg)
            tg = qdve.do(lambda e: e.stream_shuffle(L(tA), L(tC), perm1), (s_dve, 1))
            qdve.w(s_dve, tg)
            tg = qdve.do(lambda e: e.tensor_max(L(tC), L(tC), L(tA)), (s_dve, 1))
            qdve.w(s_dve, tg)
            tg = qdve.do(lambda e: e.stream_shuffle(L(tA), L(tC), perm2), (s_dve, 1))
            qdve.w(s_dve, tg)
            tg = qdve.do(lambda e: e.tensor_max(L(tC), L(tC), L(tA)), (s_dve, 1))
            qdve.w(s_dve, tg)
            tg = qdve.do(lambda e: e.tensor_tensor(L(tD), L(lg), L(tC), ALU.is_ge), (s_dve, 1))
            qdve.w(s_dve, tg)
            tg = qdve.do(lambda e: e.tensor_tensor(L(tA), L(lg), L(tB), ALU.subtract), (s_dve, 1))
            qact.w(s_dve, tg)
            ta = qact.do(lambda e: e.activation(
                L(tB), L(tA), AF.Exp, bias=kon[0:32, 2:3], scale=kon[0:32, 3:4]), (s_act, 1))
            qdve.w(s_act, ta)
            tg = qdve.do(lambda e: e.tensor_tensor(L(tB), L(tB), L(tD), ALU.mult), (s_dve, 1))
            qdve.w(s_dve, tg)
            tg = qdve.do(lambda e: e.stream_shuffle(L(tA), L(tB), perm1), (s_dve, 1))
            qdve.w(s_dve, tg)
            tg = qdve.do(lambda e: e.tensor_add(L(tC), L(tB), L(tA)), (s_dve, 1))
            qdve.w(s_dve, tg)
            tg = qdve.do(lambda e: e.stream_shuffle(L(tA), L(tC), perm2), (s_dve, 1))
            qdve.w(s_dve, tg)
            tg = qdve.do(lambda e: e.tensor_add(L(tC), L(tC), L(tA)), (s_dve, 1))
            qdve.w(s_dve, tg)
            tg = qdve.do(lambda e: e.reciprocal(L(tA), L(tC)), (s_dve, 1))
            qdve.w(s_dve, tg)
            tg = qdve.do(lambda e, sc=g.SCALE: e.scalar_tensor_tensor(
                L(tD), L(tB), sc, L(tA), ALU.mult, ALU.mult), (s_dve, 1))
            qdve.w(s_dve, tg)
            gate_tok[tb] = qdve.do(lambda e: e.tensor_copy(L(gb), L(tD)), (s_dve, 1))

        def emit_B1(tb):
            s0, s1 = tb * 128, (tb + 1) * 128
            qpe.w(s_dve, gate_tok[tb])
            qpe.w(s_ld0, tok_ld0)
            tokG = qpe.do(lambda e, s0=s0, s1=s1: e.matmul(
                pg[:, s0:s1], em_b[:, :], gb[0:NE, s0:s1], start=True, stop=True), (s_pe, 1))
            qdve.w(s_pe, tokG)
            tg = qdve.do(lambda e, s0=s0, s1=s1: e.tensor_copy(g64[:, s0:s1], pg[:, s0:s1]), (s_dve, 1))
            qdve.w(s_dve, tg)
            gmid_tok[tb] = qdve.do(lambda e, s0=s0, s1=s1: e.tensor_tensor(
                gmid[:, s0:s1], pm68[0:ER, s0:s1], g64[:, s0:s1], ALU.mult), (s_dve, 1))

        def emit_B2(tb):
            nonlocal trash_war, pd2_rd, out_dma_count, st1_last, st1_war
            y2_last = 0
            sq2_last = 0
            for rnd in range(NR2):
                di = (tb * NR2 + rnd) % 2
                qpe.w(s_dve, gmid_tok[tb])
                qpe.w(s_ld0, tok_ld0)
                qpe.w(s_dve, pd2_free[di])
                tokB = qpe.do(lambda e, tb=tb, rnd=rnd, di=di: e.matmul(
                    pd2[di][:, :],
                    gmid[:, tb * 128:(tb + 1) * 128],
                    bS_b[:, rnd * RW2:(rnd + 1) * RW2],
                    start=True, stop=True), (s_pe, 1))
                qdve.w(s_pe, tokB)
                qdve.w(s_dve, norm_tok[tb])
                qdve.w(s_act, hbf_tok[tb])
                tokY2 = qdve.do(lambda e, tb=tb, rnd=rnd, di=di: e.scalar_tensor_tensor(
                    y_sb[:, tb * C + rnd * RW2: tb * C + (rnd + 1) * RW2],
                    pd2[di][:, :], 0.0, y_sb[:, tb * C + rnd * RW2: tb * C + (rnd + 1) * RW2],
                    ALU.add, ALU.add,
                    accum_out=sums2[:, tb * NR2 + rnd: tb * NR2 + rnd + 1]), (s_dve, 1))
                pd2_free[di] = tokY2
                pd2_rd = tokY2
                y2_last = tokY2
                qact.w(s_dve, tokY2)
                qact.w(s_act, trash_war)
                tokS2 = qact.do(lambda e, tb=tb, rnd=rnd: e.activation(
                    trash[:, 0:RW2],
                    y_sb[:, tb * C + rnd * RW2: tb * C + (rnd + 1) * RW2],
                    AF.Square, accum_out=sqs2[:, tb * NR2 + rnd: tb * NR2 + rnd + 1]), (s_act, 1))
                trash_war = tokS2
                sq2_last = tokS2
            qdve.w(s_dve, y2_last)
            qdve.w(s_act, sq2_last)
            qdve.w(s_dve, st1_last)
            qdve.w(s_act, st1_war)
            t1 = qdve.do(lambda e, tb=tb: e.reduce_sum(
                st1[:, 0:1], sums2[:, tb * NR2:(tb + 1) * NR2], axis=AX.X), (s_dve, 1))
            t2 = qdve.do(lambda e, tb=tb: e.reduce_sum(
                st1[:, 1:2], sqs2[:, tb * NR2:(tb + 1) * NR2], axis=AX.X), (s_dve, 1))
            qdve.w(s_dve, t2)
            t3 = qdve.do(lambda e, tb=tb: e.tensor_scalar_mul(
                mst[:, tb:tb + 1], st1[:, 0:1], 1.0 / C), (s_dve, 1))
            qdve.w(s_dve, t3)
            t4 = qdve.do(lambda e, tb=tb: e.tensor_tensor(
                st1[:, 2:3], mst[:, tb:tb + 1], mst[:, tb:tb + 1], ALU.mult), (s_dve, 1))
            qdve.w(s_dve, t4)
            t5 = qdve.do(lambda e: e.scalar_tensor_tensor(
                st1[:, 3:4], st1[:, 1:2], 1.0 / C, st1[:, 2:3], ALU.mult, ALU.subtract), (s_dve, 1))
            qdve.w(s_dve, t5)
            t6 = qdve.do(lambda e: e.tensor_scalar_add(st1[:, 4:5], st1[:, 3:4], EPS), (s_dve, 1))
            qact.w(s_dve, t6)
            t7 = qact.do(lambda e: e.activation(st1[:, 5:6], st1[:, 4:5], AF.Ln), (s_act, 1))
            qact.w(s_act, t7)
            t8 = qact.do(lambda e, tb=tb: e.activation(
                rst[:, tb:tb + 1], st1[:, 5:6], AF.Exp, scale=kon[:, 1:2]), (s_act, 1))
            st1_last = t6
            st1_war = t8
            qdve.w(s_act, t8)
            t9o = qdve.do(lambda e, tb=tb: e.tensor_scalar(
                y_sb[:, tb * C:(tb + 1) * C], y_sb[:, tb * C:(tb + 1) * C],
                mst[:, tb:tb + 1], rst[:, tb:tb + 1], ALU.subtract, ALU.mult), (s_dve, 1))
            qsp.w(s_dve, t9o)
            ob_cnt[tb] += 16
            qsp.do(lambda e, tb=tb: e.dma_start(
                d_out[qx * 512 + tb * 128: qx * 512 + (tb + 1) * 128, :],
                y_sb[:, tb * C:(tb + 1) * C]), (s_ob[tb], 16))
            ytok[tb] = (tb, ob_cnt[tb])
            out_dma_count += 1

        for tb in range(4):
            emit_A(tb)
            emit_gate(tb)
            if tb >= 1:
                emit_B1(tb - 1)
            if tb >= 2:
                emit_B2(tb - 2)
        emit_B1(3)
        emit_B2(2)
        emit_B2(3)
        p4a.close()
        atq_free = wo_last_pe
        q4.close()

    for qx in range(NQ4):
        emit_quarter(qx)


    p34.close()

    for _s in s_ob:
        qsp.w(_s, _s.n)
    return _finish(nc, ALLQ, es, pctx)


def _finish(nc, ALLQ, es, pctx):
    qpe, qact, qdve, qgp, qsp = ALLQ
    with nc.Block() as block:
        @block.tensor
        def _(e):
            _replay(qpe, e)

        @block.scalar
        def _(e):
            _replay(qact, e)

        @block.vector
        def _(e):
            _replay(qdve, e)

        @block.gpsimd
        def _(e):
            _replay(qgp, e)

        @block.sync
        def _(e):
            _replay(qsp, e)

    es.close()
    pctx.close()
    return nc



def stage_core(g, core, X, Wq, Wk, Wv, Wo, A, B, Wr):
    T, C = g.T, g.C
    b, half = core // 2, core % 2
    js = jlist(g, half)
    qcols = np.concatenate([np.arange(j * 128, (j + 1) * 128) for j in js])
    Xb = np.asarray(X[b], dtype=np.float32)
    xT = np.ascontiguousarray(Xb.T)
    xTq = np.ascontiguousarray(Xb.T[:, qcols])
    xq = np.ascontiguousarray(Xb[qcols, :])

    inv = 1.0 / THETA ** (np.arange(64, dtype=np.float64) / 64.0)
    posk = np.arange(T, dtype=np.float64)
    angk = np.outer(posk, inv)
    ct = np.cos(angk).T.astype(np.float32)
    st = np.sin(angk).T.astype(np.float32)
    ckk = np.ascontiguousarray(np.vstack([ct, ct]))
    skk = np.ascontiguousarray(np.vstack([-st, st]))
    angq = angk[qcols]
    cq = np.cos(angq).T.astype(np.float32)
    sq = np.sin(angq).T.astype(np.float32)
    sc = 1.0 / math.sqrt(g.DH)
    ckq = np.ascontiguousarray(np.vstack([cq, cq]) * sc)
    skq = np.ascontiguousarray(np.vstack([-sq, sq]) * sc)

    m = np.zeros((g.NPAIR, 4, 128, 256), dtype=np.float32)
    pk = np.arange(128)
    for pr in range(g.NPAIR):
        for off in range(4):
            kt = 4 * pr + off
            kidx = kt * 128 + pk
            for sl2 in range(2):
                j = js[2 * pr + sl2]
                tok = j * 128 + np.arange(128)
                m[pr, off, :, sl2 * 128:(sl2 + 1) * 128] = (kidx[:, None] <= tok[None, :])
    maskq = np.ascontiguousarray(m.reshape(g.NPAIR * 4 * 128, 256))

    wkh = np.ascontiguousarray(Wk[:, half * g.KVDh:(half + 1) * g.KVDh])
    wvh = np.ascontiguousarray(Wv[:, half * g.KVDh:(half + 1) * g.KVDh])
    return {
        "xT": xT, "xTq": xTq, "xq": xq,
        "wq": Wq, "wk": wkh, "wv": wvh, "wo": Wo,
        "awT": None, "bS": None, "emat": None,
        "ckk": ckk, "skk": skk, "ckq": ckq, "skq": skq,
        "maskq": maskq, "ident": None,
    }


def kernel(L0_attn_X, L0_attn_Wq, L0_attn_Wk, L0_attn_Wv, L0_attn_Wo,
           L0_ffn_A, L0_ffn_B, L0_ffn_Wr, _geo=None, _ncores=None):
    g = _geo or Geo()
    X = np.asarray(L0_attn_X, dtype=np.float32)
    Wq = np.ascontiguousarray(np.asarray(L0_attn_Wq, dtype=np.float32))
    Wk = np.ascontiguousarray(np.asarray(L0_attn_Wk, dtype=np.float32))
    Wv = np.ascontiguousarray(np.asarray(L0_attn_Wv, dtype=np.float32))
    Wo = np.ascontiguousarray(np.asarray(L0_attn_Wo, dtype=np.float32))
    A = np.asarray(L0_ffn_A, dtype=np.float32)
    B = np.asarray(L0_ffn_B, dtype=np.float32)
    Wr = np.asarray(L0_ffn_Wr, dtype=np.float32)

    n_cores = _ncores if _ncores is not None else 2 * X.shape[0]
    awT = np.ascontiguousarray(
        np.concatenate([A.reshape(g.ER, g.C), Wr], axis=0).T)
    bS = np.ascontiguousarray(B.transpose(0, 2, 1).reshape(g.ER, g.C))
    emat = np.ascontiguousarray(np.kron(np.eye(g.NE, dtype=np.float32),
                                        np.ones((1, g.R), dtype=np.float32)))
    ident = np.eye(128, dtype=np.float32)

    in_maps = []
    for core in range(n_cores):
        d = stage_core(g, core, X, Wq, Wk, Wv, Wo, A, B, Wr)
        d["awT"], d["bS"], d["emat"], d["ident"] = awT, bS, emat, ident
        in_maps.append(d)

    nc = build_nc(g, n_cores=n_cores)
    res = run_bass_kernel_spmd(nc, in_maps, core_ids=list(range(n_cores)))

    out = np.empty((X.shape[0], g.T, g.C), dtype=np.float32)
    for core in range(n_cores):
        b, half = core // 2, core % 2
        js = jlist(g, half)
        qcols = np.concatenate([np.arange(j * 128, (j + 1) * 128) for j in js])
        out[b, qcols, :] = res.results[core]["out"]
    return out
